# version 05
# baseline (speedup 1.0000x reference)
import numpy as np
import ml_dtypes

import concourse.bass as bass
import concourse.mybir as mybir
from concourse.bass_utils import run_bass_kernel_spmd

F32 = mybir.dt.float32
BF16 = mybir.dt.bfloat16
U8 = mybir.dt.uint8
ALU = mybir.AluOpType
ACTF = mybir.ActivationFunctionType
AX = mybir.AxisListType

T, C, H, HS, I = 2048, 1024, 16, 64, 2816
NTC, NCC, NIC, NP, NQT = T // 128, C // 128, I // 128, H // 2, T // 512
ARENA_KB = 206
QW = 1408


class Carver:
    def __init__(self, arena_ap):
        self.arena = arena_ap
        self.off = 0
        self.marks = []

    def take(self, n_elems, dtype, name=""):
        esz = 2 if dtype == BF16 else 4
        nbytes = n_elems * esz
        off = (self.off + 63) & ~63
        self.off = off + nbytes
        assert self.off <= ARENA_KB * 1024, f"arena overflow at {name}: {self.off}"
        return self.arena[:, off:off + nbytes].bitcast(dtype)

    def push(self):
        self.marks.append(self.off)

    def pop(self):
        self.off = self.marks.pop()


class Plan:
    ENGINES = ("sync", "scalar", "vector", "tensor", "gpsimd")

    def __init__(self):
        self.ops = {e: [] for e in self.ENGINES}
        self.cnt = {"LD": 0, "ST": 0, "GP": 0, "PE": 0, "DV": 0, "AC": 0}

    def emit(self, eng, fn, waits=(), inc=None):
        w = [(s, v) for (s, v) in waits if v and v > 0]
        tok = None
        if inc is not None:
            s, d = inc
            self.cnt[s] += d
            tok = (s, self.cnt[s])
        self.ops[eng].append((fn, w, inc))
        return tok

    def dv(self, fn, waits=()):
        return self.emit("vector", fn, waits, ("DV", 1))

    def ac(self, fn, waits=()):
        return self.emit("scalar", fn, waits, ("AC", 1))

    def pe(self, fn, waits=()):
        return self.emit("tensor", fn, waits, ("PE", 1))

    def ld(self, fn, waits=()):
        return self.emit("sync", fn, waits, ("LD", 16))

    def st(self, fn, waits=()):
        return self.emit("scalar", fn, waits, ("ST", 16))

    def gp(self, fn, waits=()):
        return self.emit("gpsimd", fn, waits, ("GP", 16))

    def dv_drain(self):
        self.emit("vector", lambda v: v.drain())

    def barrier_token(self, sem):
        return (sem, self.cnt[sem])

    def barrier(self):
        toks = [(s, self.cnt[s]) for s in ("LD", "ST", "GP", "PE", "DV", "AC")]
        for e in self.ENGINES:
            self.ops[e].append((None, [(s, v) for (s, v) in toks if v > 0], None))


class QSeq:

    def __init__(self, plan, tmp):
        self.plan = plan
        self.tmp = tmp
        self.tiles = []
        self.i = 0
        self.heads_done = -1

    def feed(self, tiles):
        self.tiles.extend(tiles)

    def _head1(self, j):
        t = self.tiles[j]
        b = j % 2
        nb = t["w"] // 16
        rmax = self.tmp["rmax"][b][:, 0:nb]
        x3 = t["x"].rearrange("p (nb w) -> p nb w", w=16)
        return self.plan.dv(lambda v, rmax=rmax, x3=x3: v.tensor_reduce(
            rmax, x3, AX.X, ALU.max, apply_absolute_value=True), t["gates"])

    def _head2(self, j):
        b = j % 2
        nb = self.tiles[j]["w"] // 16
        rmax = self.tmp["rmax"][b][:, 0:nb]
        rr12 = self.tmp["rr12"][b][:, 0:nb]
        self.plan.dv(lambda v, rmax=rmax: v.tensor_scalar_max(rmax, rmax, 6e-12))
        return self.plan.dv(lambda v, rr12=rr12, rmax=rmax: v.reciprocal(rr12, rmax))

    def _head3(self, j):
        b = j % 2
        nb = self.tiles[j]["w"] // 16
        rmax = self.tmp["rmax"][b][:, 0:nb]
        rr12 = self.tmp["rr12"][b][:, 0:nb]
        s2 = self.tmp["s2"][b][:, 0:nb]
        self.plan.dv(lambda v, rr12=rr12: v.tensor_scalar_mul(rr12, rr12, 12.0))
        return self.plan.dv(lambda v, s2=s2, rmax=rmax: v.tensor_scalar_mul(
            s2, rmax, 1.0 / 12.0))

    def _emit_heads_guarded(self, j):
        if self.tiles[j].get("prep"):
            self.tiles[j]["prep"]()
            self.plan.dv_drain()
        self._head1(j)
        self.plan.dv_drain()
        self._head2(j)
        self.plan.dv_drain()
        self._head3(j)
        self.plan.dv_drain()
        self.heads_done = j

    def start(self):
        self._emit_heads_guarded(0)

    def emit_next(self):
        i = self.i
        if self.heads_done < i:
            self._emit_heads_guarded(i)
        t = self.tiles[i]
        b = i % 2
        w = t["w"]
        nb = w // 16
        plan, tmp = self.plan, self.tmp
        rr12 = tmp["rr12"][b][:, 0:nb]
        s2 = tmp["s2"][b][:, 0:nb]
        v2 = tmp["v2"][:, 0:w]
        u = tmp["u"][:, 0:w]
        g = tmp["g"][:, 0:w]
        hh = tmp["hh"][:, 0:w]
        sgn = tmp["sgn"][:, 0:w]
        x3 = t["x"].rearrange("p (nb w) -> p nb w", w=16)
        v23 = v2.rearrange("p (nb w) -> p nb w", w=16)
        rr12_bc = rr12.unsqueeze(2).broadcast_to([128, nb, 16])
        s2_bc = s2.unsqueeze(2).broadcast_to([128, nb, 16])

        have_next = (i + 1 < len(self.tiles)) and (self.heads_done == i)
        if have_next and self.tiles[i + 1].get("prep"):
            self.tiles[i + 1]["prep"]()

        wv2 = list(t["gates"])
        tb1 = plan.dv(lambda v: v.tensor_tensor(v23, x3, rr12_bc, ALU.mult), wv2)
        plan.dv(lambda v: v.tensor_tensor(u, v2, v2, ALU.mult))
        plan.dv(lambda v: v.tensor_scalar(sgn, v2, 0.0, 2.0, ALU.is_gt, ALU.mult))
        plan.dv(lambda v: v.tensor_scalar_sub(sgn, sgn, 1.0))
        if have_next:
            self._head1(i + 1)
        plan.dv(lambda v: v.tensor_single_scalar(g, u, 0.25, ALU.is_gt))
        plan.dv(lambda v: v.scalar_tensor_tensor(g, u, 2.25, g, ALU.is_gt, ALU.add))
        plan.dv(lambda v: v.scalar_tensor_tensor(g, u, 6.25, g, ALU.is_gt, ALU.add))
        plan.dv(lambda v: v.scalar_tensor_tensor(g, u, 12.25, g, ALU.is_gt, ALU.add))
        if have_next:
            self._head2(i + 1)
        plan.dv(lambda v: v.tensor_single_scalar(hh, u, 100.0, ALU.is_gt))
        plan.dv(lambda v: v.scalar_tensor_tensor(hh, u, 25.0, hh, ALU.is_gt, ALU.add))
        plan.dv(lambda v: v.scalar_tensor_tensor(hh, u, 49.0, hh, ALU.is_gt, ALU.add))
        plan.dv(lambda v: v.scalar_tensor_tensor(hh, u, 100.0, hh, ALU.is_gt, ALU.add))
        if have_next:
            self._head3(i + 1)
            self.heads_done = i + 1
        plan.dv(lambda v: v.scalar_tensor_tensor(g, hh, 2.0, g, ALU.mult, ALU.add))
        plan.dv(lambda v: v.tensor_tensor(g, g, sgn, ALU.mult))
        o3 = t["out"].rearrange("p (nb w) -> p nb w", w=16)
        g3 = g.rearrange("p (nb w) -> p nb w", w=16)
        wout = [t["out_gate"]] if t.get("out_gate") else []
        tok = plan.dv(lambda v: v.tensor_tensor(o3, g3, s2_bc, ALU.mult), wout)
        if t.get("on_done"):
            t["on_done"](tok)
        self.i += 1
        if self.i < len(self.tiles) and self.heads_done < self.i:
            self._emit_heads_guarded(self.i)
        return tok

    def pending(self):
        return len(self.tiles) - self.i


def build_program():
    nc = bass.Bass("TRN2", target_bir_lowering=False)

    x_h = nc.dram_tensor("x", [T, C], F32, kind="ExternalInput").ap()
    wqkT_h = nc.dram_tensor("wqkT", [C, 2048], F32, kind="ExternalInput").ap()
    wvT_h = nc.dram_tensor("wvT", [C, C], F32, kind="ExternalInput").ap()
    wpT_h = nc.dram_tensor("wpT", [C, C], F32, kind="ExternalInput").ap()
    bproj_h = nc.dram_tensor("bproj", [1, C], F32, kind="ExternalInput").ap()
    wg_h = nc.dram_tensor("w_gate", [I, C], F32, kind="ExternalInput").ap()
    wu_h = nc.dram_tensor("w_up", [I, C], F32, kind="ExternalInput").ap()
    wd_h = nc.dram_tensor("w_down", [C, I], F32, kind="ExternalInput").ap()
    cmask_h = nc.dram_tensor("cmask", [128, 896], BF16, kind="ExternalInput").ap()
    ones_h = nc.dram_tensor("ones_all", [128, 128], F32, kind="ExternalInput").ap()
    onesb_h = nc.dram_tensor("ones_bf", [128, 128], BF16, kind="ExternalInput").ap()
    ones33_h = nc.dram_tensor("ones33", [128, 33], BF16, kind="ExternalInput").ap()
    eye_h = nc.dram_tensor("eye64", [64, 64], BF16, kind="ExternalInput").ap()
    out_h = nc.dram_tensor("out", [T, C], F32, kind="ExternalOutput").ap()

    s_xnbf = nc.dram_tensor("s_xnbf", [T, C], BF16, kind="Internal").ap()
    s_xq = nc.dram_tensor("s_xq", [T, C], BF16, kind="Internal").ap()
    s_xqT = nc.dram_tensor("s_xqT", [C, T], BF16, kind="Internal").ap()
    s_hq = nc.dram_tensor("s_hq", [T, I], BF16, kind="Internal").ap()
    s_hqT = nc.dram_tensor("s_hqT", [I, T], BF16, kind="Internal").ap()
    s_catT = nc.dram_tensor("s_catT", [C, T], BF16, kind="Internal").ap()
    s_wgq = nc.dram_tensor("s_wgq", [I, C], BF16, kind="Internal").ap()
    s_wuq = nc.dram_tensor("s_wuq", [I, C], BF16, kind="Internal").ap()
    s_wdq = nc.dram_tensor("s_wdq", [C, I], BF16, kind="Internal").ap()

    plan = Plan()

    with (
        nc.sbuf_tensor("arena", [128, ARENA_KB * 1024], U8) as arena_t,
        nc.psum_tensor("ps_big", [128, 3072], F32) as ps_big,
        nc.psum_tensor("ps_sm", [128, 1024], F32) as ps_sm,
        nc.semaphore("LD") as sem_LD,
        nc.semaphore("ST") as sem_ST,
        nc.semaphore("GP") as sem_GP,
        nc.semaphore("PE") as sem_PE,
        nc.semaphore("DV") as sem_DV,
        nc.semaphore("AC") as sem_AC,
        nc.Block() as block,
    ):
        sems = {"LD": sem_LD, "ST": sem_ST, "GP": sem_GP,
                "PE": sem_PE, "DV": sem_DV, "AC": sem_AC}
        cv = Carver(arena_t[:])

        cmask = cv.take(896, BF16, "cmask")
        ones_all = cv.take(128, F32, "ones")
        ones_bf = cv.take(128, BF16, "onesb")
        ones33 = cv.take(33, BF16, "ones33")
        eye64 = cv.take(64, BF16, "eye")
        bproj = cv.take(C, F32, "bproj")
        rsum = cv.take(1024, F32, "rsum")
        rb = cv.take(1024, F32, "rb")
        x_t = [cv.take(C, F32, "x0"), cv.take(C, F32, "x1")]
        catb = [cv.take(512, BF16, "cat0"), cv.take(512, BF16, "cat1")]
        stgB = cv.take(512, BF16, "stgB")

        def carve_qtmp():
            return {
                "rmax": [cv.take(QW // 16, F32), cv.take(QW // 16, F32)],
                "rr12": [cv.take(QW // 16, F32), cv.take(QW // 16, F32)],
                "s2": [cv.take(QW // 16, F32), cv.take(QW // 16, F32)],
                "v2": cv.take(QW, F32), "u": cv.take(QW, F32),
                "g": cv.take(QW, F32), "hh": cv.take(QW, F32),
                "sgn": cv.take(QW, F32),
            }

        plan.ld(lambda e: e.dma_start(out=cmask, in_=cmask_h))
        plan.ld(lambda e: e.dma_start(out=ones_all, in_=ones_h))
        plan.ld(lambda e: e.dma_start(out=ones_bf, in_=onesb_h))
        plan.ld(lambda e: e.dma_start(out=ones33, in_=ones33_h))
        plan.ld(lambda e: e.dma_start(out=eye64[0:64, :], in_=eye_h))
        plan.ld(lambda e: e.dma_start(out=bproj[0:1, :], in_=bproj_h))
        t_consts = plan.barrier_token("LD")

        cv.push()
        qtmp = carve_qtmp()
        stats = cv.take(4 * NTC, F32, "stats")
        msum_all = stats[:, 0:NTC]
        mean_all = stats[:, 2 * NTC:3 * NTC]
        rstd_all = stats[:, 3 * NTC:4 * NTC]
        sumsq_all = stats[:, NTC:2 * NTC]
        sqs = qtmp["v2"][:, 0:C]

        tok_xread = [None, None]
        tok_xact = [None, None]
        tok_sq = None
        for i in range(NTC):
            b = i % 2
            w = [t for t in (tok_xread[b], tok_xact[b]) if t]
            tld = plan.ld(lambda e, i=i, b=b: e.dma_start(
                out=x_t[b], in_=x_h[i * 128:(i + 1) * 128, :]), w + [t_consts])
            tok_xact[b] = plan.ac(lambda e, b=b: e.activation(sqs, x_t[b], ACTF.Square),
                                  [tld] + ([tok_sq] if tok_sq else []))
            tok_xread[b] = plan.dv(lambda e, i=i, b=b: e.reduce_sum(
                msum_all[:, i:i + 1], x_t[b], AX.X), [tld])
            tok_sq = plan.dv(lambda e, i=i: e.reduce_sum(
                sumsq_all[:, i:i + 1], sqs, AX.X), [tok_xact[b]])
        plan.dv_drain()
        plan.dv(lambda e: e.tensor_scalar_mul(mean_all, msum_all, 1.0 / C))
        plan.dv(lambda e: e.tensor_scalar(rstd_all, sumsq_all, 1.0 / C, 1e-5,
                                          ALU.mult, ALU.add))
        plan.dv_drain()
        plan.dv(lambda e: e.tensor_mul(msum_all, mean_all, mean_all))
        plan.dv_drain()
        tvar = plan.dv(lambda e: e.tensor_sub(rstd_all, rstd_all, msum_all))
        tsd = plan.ac(lambda e: e.activation(rstd_all, rstd_all, ACTF.Sqrt), [tvar])
        plan.dv(lambda e: e.reciprocal(rstd_all, rstd_all), [tsd])
        plan.dv_drain()

        xn_t = [cv.take(C, F32, "xn0"), cv.take(C, F32, "xn1")]
        bfb = [cv.take(C, BF16, "bf0"), cv.take(C, BF16, "bf1")]
        qxb = [cv.take(C, BF16, "qx0"), cv.take(C, BF16, "qx1")]

        qs1 = QSeq(plan, qtmp)
        st1 = {"apply": [None, None], "st_bf": [None, None], "st_qx": [None, None]}

        def mk_prep1(i):
            def prep():
                b = i % 2
                wld = [t for t in (st1["apply"][b], tok_xread[b], tok_xact[b]) if t]
                tld = plan.ld(lambda e: e.dma_start(
                    out=x_t[b], in_=x_h[i * 128:(i + 1) * 128, :]), wld)
                tap = plan.dv(lambda e: e.tensor_scalar(
                    xn_t[b], x_t[b], mean_all[:, i:i + 1], rstd_all[:, i:i + 1],
                    ALU.subtract, ALU.mult), [tld])
                st1["apply"][b] = tap
                tiles1[i]["gates"] = [tap]
                tcast = plan.ac(lambda e: e.copy(bfb[b], xn_t[b]),
                                [tap] + ([st1["st_bf"][b]] if st1["st_bf"][b] else []))
                st1["st_bf"][b] = plan.st(lambda e: e.dma_start(
                    out=s_xnbf[i * 128:(i + 1) * 128, :], in_=bfb[b]), [tcast])
            return prep

        def mk_done1(i):
            def done(tok):
                b = i % 2
                st1["st_qx"][b] = plan.st(lambda e: e.dma_start(
                    out=s_xq[i * 128:(i + 1) * 128, :], in_=qxb[b]), [tok])
            return done

        tiles1 = [dict(x=xn_t[i % 2], out=qxb[i % 2], w=C, gates=[],
                       prep=mk_prep1(i), on_done=mk_done1(i), out_gate=None)
                  for i in range(NTC)]
        qs1.feed(tiles1)
        qs1.start()
        while qs1.pending():
            i = qs1.i
            qs1.tiles[i]["out_gate"] = st1["st_qx"][i % 2]
            qs1.emit_next()
        plan.barrier()
        cv.pop()

        cv.push()
        qT = cv.take(NP * T, BF16, "qT")
        kT = cv.take(NP * T, BF16, "kT")
        vn = cv.take(H * NTC * 65, BF16, "vn")
        cv.push()
        xnT = cv.take(NCC * T, BF16, "xnT")
        stg = [cv.take(T, BF16, "stg0"), cv.take(T, BF16, "stg1")]
        wqk = [cv.take(NCC * 256, BF16, "wqk0"), cv.take(NCC * 256, BF16, "wqk1")]
        wvT = cv.take(NCC * 1024, BF16, "wvT")

        t_st_all = plan.barrier_token("ST")
        for ccx in range(NCC):
            plan.ld(lambda e, ccx=ccx: e.dma_start_transpose(
                xnT[:, ccx * T:(ccx + 1) * T], s_xnbf[:, ccx * 128:(ccx + 1) * 128]),
                [t_st_all])
        stg_tok = [None, None]
        for ccx in range(NCC):
            b = ccx % 2
            tt = plan.ld(lambda e, ccx=ccx, b=b: e.dma_start_transpose(
                stg[b], s_xq[:, ccx * 128:(ccx + 1) * 128]),
                ([t_st_all] if ccx < 2 else []) + ([stg_tok[b]] if stg_tok[b] else []))
            stg_tok[b] = plan.st(lambda e, ccx=ccx, b=b: e.dma_start(
                out=s_xqT[ccx * 128:(ccx + 1) * 128, :], in_=stg[b]), [tt])
        vn3 = vn.rearrange("p (b o) -> p b o", o=65)
        t_vn_ones = plan.dv(lambda e: e.memset(vn3[:, :, 64:65], 1.0))
        wvT_v = wvT.rearrange("p (cc n) -> p cc n", n=1024)
        t_wv = plan.gp(lambda e: e.dma_start(
            out=wvT_v, in_=wvT_h.rearrange("(cc p) n -> p cc n", p=128)))
        wqk_tok = [None, None]
        wqk_used = [None, None]

        def load_wqk(p):
            b = p % 2
            wqk_tok[b] = plan.gp(lambda e, p=p, b=b: e.dma_start(
                out=wqk[b].rearrange("p (cc n) -> p cc n", n=256),
                in_=wqkT_h.rearrange("(cc p) n -> p cc n", p=128)[:, :, p * 256:(p + 1) * 256]),
                [wqk_used[b]] if wqk_used[b] else [])

        load_wqk(0)
        psQ = [ps_big[:, 0:512], ps_big[:, 1024:1536]]
        psK = [ps_big[:, 512:1024], ps_big[:, 1536:2048]]
        psV = [ps_big[:, 2048:2560], ps_big[:, 2560:3072],
               ps_sm[:, 0:512], ps_sm[:, 512:1024]]
        evacQ = {}
        t_xnT = plan.barrier_token("LD")
        for p in range(NP):
            b = p % 2
            if p + 1 < NP:
                load_wqk(p + 1)
            wqk_v = wqk[b].rearrange("p (cc n) -> p cc n", n=256)
            for qt in range(NQT):
                ib = (p * NQT + qt) % 2
                wts = [t_xnT, wqk_tok[b]] + ([evacQ.get(ib)] if evacQ.get(ib) else [])
                tpe = None
                for ccx in range(NCC):
                    st0, sp = (ccx == 0), (ccx == NCC - 1)
                    plan.pe(lambda e, qt=qt, ccx=ccx, ib=ib, wqk_v=wqk_v, st0=st0, sp=sp:
                            e.matmul(psQ[ib], wqk_v[:, ccx, 0:128],
                                     xnT[:, ccx * T + qt * 512: ccx * T + qt * 512 + 512],
                                     start=st0, stop=sp),
                            wts if ccx == 0 else [])
                    tpe = plan.pe(lambda e, qt=qt, ccx=ccx, ib=ib, wqk_v=wqk_v, st0=st0, sp=sp:
                                  e.matmul(psK[ib], wqk_v[:, ccx, 128:256],
                                           xnT[:, ccx * T + qt * 512: ccx * T + qt * 512 + 512],
                                           start=st0, stop=sp))
                plan.dv(lambda e, p=p, qt=qt, ib=ib: e.tensor_copy(
                    qT[:, p * T + qt * 512: p * T + qt * 512 + 512], psQ[ib]), [tpe])
                evacQ[ib] = plan.dv(lambda e, p=p, qt=qt, ib=ib: e.tensor_copy(
                    kT[:, p * T + qt * 512: p * T + qt * 512 + 512], psK[ib]))
            wqk_used[b] = evacQ[(p * NQT + NQT - 1) % 2]
        evacV = {}
        for tcv in range(NTC):
            vb0, vb1 = (2 * tcv) % 4, (2 * tcv + 1) % 4
            wts = [t_wv, t_vn_ones] + [evacV[q] for q in (vb0, vb1) if q in evacV]
            tpe = None
            for ccx in range(NCC):
                st0, sp = (ccx == 0), (ccx == NCC - 1)
                plan.pe(lambda e, tcv=tcv, ccx=ccx, vb0=vb0, st0=st0, sp=sp:
                        e.matmul(psV[vb0], xnT[:, ccx * T + tcv * 128: ccx * T + tcv * 128 + 128],
                                 wvT_v[:, ccx, 0:512], start=st0, stop=sp),
                        wts if ccx == 0 else [])
                tpe = plan.pe(lambda e, tcv=tcv, ccx=ccx, vb1=vb1, st0=st0, sp=sp:
                              e.matmul(psV[vb1], xnT[:, ccx * T + tcv * 128: ccx * T + tcv * 128 + 128],
                                       wvT_v[:, ccx, 512:1024], start=st0, stop=sp))
            dst3 = vn.rearrange("p (hh kc o) -> p hh kc o", kc=NTC, o=65)
            for half, q in ((0, vb0), (1, vb1)):
                src = psV[q].rearrange("p (h d) -> p h d", d=64)
                evacV[q] = plan.dv(
                    lambda e, src=src, dst3=dst3, half=half, tcv=tcv:
                    e.tensor_copy(dst3[:, half * 8:(half + 1) * 8, tcv, 0:64], src),
                    [tpe] if half == 0 else [])
        cv.pop()
        plan.barrier()

        cv.push()
        qtmp4 = carve_qtmp()
        pt = [cv.take(1024, BF16, "pt0"), cv.take(1024, BF16, "pt1")]
        m8 = cv.take(T, BF16, "m8")
        prod = cv.take(512, BF16, "prod")
        qn_sb = cv.take(512, F32, "qn")
        mt_sb = cv.take(512, F32, "mt")
        m2_sb = cv.take(512, F32, "m2t")
        kmax2 = cv.take(4, F32, "kmax2")
        wld = [cv.take(QW, F32, "wld0"), cv.take(QW, F32, "wld1")]
        wqb = [cv.take(QW, BF16, "wqb0"), cv.take(QW, BF16, "wqb1")]

        qs2 = QSeq(plan, qtmp4)
        wqs = {"ld": {}, "st": [None, None]}

        wspecs = []
        for (src_h, dst_h, nrt, width) in ((wg_h, s_wgq, NIC, C), (wu_h, s_wuq, NIC, C)):
            for r in range(nrt):
                wspecs.append((src_h, dst_h, r, 0, width))
        for r in range(NCC):
            wspecs.append((wd_h, s_wdq, r, 0, 1408))
            wspecs.append((wd_h, s_wdq, r, 1408, 1408))
        NWQ = len(wspecs)

        def mk_wprep(j):
            def prep():
                src_h, dst_h, r, c0, width = wspecs[j]
                b = j % 2
                wts = [t for t in (wqs["ld"].get(j - 2),) if t]
                wqs["ld"][j] = plan.ld(lambda e: e.dma_start(
                    out=wld[b][:, 0:width],
                    in_=src_h[r * 128:(r + 1) * 128, c0:c0 + width]), wts)
                wtiles[j]["gates"] = [wqs["ld"][j]]
            return prep

        def mk_wdone(j):
            def done(tok):
                src_h, dst_h, r, c0, width = wspecs[j]
                b = j % 2
                wqs["st"][b] = plan.st(lambda e: e.dma_start(
                    out=dst_h[r * 128:(r + 1) * 128, c0:c0 + width],
                    in_=wqb[b][:, 0:width]), [tok])
                wqs["ld"][j] = tok
            return done

        wtiles = [dict(x=wld[j % 2][:, 0:wspecs[j][4]],
                       out=wqb[j % 2][:, 0:wspecs[j][4]], w=wspecs[j][4],
                       gates=[], prep=mk_wprep(j), on_done=mk_wdone(j),
                       out_gate=None) for j in range(NWQ)]
        qs2.feed(wtiles)

        def emit_wq_some(n):
            for _ in range(n):
                if not qs2.pending():
                    return
                j = qs2.i
                qs2.tiles[j]["gates"] = [wqs["ld"][j]] if j in wqs["ld"] else []
                qs2.tiles[j]["out_gate"] = wqs["st"][j % 2]
                qs2.emit_next()

        psS = [ps_big[:, 0:1024], ps_big[:, 1024:2048]]
        psAccA = ps_big[:, 2048:2560]
        psAccB = ps_big[:, 2560:3072]
        psRbA = ps_sm[:, 0:512]
        psRbB = ps_sm[:, 512:1024]
        psM = ps_big[:, 0:512]
        psQ2 = ps_big[:, 512:1024]
        psK2 = ps_sm[:, 0:512]
        vn4 = vn.rearrange("p (hh kc o) -> p hh kc o", kc=NTC, o=65)

        exp_tok = {}
        mm2_tok = {}
        acc_free = None
        rb_free = None
        sh_prev = None
        shev_prev = None
        cat_st = [None, None]
        prod_pe_gate = None
        m8_pe_gate = None

        qs2.start()
        for p in range(NP):
            hA, hB = 2 * p, 2 * p + 1
            wz = [t for t in (sh_prev, exp_tok.get(0), exp_tok.get(1)) if t]
            tz = plan.dv(lambda e: e.memset(kmax2[:, 0:1], 0.0), wz)
            tred = None
            for tt in range(NQT):
                sl = p * T + tt * 512
                w = ([tred] if tred else []) + ([prod_pe_gate] if prod_pe_gate else [])
                tq = plan.dv(lambda e, sl=sl: e.tensor_tensor(
                    prod, kT[:, sl:sl + 512], kT[:, sl:sl + 512], ALU.mult), w)
                wk2 = [tq, tz] + ([shev_prev] if (tt == 0 and shev_prev) else [])
                tm = plan.pe(lambda e: e.matmul(psK2[0:33, :], ones33[:, 0:33], prod,
                                                start=True, stop=True), wk2)
                prod_pe_gate = tm
                tred = plan.dv(lambda e: e.reduce_max(kmax2[0:33, 1:2], psK2[0:33, :],
                                                      AX.X), [tm])
                plan.dv_drain()
                tred = plan.dv(lambda e: e.tensor_max(kmax2[0:33, 0:1], kmax2[0:33, 0:1],
                                                      kmax2[0:33, 1:2]))
                plan.dv_drain()
            tkm = plan.ac(lambda e: e.activation(kmax2[0:33, 2:3], kmax2[0:33, 0:1],
                                                 ACTF.Sqrt), [tred])
            t_m2_prev = None
            t_sq_prev = None
            for tt in range(NQT):
                sl = p * T + tt * 512
                slm = tt * 512
                w1 = [tred, prod_pe_gate] + ([m8_pe_gate] if (tt == 0 and m8_pe_gate) else [])
                tq1 = plan.dv(lambda e, sl=sl: e.tensor_tensor(
                    prod, qT[:, sl:sl + 512], kT[:, sl:sl + 512], ALU.mult), w1)
                tm1 = plan.pe(lambda e: e.matmul(psM[0:33, :], ones33[:, 0:33], prod,
                                                 start=True, stop=True),
                              [tq1] + ([t_m2_prev] if t_m2_prev else []))
                tq2 = plan.dv(lambda e, sl=sl: e.tensor_tensor(
                    prod, qT[:, sl:sl + 512], qT[:, sl:sl + 512], ALU.mult), [tm1])
                tm2 = plan.pe(lambda e: e.matmul(psQ2[0:33, :], ones33[:, 0:33], prod,
                                                 start=True, stop=True),
                              [tq2] + ([t_sq_prev] if t_sq_prev else []))
                prod_pe_gate = tm2
                tsq = plan.ac(lambda e: e.activation(qn_sb[0:33, :], psQ2[0:33, :],
                                                     ACTF.Sqrt), [tm2, tkm])
                t_sq_prev = tsq
                plan.dv(lambda e: e.tensor_scalar(
                    mt_sb[0:33, :], qn_sb[0:33, :], kmax2[0:33, 2:3], 0.125,
                    ALU.mult, ALU.mult), [tsq])
                plan.dv(lambda e: e.tensor_scalar_sub(mt_sb[0:33, :], mt_sb[0:33, :], 70.0))
                t_m2_prev = plan.dv(lambda e: e.tensor_scalar_mul(
                    m2_sb[0:33, :], psM[0:33, :], 0.125))
                plan.dv_drain()
                plan.dv(lambda e: e.tensor_max(mt_sb[0:33, :], mt_sb[0:33, :],
                                               m2_sb[0:33, :]))
                t_m8w = plan.dv(lambda e, slm=slm: e.tensor_scalar_mul(
                    m8[0:33, slm:slm + 512], mt_sb[0:33, :], -8.0))
            plan.dv_drain()
            mblock_done = [t_m8w, t_sq_prev, t_m2_prev]

            for qt in range(NQT):
                nkc = 4 * qt + 4
                gates = {}

                def emit_mm2(j, nkc=nkc, hA=hA, hB=hB, gates=gates):
                    nonlocal acc_free
                    sbj = j % 2
                    st0, sp = (j == 0), (j == nkc - 1)
                    wj = [gates[j]]
                    if j == 0 and acc_free:
                        wj = wj + [acc_free]
                    plan.pe(lambda e, hA=hA, j=j, sbj=sbj, st0=st0, sp=sp:
                            e.matmul(psAccA[0:65, :], vn4[:, hA, j, :],
                                     pt[sbj][:, 0:512], start=st0, stop=sp), wj)
                    tj = plan.pe(lambda e, hB=hB, j=j, sbj=sbj, st0=st0, sp=sp:
                                 e.matmul(psAccB[0:65, :], vn4[:, hB, j, :],
                                          pt[sbj][:, 512:1024], start=st0, stop=sp))
                    mm2_tok[sbj] = tj
                    return tj

                for kc in range(nkc):
                    sb = kc % 2
                    w1 = [exp_tok.get(sb)] if exp_tok.get(sb) else []
                    if qt == 0 and kc <= 1:
                        w1 = w1 + mblock_done
                    plan.pe(lambda e, p=p, qt=qt, kc=kc, sb=sb:
                            e.matmul(psS[sb][:, 0:512],
                                     kT[0:64, p * T + kc * 128: p * T + kc * 128 + 128],
                                     qT[0:64, p * T + qt * 512: p * T + qt * 512 + 512],
                                     start=True, stop=False), w1)
                    plan.pe(lambda e, qt=qt, sb=sb:
                            e.matmul(psS[sb][:, 0:512], ones_bf[0:1, 0:128],
                                     m8[0:1, qt * 512: qt * 512 + 512],
                                     start=False, stop=True))
                    plan.pe(lambda e, p=p, qt=qt, kc=kc, sb=sb:
                            e.matmul(psS[sb][:, 512:1024],
                                     kT[64:128, p * T + kc * 128: p * T + kc * 128 + 128],
                                     qT[64:128, p * T + qt * 512: p * T + qt * 512 + 512],
                                     start=True, stop=False))
                    t_mm1 = plan.pe(lambda e, qt=qt, sb=sb:
                                    e.matmul(psS[sb][:, 512:1024], ones_bf[32:33, 0:128],
                                             m8[32:33, qt * 512: qt * 512 + 512],
                                             start=False, stop=True))
                    m8_pe_gate = t_mm1
                    if kc >= 1:
                        emit_mm2(kc - 1)
                    we = [t_mm1] + ([mm2_tok.get(sb)] if mm2_tok.get(sb) else [])
                    texp = plan.ac(lambda e, sb=sb: e.activation(
                        pt[sb], psS[sb], ACTF.Exp, scale=0.125), we)
                    exp_tok[sb] = texp
                    gate = texp
                    if kc >= 4 * qt:
                        off = 384 - (kc * 128 - qt * 512)
                        cm = cmask[:, off:off + 512].unsqueeze(1).broadcast_to([128, 2, 512])
                        pt3 = pt[sb].rearrange("p (t n) -> p t n", t=2)
                        gate = plan.dv(lambda e, sb=sb, cm=cm, pt3=pt3:
                                       e.tensor_tensor(pt3, pt3, cm, ALU.mult), [texp])
                    gates[kc] = gate
                t_last_mm2 = emit_mm2(nkc - 1)
                wrb = [rb_free] if rb_free else []
                plan.dv(lambda e: e.tensor_copy(rsum[64:65, 0:512], psAccA[64:65, :]),
                        [t_last_mm2] + wrb)
                plan.dv(lambda e: e.tensor_copy(rsum[64:65, 512:1024], psAccB[64:65, :]))
                plan.dv_drain()
                trc = plan.dv(lambda e: e.reciprocal(rsum[64:65, :], rsum[64:65, :]))
                wsh = [shev_prev] if shev_prev else []
                plan.pe(lambda e: e.matmul(psRbA[0:64, :], ones_all[64:65, 0:64],
                                           rsum[64:65, 0:512], start=True, stop=True),
                        [trc] + wsh)
                trb = plan.pe(lambda e: e.matmul(psRbB[0:64, :], ones_all[64:65, 0:64],
                                                 rsum[64:65, 512:1024], start=True, stop=True))
                plan.dv(lambda e: e.tensor_copy(rb[0:64, 0:512], psRbA[0:64, :]), [trb])
                t_rbev = plan.dv(lambda e: e.tensor_copy(rb[0:64, 512:1024], psRbB[0:64, :]))
                rb_free = t_rbev
                cb = (p * NQT + qt) % 2
                wcat = [cat_st[cb]] if cat_st[cb] else []
                plan.dv(lambda e, cb=cb: e.tensor_tensor(
                    catb[cb][0:64, :], psAccA[0:64, :], rb[0:64, 0:512], ALU.mult), wcat)
                wstg = [sh_prev] if sh_prev else []
                t_nB = plan.dv(lambda e: e.tensor_tensor(
                    stgB[0:64, :], psAccB[0:64, :], rb[0:64, 512:1024], ALU.mult), wstg)
                acc_free = t_nB
                t_sh = plan.pe(lambda e: e.matmul(ps_sm[64:128, 0:512], eye64[0:64, :],
                                                  stgB[0:64, :], start=True, stop=True),
                               [t_nB, t_rbev])
                sh_prev = t_sh
                t_shev = plan.dv(lambda e, cb=cb: e.tensor_copy(
                    catb[cb][64:128, :], ps_sm[64:128, 0:512]), [t_sh])
                shev_prev = t_shev
                cat_st[cb] = plan.st(lambda e, p=p, qt=qt, cb=cb: e.dma_start(
                    out=s_catT[p * 128:(p + 1) * 128, qt * 512:qt * 512 + 512],
                    in_=catb[cb]), [t_shev])
                emit_wq_some(2)
        while qs2.pending():
            emit_wq_some(1)
        plan.barrier()
        cv.pop()
        plan.barrier()
        cv.pop()
        plan.barrier()

        cv.push()
        Wgq = cv.take(NCC * I, BF16, "Wgq")
        Wuq = cv.take(NCC * I, BF16, "Wuq")
        cv.push()
        t_wq_done = plan.barrier_token("ST")
        t_wg_tr = None
        for ccx in range(NCC):
            plan.ld(lambda e, ccx=ccx: e.dma_start_transpose(
                Wgq[:, ccx * I:(ccx + 1) * I], s_wgq[:, ccx * 128:(ccx + 1) * 128]),
                [t_wq_done])
            t_wg_tr = plan.ld(lambda e, ccx=ccx: e.dma_start_transpose(
                Wuq[:, ccx * I:(ccx + 1) * I], s_wuq[:, ccx * 128:(ccx + 1) * 128]))

        qtmp5 = carve_qtmp()
        xq_sb = [cv.take(NCC * 128, BF16, "xq0"), cv.take(NCC * 128, BF16, "xq1")]
        sg_sb = cv.take(I, F32, "sg")
        h_sb = cv.take(I, F32, "h")
        hq_sb = [cv.take(I, BF16, "hq0"), cv.take(I, BF16, "hq1")]
        qs3 = QSeq(plan, qtmp5)
        hstate = {"st": [None, None]}
        xq_used = [None, None]

        def mk_hdone(tcf, half):
            def done(tok):
                if half == 1:
                    b = tcf % 2
                    hstate["st"][b] = plan.st(lambda e: e.dma_start(
                        out=s_hq[tcf * 128:(tcf + 1) * 128, :], in_=hq_sb[b]), [tok])
            return done

        psH = [ps_big[:, 0:1408], ps_big[:, 1536:2944]]
        segs3 = [(0, 512), (512, 512), (1024, 384)]
        gu_free = [None, None]
        pend_tiles = []
        started = False
        for tcf in range(NTC):
            b = tcf % 2
            tldx = plan.ld(lambda e, tcf=tcf, b=b: e.dma_start(
                out=xq_sb[b].rearrange("p (cc n) -> p cc n", n=128),
                in_=s_xqT.rearrange("(cc p) m -> p cc m", p=128)[:, :, tcf * 128:(tcf + 1) * 128]),
                [xq_used[b]] if xq_used[b] else [])
            xqv = xq_sb[b].rearrange("p (cc n) -> p cc n", n=128)
            tsilu = [None, None]
            tg = [None, None]
            for half in (0, 1):
                hofs = half * 1408
                for ccx in range(NCC):
                    for si, (s0, sw) in enumerate(segs3):
                        wts = []
                        if ccx == 0 and si == 0:
                            wts = [tldx, t_wg_tr] + ([gu_free[half]] if gu_free[half] else [])
                        tg[half] = plan.pe(
                            lambda e, ccx=ccx, s0=s0, sw=sw, xqv=xqv, half=half, hofs=hofs:
                            e.matmul(psH[half][:, s0:s0 + sw], xqv[:, ccx, :],
                                     Wgq[:, ccx * I + hofs + s0: ccx * I + hofs + s0 + sw],
                                     start=(ccx == 0), stop=(ccx == NCC - 1)), wts)
                tsilu[half] = plan.ac(
                    lambda e, half=half, hofs=hofs: e.activation(
                        sg_sb[:, hofs:hofs + 1408], psH[half], ACTF.Silu), [tg[half]])
            for half in (0, 1):
                hofs = half * 1408
                tu = None
                for ccx in range(NCC):
                    for si, (s0, sw) in enumerate(segs3):
                        wts = [tsilu[half]] if (ccx == 0 and si == 0) else []
                        tu = plan.pe(
                            lambda e, ccx=ccx, s0=s0, sw=sw, xqv=xqv, half=half, hofs=hofs:
                            e.matmul(psH[half][:, s0:s0 + sw], xqv[:, ccx, :],
                                     Wuq[:, ccx * I + hofs + s0: ccx * I + hofs + s0 + sw],
                                     start=(ccx == 0), stop=(ccx == NCC - 1)), wts)
                if half == 1:
                    xq_used[b] = tu
                th = plan.dv(lambda e, half=half, hofs=hofs: e.tensor_tensor(
                    h_sb[:, hofs:hofs + 1408], sg_sb[:, hofs:hofs + 1408],
                    psH[half], ALU.mult), [tu])
                gu_free[half] = th
                pend_tiles.append(dict(x=h_sb[:, hofs:hofs + 1408],
                                       out=hq_sb[b][:, hofs:hofs + 1408],
                                       w=1408, gates=[th], prep=None,
                                       on_done=mk_hdone(tcf, half),
                                       out_gate=hstate["st"][b]))
            qs3.feed(pend_tiles)
            pend_tiles = []
            if not started:
                qs3.start()
                started = True
            qs3.emit_next()
            qs3.emit_next()
        plan.barrier()
        cv.pop()
        cv.pop()
        plan.barrier()

        cv.push()
        Wdq = cv.take(NIC * 1024, BF16, "Wdq")
        _wu_dead = cv.take(NCC * I, BF16, "wud")
        wpT = cv.take(NCC * 1024, BF16, "wpT")
        hqT_sb = [cv.take(NIC * 128, BF16, "hqT0"), cv.take(NIC * 128, BF16, "hqT1")]
        cat_sb = [cv.take(NCC * 128, BF16, "catA"), cv.take(NCC * 128, BF16, "catB")]
        stgT = [cv.take(T, BF16, "stgT0"), cv.take(T, BF16, "stgT1")]
        out_t = [cv.take(C, F32, "o0"), cv.take(C, F32, "o1")]

        t_hq_all = plan.barrier_token("ST")
        stg_tok2 = [None, None]
        for icx in range(NIC):
            b = icx % 2
            tt = plan.ld(lambda e, icx=icx, b=b: e.dma_start_transpose(
                stgT[b], s_hq[:, icx * 128:(icx + 1) * 128]),
                [t_hq_all] + ([stg_tok2[b]] if stg_tok2[b] else []))
            stg_tok2[b] = plan.st(lambda e, icx=icx, b=b: e.dma_start(
                out=s_hqT[icx * 128:(icx + 1) * 128, :], in_=stgT[b]), [tt])
        t_wd_tr = None
        for icx in range(NIC):
            t_wd_tr = plan.ld(lambda e, icx=icx: e.dma_start_transpose(
                Wdq[:, icx * 1024:(icx + 1) * 1024], s_wdq[:, icx * 128:(icx + 1) * 128]),
                [t_wq_done])
        t_wp = plan.gp(lambda e: e.dma_start(
            out=wpT.rearrange("p (cc n) -> p cc n", n=1024),
            in_=wpT_h.rearrange("(cc p) n -> p cc n", p=128)))
        t_hqT_done = plan.barrier_token("ST")

        psD = [ps_big[:, 0:1024], ps_big[:, 1024:2048]]
        dn_used = [None, None]
        dn_psfree = [None, None]
        out_st = [None, None]
        for tcd in range(NTC):
            b = tcd % 2
            wl = [t_hqT_done, t_wd_tr, t_wp] if tcd < 2 else []
            tld1 = plan.ld(lambda e, tcd=tcd, b=b: e.dma_start(
                out=hqT_sb[b].rearrange("p (ic n) -> p ic n", n=128),
                in_=s_hqT.rearrange("(ic p) m -> p ic m", p=128)[:, :, tcd * 128:(tcd + 1) * 128]),
                wl + [t for t in (dn_used[b], dn_psfree[b]) if t])
            tld2 = plan.ld(lambda e, tcd=tcd, b=b: e.dma_start(
                out=cat_sb[b].rearrange("p (cc n) -> p cc n", n=128),
                in_=s_catT.rearrange("(cc p) m -> p cc m", p=128)[:, :, tcd * 128:(tcd + 1) * 128]))
            tld3 = plan.ld(lambda e, tcd=tcd, b=b: e.dma_start(
                out=x_t[b], in_=x_h[tcd * 128:(tcd + 1) * 128, :]))
            hqv = hqT_sb[b].rearrange("p (ic n) -> p ic n", n=128)
            catv = cat_sb[b].rearrange("p (cc n) -> p cc n", n=128)
            tpe = None
            first = True
            wts0 = [tld1, tld2] + ([dn_psfree[b]] if dn_psfree[b] else [])
            for icx in range(NIC):
                plan.pe(lambda e, icx=icx, b=b, hqv=hqv, first=first:
                        e.matmul(psD[b][:, 0:512], hqv[:, icx, :],
                                 Wdq[:, icx * 1024: icx * 1024 + 512],
                                 start=first, stop=False), wts0 if first else [])
                tpe = plan.pe(lambda e, icx=icx, b=b, hqv=hqv, first=first:
                              e.matmul(psD[b][:, 512:1024], hqv[:, icx, :],
                                       Wdq[:, icx * 1024 + 512: icx * 1024 + 1024],
                                       start=first, stop=False))
                first = False
            for ccx in range(NCC):
                plan.pe(lambda e, ccx=ccx, b=b, catv=catv:
                        e.matmul(psD[b][:, 0:512], catv[:, ccx, :],
                                 wpT[:, ccx * 1024: ccx * 1024 + 512],
                                 start=False, stop=False))
                tpe = plan.pe(lambda e, ccx=ccx, b=b, catv=catv:
                              e.matmul(psD[b][:, 512:1024], catv[:, ccx, :],
                                       wpT[:, ccx * 1024 + 512: ccx * 1024 + 1024],
                                       start=False, stop=False))
            plan.pe(lambda e, b=b: e.matmul(psD[b][:, 0:512], ones_all[0:1, 0:128],
                                            bproj[0:1, 0:512], start=False, stop=True))
            tpe = plan.pe(lambda e, b=b: e.matmul(psD[b][:, 512:1024], ones_all[0:1, 0:128],
                                                  bproj[0:1, 512:1024], start=False, stop=True))
            dn_used[b] = tpe
            tep = plan.dv(lambda e, b=b: e.tensor_tensor(out_t[b], psD[b], x_t[b], ALU.add),
                          [tpe, tld3] + ([out_st[b]] if out_st[b] else []))
            dn_psfree[b] = tep
            out_st[b] = plan.st(lambda e, tcd=tcd, b=b: e.dma_start(
                out=out_h[tcd * 128:(tcd + 1) * 128, :], in_=out_t[b]), [tep])
        cv.pop()

        plan.emit("scalar", None, [("ST", plan.cnt["ST"])])
        plan.emit("sync", None, [("ST", plan.cnt["ST"])])

        def run_plan(eng_obj, name):
            for fn, waits, inc in plan.ops[name]:
                for s, v in waits:
                    eng_obj.wait_ge(sems[s], v)
                if fn is None:
                    continue
                inst = fn(eng_obj)
                if inc is not None:
                    s, d = inc
                    inst.then_inc(sems[s], d)

        @block.sync
        def _(sync):
            run_plan(sync, "sync")

        @block.scalar
        def _(scalar):
            run_plan(scalar, "scalar")

        @block.vector
        def _(vector):
            run_plan(vector, "vector")

        @block.tensor
        def _(tensor):
            run_plan(tensor, "tensor")

        @block.gpsimd
        def _(gpsimd):
            run_plan(gpsimd, "gpsimd")

    return nc


_NC_CACHE = None


def kernel(L0_attn_x, L0_attn_w_qkv, L0_attn_w_proj, L0_attn_b_proj,
           L0_ffn_w_gate, L0_ffn_w_up, L0_ffn_w_down):
    global _NC_CACHE
    x = np.ascontiguousarray(np.asarray(L0_attn_x, dtype=np.float32))
    wqkv = np.asarray(L0_attn_w_qkv, dtype=np.float32)
    wproj = np.asarray(L0_attn_w_proj, dtype=np.float32)
    bproj = np.asarray(L0_attn_b_proj, dtype=np.float32)
    wg = np.ascontiguousarray(np.asarray(L0_ffn_w_gate, dtype=np.float32))
    wu = np.ascontiguousarray(np.asarray(L0_ffn_w_up, dtype=np.float32))
    wd = np.ascontiguousarray(np.asarray(L0_ffn_w_down, dtype=np.float32))

    wT = wqkv.transpose(2, 0, 1)
    blocks = []
    for p in range(NP):
        blocks += [wT[:, 2 * p, 0:64], wT[:, 2 * p + 1, 0:64],
                   wT[:, 2 * p, 64:128], wT[:, 2 * p + 1, 64:128]]
    wqkT = np.ascontiguousarray(np.concatenate(blocks, axis=1))
    wvT = np.ascontiguousarray(wT[:, :, 128:192].reshape(C, C))
    wpT = np.ascontiguousarray(wproj.T)
    bproj2 = np.ascontiguousarray(bproj.reshape(1, C))

    jj = np.arange(896)[None, :]
    kk = np.arange(128)[:, None]
    cmask = ((jj - kk) >= 384).astype(ml_dtypes.bfloat16)
    ones_all = np.ones((128, 128), dtype=np.float32)
    ones_bf = np.ones((128, 128), dtype=ml_dtypes.bfloat16)
    ones33 = np.zeros((128, 33), dtype=ml_dtypes.bfloat16)
    ones33[0:64, 0] = 1
    ones33[64:128, 32] = 1
    eye64 = np.eye(64, dtype=ml_dtypes.bfloat16)

    if _NC_CACHE is None:
        _NC_CACHE = build_program()
    nc = _NC_CACHE

    in_maps = []
    for b in range(8):
        in_maps.append({
            "x": np.ascontiguousarray(x[b]),
            "wqkT": wqkT, "wvT": wvT, "wpT": wpT, "bproj": bproj2,
            "w_gate": wg, "w_up": wu, "w_down": wd,
            "cmask": cmask, "ones_all": ones_all, "ones_bf": ones_bf,
            "ones33": ones33, "eye64": eye64,
        })
    res = run_bass_kernel_spmd(nc, in_maps, core_ids=list(range(8)))
    out = np.stack([np.asarray(r["out"]) for r in res.results], axis=0)
    return out.astype(np.float32)


# version 06
# speedup vs baseline: 1.2090x; 1.2090x over previous
import numpy as np
import ml_dtypes

import concourse.bass as bass
import concourse.mybir as mybir
from concourse.bass_utils import run_bass_kernel_spmd

F32 = mybir.dt.float32
BF16 = mybir.dt.bfloat16
U8 = mybir.dt.uint8
ALU = mybir.AluOpType
ACTF = mybir.ActivationFunctionType
AX = mybir.AxisListType

T, C, H, HS, I = 2048, 1024, 16, 64, 2816
NTC, NCC, NIC, NP, NQT = T // 128, C // 128, I // 128, H // 2, T // 512
ARENA_KB = 206
QW = 1408


class Carver:
    def __init__(self, arena_ap):
        self.arena = arena_ap
        self.off = 0
        self.marks = []

    def take(self, n_elems, dtype, name=""):
        esz = 2 if dtype == BF16 else 4
        nbytes = n_elems * esz
        off = (self.off + 63) & ~63
        self.off = off + nbytes
        assert self.off <= ARENA_KB * 1024, f"arena overflow at {name}: {self.off}"
        return self.arena[:, off:off + nbytes].bitcast(dtype)

    def push(self):
        self.marks.append(self.off)

    def pop(self):
        self.off = self.marks.pop()


class Plan:
    ENGINES = ("sync", "scalar", "vector", "tensor", "gpsimd")

    def __init__(self):
        self.ops = {e: [] for e in self.ENGINES}
        self.cnt = {"LD": 0, "ST": 0, "GP": 0, "PE": 0, "DV": 0, "AC": 0}

    def emit(self, eng, fn, waits=(), inc=None):
        w = [(s, v) for (s, v) in waits if v and v > 0]
        tok = None
        if inc is not None:
            s, d = inc
            self.cnt[s] += d
            tok = (s, self.cnt[s])
        self.ops[eng].append((fn, w, inc))
        return tok

    def dv(self, fn, waits=()):
        return self.emit("vector", fn, waits, ("DV", 1))

    def ac(self, fn, waits=()):
        return self.emit("scalar", fn, waits, ("AC", 1))

    def pe(self, fn, waits=()):
        return self.emit("tensor", fn, waits, ("PE", 1))

    def ld(self, fn, waits=()):
        return self.emit("sync", fn, waits, ("LD", 16))

    def st(self, fn, waits=()):
        return self.emit("scalar", fn, waits, ("ST", 16))

    def gp(self, fn, waits=()):
        return self.emit("gpsimd", fn, waits, ("GP", 16))

    def dv_drain(self):
        self.emit("vector", lambda v: v.drain())

    def barrier_token(self, sem):
        return (sem, self.cnt[sem])

    def barrier(self):
        toks = [(s, self.cnt[s]) for s in ("LD", "ST", "GP", "PE", "DV", "AC")]
        for e in self.ENGINES:
            self.ops[e].append((None, [(s, v) for (s, v) in toks if v > 0], None))


class QSeq:

    def __init__(self, plan, tmp):
        self.plan = plan
        self.tmp = tmp
        self.tiles = []
        self.i = 0
        self.heads_done = -1
        self._gen = self._generator()
        self.exhausted = False

    def feed(self, tiles):
        self.tiles.extend(tiles)
        self.exhausted = False

    def _head1(self, j):
        t = self.tiles[j]
        b = j % 2
        nb = t["w"] // 16
        rmax = self.tmp["rmax"][b][:, 0:nb]
        x3 = t["x"].rearrange("p (nb w) -> p nb w", w=16)
        return self.plan.dv(lambda v, rmax=rmax, x3=x3: v.tensor_reduce(
            rmax, x3, AX.X, ALU.max, apply_absolute_value=True), t["gates"])

    def _head2(self, j):
        b = j % 2
        nb = self.tiles[j]["w"] // 16
        rmax = self.tmp["rmax"][b][:, 0:nb]
        rr12 = self.tmp["rr12"][b][:, 0:nb]
        self.plan.dv(lambda v, rmax=rmax: v.tensor_scalar_max(rmax, rmax, 6e-12))
        return self.plan.dv(lambda v, rr12=rr12, rmax=rmax: v.reciprocal(rr12, rmax))

    def _head3(self, j):
        b = j % 2
        nb = self.tiles[j]["w"] // 16
        rmax = self.tmp["rmax"][b][:, 0:nb]
        rr12 = self.tmp["rr12"][b][:, 0:nb]
        s2 = self.tmp["s2"][b][:, 0:nb]
        self.plan.dv(lambda v, rr12=rr12: v.tensor_scalar_mul(rr12, rr12, 12.0))
        return self.plan.dv(lambda v, s2=s2, rmax=rmax: v.tensor_scalar_mul(
            s2, rmax, 1.0 / 12.0))

    def _emit_heads_guarded(self, j):
        if self.tiles[j].get("prep"):
            self.tiles[j]["prep"]()
            self.plan.dv_drain()
        self._head1(j)
        self.plan.dv_drain()
        self._head2(j)
        self.plan.dv_drain()
        self._head3(j)
        self.plan.dv_drain()
        self.heads_done = j

    def _generator(self):
        plan, tmp = self.plan, self.tmp
        while True:
            if self.i >= len(self.tiles):
                yield "idle"
                continue
            i = self.i
            if self.heads_done < i:
                self._emit_heads_guarded(i)
                yield 1
            t = self.tiles[i]
            b = i % 2
            w = t["w"]
            nb = w // 16
            rr12 = tmp["rr12"][b][:, 0:nb]
            s2 = tmp["s2"][b][:, 0:nb]
            v2 = tmp["v2"][:, 0:w]
            u = tmp["u"][:, 0:w]
            g = tmp["g"][:, 0:w]
            hh = tmp["hh"][:, 0:w]
            sgn = tmp["sgn"][:, 0:w]
            x3 = t["x"].rearrange("p (nb w) -> p nb w", w=16)
            v23 = v2.rearrange("p (nb w) -> p nb w", w=16)
            rr12_bc = rr12.unsqueeze(2).broadcast_to([128, nb, 16])
            s2_bc = s2.unsqueeze(2).broadcast_to([128, nb, 16])

            have_next = (i + 1 < len(self.tiles)) and (self.heads_done == i)
            if have_next and self.tiles[i + 1].get("prep"):
                self.tiles[i + 1]["prep"]()
            plan.dv(lambda v, v23=v23, x3=x3, rr12_bc=rr12_bc:
                    v.tensor_tensor(v23, x3, rr12_bc, ALU.mult), list(t["gates"]))
            yield 1
            plan.dv(lambda v, u=u, v2=v2: v.tensor_tensor(u, v2, v2, ALU.mult))
            yield 1
            plan.dv(lambda v, sgn=sgn, v2=v2: v.tensor_scalar(
                sgn, v2, 0.0, 2.0, ALU.is_gt, ALU.mult))
            plan.dv(lambda v, sgn=sgn: v.tensor_scalar_sub(sgn, sgn, 1.0))
            yield 1
            if have_next:
                self._head1(i + 1)
                yield 1
            plan.dv(lambda v, g=g, u=u: v.tensor_single_scalar(g, u, 0.25, ALU.is_gt))
            yield 1
            plan.dv(lambda v, g=g, u=u: v.scalar_tensor_tensor(g, u, 2.25, g, ALU.is_gt, ALU.add))
            yield 1
            plan.dv(lambda v, g=g, u=u: v.scalar_tensor_tensor(g, u, 6.25, g, ALU.is_gt, ALU.add))
            yield 1
            plan.dv(lambda v, g=g, u=u: v.scalar_tensor_tensor(g, u, 12.25, g, ALU.is_gt, ALU.add))
            yield 1
            if have_next:
                self._head2(i + 1)
                yield 1
            plan.dv(lambda v, hh=hh, u=u: v.tensor_single_scalar(hh, u, 100.0, ALU.is_gt))
            yield 1
            plan.dv(lambda v, hh=hh, u=u: v.scalar_tensor_tensor(hh, u, 25.0, hh, ALU.is_gt, ALU.add))
            yield 1
            plan.dv(lambda v, hh=hh, u=u: v.scalar_tensor_tensor(hh, u, 49.0, hh, ALU.is_gt, ALU.add))
            yield 1
            plan.dv(lambda v, hh=hh, u=u: v.scalar_tensor_tensor(hh, u, 100.0, hh, ALU.is_gt, ALU.add))
            yield 1
            if have_next:
                self._head3(i + 1)
                self.heads_done = i + 1
                yield 1
            plan.dv(lambda v, g=g, hh=hh: v.scalar_tensor_tensor(g, hh, 2.0, g, ALU.mult, ALU.add))
            yield 1
            plan.dv(lambda v, g=g, sgn=sgn: v.tensor_tensor(g, g, sgn, ALU.mult))
            yield 1
            o3 = t["out"].rearrange("p (nb w) -> p nb w", w=16)
            g3 = g.rearrange("p (nb w) -> p nb w", w=16)
            wout = [t["out_gate"]] if t.get("out_gate") else []
            tok = plan.dv(lambda v, o3=o3, g3=g3, s2_bc=s2_bc:
                          v.tensor_tensor(o3, g3, s2_bc, ALU.mult), wout)
            if t.get("on_done"):
                t["on_done"](tok)
            self.i += 1
            yield "tile"

    def fill(self, n):
        for _ in range(n):
            if self.i >= len(self.tiles):
                return
            next(self._gen)

    def emit_next(self):
        i0 = self.i
        while self.i == i0:
            next(self._gen)

    def start(self):
        pass

    def pending(self):
        return len(self.tiles) - self.i


def build_program():
    nc = bass.Bass("TRN2", target_bir_lowering=False)

    x_h = nc.dram_tensor("x", [T, C], F32, kind="ExternalInput").ap()
    wqkT_h = nc.dram_tensor("wqkT", [C, 2048], F32, kind="ExternalInput").ap()
    wvT_h = nc.dram_tensor("wvT", [C, C], F32, kind="ExternalInput").ap()
    wpT_h = nc.dram_tensor("wpT", [C, C], F32, kind="ExternalInput").ap()
    bproj_h = nc.dram_tensor("bproj", [1, C], F32, kind="ExternalInput").ap()
    wg_h = nc.dram_tensor("w_gate", [I, C], F32, kind="ExternalInput").ap()
    wu_h = nc.dram_tensor("w_up", [I, C], F32, kind="ExternalInput").ap()
    wd_h = nc.dram_tensor("w_down", [C, I], F32, kind="ExternalInput").ap()
    cmask_h = nc.dram_tensor("cmask", [128, 896], BF16, kind="ExternalInput").ap()
    ones_h = nc.dram_tensor("ones_all", [128, 128], F32, kind="ExternalInput").ap()
    onesb_h = nc.dram_tensor("ones_bf", [128, 128], BF16, kind="ExternalInput").ap()
    ones33_h = nc.dram_tensor("ones33", [128, 33], BF16, kind="ExternalInput").ap()
    eye_h = nc.dram_tensor("eye64", [64, 64], BF16, kind="ExternalInput").ap()
    out_h = nc.dram_tensor("out", [T, C], F32, kind="ExternalOutput").ap()

    s_xnbf = nc.dram_tensor("s_xnbf", [T, C], BF16, kind="Internal").ap()
    s_xq = nc.dram_tensor("s_xq", [T, C], BF16, kind="Internal").ap()
    s_xqT = nc.dram_tensor("s_xqT", [C, T], BF16, kind="Internal").ap()
    s_hq = nc.dram_tensor("s_hq", [T, I], BF16, kind="Internal").ap()
    s_hqT = nc.dram_tensor("s_hqT", [I, T], BF16, kind="Internal").ap()
    s_catT = nc.dram_tensor("s_catT", [C, T], BF16, kind="Internal").ap()
    s_wgq = nc.dram_tensor("s_wgq", [I, C], BF16, kind="Internal").ap()
    s_wuq = nc.dram_tensor("s_wuq", [I, C], BF16, kind="Internal").ap()
    s_wdq = nc.dram_tensor("s_wdq", [C, I], BF16, kind="Internal").ap()

    plan = Plan()

    with (
        nc.sbuf_tensor("arena", [128, ARENA_KB * 1024], U8) as arena_t,
        nc.psum_tensor("ps_big", [128, 3072], F32) as ps_big,
        nc.psum_tensor("ps_sm", [128, 1024], F32) as ps_sm,
        nc.semaphore("LD") as sem_LD,
        nc.semaphore("ST") as sem_ST,
        nc.semaphore("GP") as sem_GP,
        nc.semaphore("PE") as sem_PE,
        nc.semaphore("DV") as sem_DV,
        nc.semaphore("AC") as sem_AC,
        nc.Block() as block,
    ):
        sems = {"LD": sem_LD, "ST": sem_ST, "GP": sem_GP,
                "PE": sem_PE, "DV": sem_DV, "AC": sem_AC}
        cv = Carver(arena_t[:])

        cmask = cv.take(896, BF16, "cmask")
        ones_all = cv.take(128, F32, "ones")
        ones_bf = cv.take(128, BF16, "onesb")
        ones33 = cv.take(33, BF16, "ones33")
        eye64 = cv.take(64, BF16, "eye")
        bproj = cv.take(C, F32, "bproj")
        rsum = cv.take(1024, F32, "rsum")
        rb = cv.take(1024, F32, "rb")
        x_t = [cv.take(C, F32, "x0"), cv.take(C, F32, "x1")]
        catb = [cv.take(512, BF16, "cat0"), cv.take(512, BF16, "cat1")]
        stgB = cv.take(512, BF16, "stgB")

        def carve_qtmp():
            return {
                "rmax": [cv.take(QW // 16, F32), cv.take(QW // 16, F32)],
                "rr12": [cv.take(QW // 16, F32), cv.take(QW // 16, F32)],
                "s2": [cv.take(QW // 16, F32), cv.take(QW // 16, F32)],
                "v2": cv.take(QW, F32), "u": cv.take(QW, F32),
                "g": cv.take(QW, F32), "hh": cv.take(QW, F32),
                "sgn": cv.take(QW, F32),
            }

        plan.ld(lambda e: e.dma_start(out=cmask, in_=cmask_h))
        plan.ld(lambda e: e.dma_start(out=ones_all, in_=ones_h))
        plan.ld(lambda e: e.dma_start(out=ones_bf, in_=onesb_h))
        plan.ld(lambda e: e.dma_start(out=ones33, in_=ones33_h))
        plan.ld(lambda e: e.dma_start(out=eye64[0:64, :], in_=eye_h))
        plan.ld(lambda e: e.dma_start(out=bproj[0:1, :], in_=bproj_h))
        t_consts = plan.barrier_token("LD")

        cv.push()
        qtmp = carve_qtmp()
        stats = cv.take(4 * NTC, F32, "stats")
        msum_all = stats[:, 0:NTC]
        mean_all = stats[:, 2 * NTC:3 * NTC]
        rstd_all = stats[:, 3 * NTC:4 * NTC]
        sumsq_all = stats[:, NTC:2 * NTC]
        sqs = qtmp["v2"][:, 0:C]

        tok_xread = [None, None]
        tok_xact = [None, None]
        tok_sq = None
        for i in range(NTC):
            b = i % 2
            w = [t for t in (tok_xread[b], tok_xact[b]) if t]
            tld = plan.ld(lambda e, i=i, b=b: e.dma_start(
                out=x_t[b], in_=x_h[i * 128:(i + 1) * 128, :]), w + [t_consts])
            tok_xact[b] = plan.ac(lambda e, b=b: e.activation(sqs, x_t[b], ACTF.Square),
                                  [tld] + ([tok_sq] if tok_sq else []))
            tok_xread[b] = plan.dv(lambda e, i=i, b=b: e.reduce_sum(
                msum_all[:, i:i + 1], x_t[b], AX.X), [tld])
            tok_sq = plan.dv(lambda e, i=i: e.reduce_sum(
                sumsq_all[:, i:i + 1], sqs, AX.X), [tok_xact[b]])
        plan.dv_drain()
        plan.dv(lambda e: e.tensor_scalar_mul(mean_all, msum_all, 1.0 / C))
        plan.dv(lambda e: e.tensor_scalar(rstd_all, sumsq_all, 1.0 / C, 1e-5,
                                          ALU.mult, ALU.add))
        plan.dv_drain()
        plan.dv(lambda e: e.tensor_mul(msum_all, mean_all, mean_all))
        plan.dv_drain()
        tvar = plan.dv(lambda e: e.tensor_sub(rstd_all, rstd_all, msum_all))
        tsd = plan.ac(lambda e: e.activation(rstd_all, rstd_all, ACTF.Sqrt), [tvar])
        plan.dv(lambda e: e.reciprocal(rstd_all, rstd_all), [tsd])
        plan.dv_drain()

        xn_t = [cv.take(C, F32, "xn0"), cv.take(C, F32, "xn1")]
        bfb = [cv.take(C, BF16, "bf0"), cv.take(C, BF16, "bf1")]
        qxb = [cv.take(C, BF16, "qx0"), cv.take(C, BF16, "qx1")]

        qs1 = QSeq(plan, qtmp)
        st1 = {"apply": [None, None], "st_bf": [None, None], "st_qx": [None, None]}

        def mk_prep1(i):
            def prep():
                b = i % 2
                wld = [t for t in (st1["apply"][b], tok_xread[b], tok_xact[b]) if t]
                tld = plan.ld(lambda e: e.dma_start(
                    out=x_t[b], in_=x_h[i * 128:(i + 1) * 128, :]), wld)
                tap = plan.dv(lambda e: e.tensor_scalar(
                    xn_t[b], x_t[b], mean_all[:, i:i + 1], rstd_all[:, i:i + 1],
                    ALU.subtract, ALU.mult), [tld])
                st1["apply"][b] = tap
                tiles1[i]["gates"] = [tap]
                tiles1[i]["out_gate"] = st1["st_qx"][b]
                tcast = plan.ac(lambda e: e.copy(bfb[b], xn_t[b]),
                                [tap] + ([st1["st_bf"][b]] if st1["st_bf"][b] else []))
                st1["st_bf"][b] = plan.st(lambda e: e.dma_start(
                    out=s_xnbf[i * 128:(i + 1) * 128, :], in_=bfb[b]), [tcast])
            return prep

        def mk_done1(i):
            def done(tok):
                b = i % 2
                st1["st_qx"][b] = plan.st(lambda e: e.dma_start(
                    out=s_xq[i * 128:(i + 1) * 128, :], in_=qxb[b]), [tok])
            return done

        tiles1 = [dict(x=xn_t[i % 2], out=qxb[i % 2], w=C, gates=[],
                       prep=mk_prep1(i), on_done=mk_done1(i), out_gate=None)
                  for i in range(NTC)]
        qs1.feed(tiles1)
        while qs1.pending():
            qs1.emit_next()
        plan.barrier()
        cv.pop()

        cv.push()
        qT = cv.take(NP * T, BF16, "qT")
        kT = cv.take(NP * T, BF16, "kT")
        vn = cv.take(H * NTC * 65, BF16, "vn")
        cv.push()
        xnT = cv.take(NCC * T, BF16, "xnT")
        stg = [cv.take(T, BF16, "stg0"), cv.take(T, BF16, "stg1")]
        wqk = [cv.take(NCC * 256, BF16, "wqk0"), cv.take(NCC * 256, BF16, "wqk1")]
        wvT = cv.take(NCC * 1024, BF16, "wvT")

        t_st_all = plan.barrier_token("ST")
        for ccx in range(NCC):
            plan.ld(lambda e, ccx=ccx: e.dma_start_transpose(
                xnT[:, ccx * T:(ccx + 1) * T], s_xnbf[:, ccx * 128:(ccx + 1) * 128]),
                [t_st_all])
        stg_tok = [None, None]
        for ccx in range(NCC):
            b = ccx % 2
            tt = plan.ld(lambda e, ccx=ccx, b=b: e.dma_start_transpose(
                stg[b], s_xq[:, ccx * 128:(ccx + 1) * 128]),
                ([t_st_all] if ccx < 2 else []) + ([stg_tok[b]] if stg_tok[b] else []))
            stg_tok[b] = plan.st(lambda e, ccx=ccx, b=b: e.dma_start(
                out=s_xqT[ccx * 128:(ccx + 1) * 128, :], in_=stg[b]), [tt])
        vn3 = vn.rearrange("p (b o) -> p b o", o=65)
        t_vn_ones = plan.dv(lambda e: e.memset(vn3[:, :, 64:65], 1.0))
        wvT_v = wvT.rearrange("p (cc n) -> p cc n", n=1024)
        t_wv = plan.gp(lambda e: e.dma_start(
            out=wvT_v, in_=wvT_h.rearrange("(cc p) n -> p cc n", p=128)))
        wqk_tok = [None, None]
        wqk_used = [None, None]

        def load_wqk(p):
            b = p % 2
            wqk_tok[b] = plan.gp(lambda e, p=p, b=b: e.dma_start(
                out=wqk[b].rearrange("p (cc n) -> p cc n", n=256),
                in_=wqkT_h.rearrange("(cc p) n -> p cc n", p=128)[:, :, p * 256:(p + 1) * 256]),
                [wqk_used[b]] if wqk_used[b] else [])

        load_wqk(0)
        psQ = [ps_big[:, 0:512], ps_big[:, 1024:1536]]
        psK = [ps_big[:, 512:1024], ps_big[:, 1536:2048]]
        psV = [ps_big[:, 2048:2560], ps_big[:, 2560:3072],
               ps_sm[:, 0:512], ps_sm[:, 512:1024]]
        evacQ = {}
        t_xnT = plan.barrier_token("LD")
        for p in range(NP):
            b = p % 2
            if p + 1 < NP:
                load_wqk(p + 1)
            wqk_v = wqk[b].rearrange("p (cc n) -> p cc n", n=256)
            for qt in range(NQT):
                ib = (p * NQT + qt) % 2
                wts = [t_xnT, wqk_tok[b]] + ([evacQ.get(ib)] if evacQ.get(ib) else [])
                tpe = None
                for ccx in range(NCC):
                    st0, sp = (ccx == 0), (ccx == NCC - 1)
                    plan.pe(lambda e, qt=qt, ccx=ccx, ib=ib, wqk_v=wqk_v, st0=st0, sp=sp:
                            e.matmul(psQ[ib], wqk_v[:, ccx, 0:128],
                                     xnT[:, ccx * T + qt * 512: ccx * T + qt * 512 + 512],
                                     start=st0, stop=sp),
                            wts if ccx == 0 else [])
                    tpe = plan.pe(lambda e, qt=qt, ccx=ccx, ib=ib, wqk_v=wqk_v, st0=st0, sp=sp:
                                  e.matmul(psK[ib], wqk_v[:, ccx, 128:256],
                                           xnT[:, ccx * T + qt * 512: ccx * T + qt * 512 + 512],
                                           start=st0, stop=sp))
                plan.dv(lambda e, p=p, qt=qt, ib=ib: e.tensor_copy(
                    qT[:, p * T + qt * 512: p * T + qt * 512 + 512], psQ[ib]), [tpe])
                evacQ[ib] = plan.dv(lambda e, p=p, qt=qt, ib=ib: e.tensor_copy(
                    kT[:, p * T + qt * 512: p * T + qt * 512 + 512], psK[ib]))
            wqk_used[b] = evacQ[(p * NQT + NQT - 1) % 2]
        evacV = {}
        for tcv in range(NTC):
            vb0, vb1 = (2 * tcv) % 4, (2 * tcv + 1) % 4
            wts = [t_wv, t_vn_ones] + [evacV[q] for q in (vb0, vb1) if q in evacV]
            tpe = None
            for ccx in range(NCC):
                st0, sp = (ccx == 0), (ccx == NCC - 1)
                plan.pe(lambda e, tcv=tcv, ccx=ccx, vb0=vb0, st0=st0, sp=sp:
                        e.matmul(psV[vb0], xnT[:, ccx * T + tcv * 128: ccx * T + tcv * 128 + 128],
                                 wvT_v[:, ccx, 0:512], start=st0, stop=sp),
                        wts if ccx == 0 else [])
                tpe = plan.pe(lambda e, tcv=tcv, ccx=ccx, vb1=vb1, st0=st0, sp=sp:
                              e.matmul(psV[vb1], xnT[:, ccx * T + tcv * 128: ccx * T + tcv * 128 + 128],
                                       wvT_v[:, ccx, 512:1024], start=st0, stop=sp))
            dst3 = vn.rearrange("p (hh kc o) -> p hh kc o", kc=NTC, o=65)
            for half, q in ((0, vb0), (1, vb1)):
                src = psV[q].rearrange("p (h d) -> p h d", d=64)
                evacV[q] = plan.dv(
                    lambda e, src=src, dst3=dst3, half=half, tcv=tcv:
                    e.tensor_copy(dst3[:, half * 8:(half + 1) * 8, tcv, 0:64], src),
                    [tpe] if half == 0 else [])
        cv.pop()
        plan.barrier()

        cv.push()
        qtmp4 = carve_qtmp()
        pt = [cv.take(1024, BF16, "pt0"), cv.take(1024, BF16, "pt1")]
        m8 = cv.take(T, BF16, "m8")
        prod = cv.take(512, BF16, "prod")
        qn_sb = cv.take(512, F32, "qn")
        mt_sb = cv.take(512, F32, "mt")
        m2_sb = cv.take(512, F32, "m2t")
        kmax2 = cv.take(4, F32, "kmax2")
        wld = [cv.take(QW, F32, "wld0"), cv.take(QW, F32, "wld1")]
        wqb = [cv.take(QW, BF16, "wqb0"), cv.take(QW, BF16, "wqb1")]

        qs2 = QSeq(plan, qtmp4)
        wqs = {"ld": {}, "st": [None, None]}

        wspecs = []
        for (src_h, dst_h, nrt, width) in ((wg_h, s_wgq, NIC, C), (wu_h, s_wuq, NIC, C)):
            for r in range(nrt):
                wspecs.append((src_h, dst_h, r, 0, width))
        for r in range(NCC):
            wspecs.append((wd_h, s_wdq, r, 0, 1408))
            wspecs.append((wd_h, s_wdq, r, 1408, 1408))
        NWQ = len(wspecs)

        def mk_wprep(j):
            def prep():
                src_h, dst_h, r, c0, width = wspecs[j]
                b = j % 2
                wts = [t for t in (wqs["ld"].get(j - 2),) if t]
                wqs["ld"][j] = plan.ld(lambda e: e.dma_start(
                    out=wld[b][:, 0:width],
                    in_=src_h[r * 128:(r + 1) * 128, c0:c0 + width]), wts)
                wtiles[j]["gates"] = [wqs["ld"][j]]
                wtiles[j]["out_gate"] = wqs["st"][b]
            return prep

        def mk_wdone(j):
            def done(tok):
                src_h, dst_h, r, c0, width = wspecs[j]
                b = j % 2
                wqs["st"][b] = plan.st(lambda e: e.dma_start(
                    out=dst_h[r * 128:(r + 1) * 128, c0:c0 + width],
                    in_=wqb[b][:, 0:width]), [tok])
                wqs["ld"][j] = tok
            return done

        wtiles = [dict(x=wld[j % 2][:, 0:wspecs[j][4]],
                       out=wqb[j % 2][:, 0:wspecs[j][4]], w=wspecs[j][4],
                       gates=[], prep=mk_wprep(j), on_done=mk_wdone(j),
                       out_gate=None) for j in range(NWQ)]
        qs2.feed(wtiles)

        psS = [ps_big[:, 0:1024], ps_big[:, 1024:2048]]
        psAccA = ps_big[:, 2048:2560]
        psAccB = ps_big[:, 2560:3072]
        psRbA = ps_sm[:, 0:512]
        psRbB = ps_sm[:, 512:1024]
        psM = ps_big[:, 0:512]
        psQ2 = ps_big[:, 512:1024]
        psK2 = ps_sm[:, 0:512]
        vn4 = vn.rearrange("p (hh kc o) -> p hh kc o", kc=NTC, o=65)

        exp_tok = {}
        mm2_tok = {}
        acc_free = None
        rb_free = None
        sh_prev = None
        shev_prev = None
        cat_st = [None, None]
        prod_pe_gate = None
        m8_pe_gate = None

        for p in range(NP):
            hA, hB = 2 * p, 2 * p + 1
            wz = [t for t in (sh_prev, exp_tok.get(0), exp_tok.get(1)) if t]
            tz = plan.dv(lambda e: e.memset(kmax2[:, 0:1], 0.0), wz)
            tred = None
            for tt in range(NQT):
                sl = p * T + tt * 512
                w = ([tred] if tred else []) + ([prod_pe_gate] if prod_pe_gate else [])
                tq = plan.dv(lambda e, sl=sl: e.tensor_tensor(
                    prod, kT[:, sl:sl + 512], kT[:, sl:sl + 512], ALU.mult), w)
                wk2 = [tq, tz] + ([shev_prev] if (tt == 0 and shev_prev) else [])
                tm = plan.pe(lambda e: e.matmul(psK2[0:33, :], ones33[:, 0:33], prod,
                                                start=True, stop=True), wk2)
                prod_pe_gate = tm
                tred = plan.dv(lambda e: e.reduce_max(kmax2[0:33, 1:2], psK2[0:33, :],
                                                      AX.X), [tm])
                plan.dv_drain()
                tred = plan.dv(lambda e: e.tensor_max(kmax2[0:33, 0:1], kmax2[0:33, 0:1],
                                                      kmax2[0:33, 1:2]))
                plan.dv_drain()
                qs2.fill(2)
            tkm = plan.ac(lambda e: e.activation(kmax2[0:33, 2:3], kmax2[0:33, 0:1],
                                                 ACTF.Sqrt), [tred])
            t_m2_prev = None
            t_sq_prev = None
            for tt in range(NQT):
                sl = p * T + tt * 512
                slm = tt * 512
                w1 = [tred, prod_pe_gate] + ([m8_pe_gate] if (tt == 0 and m8_pe_gate) else [])
                tq1 = plan.dv(lambda e, sl=sl: e.tensor_tensor(
                    prod, qT[:, sl:sl + 512], kT[:, sl:sl + 512], ALU.mult), w1)
                tm1 = plan.pe(lambda e: e.matmul(psM[0:33, :], ones33[:, 0:33], prod,
                                                 start=True, stop=True),
                              [tq1] + ([t_m2_prev] if t_m2_prev else []))
                tq2 = plan.dv(lambda e, sl=sl: e.tensor_tensor(
                    prod, qT[:, sl:sl + 512], qT[:, sl:sl + 512], ALU.mult), [tm1])
                tm2 = plan.pe(lambda e: e.matmul(psQ2[0:33, :], ones33[:, 0:33], prod,
                                                 start=True, stop=True),
                              [tq2] + ([t_sq_prev] if t_sq_prev else []))
                prod_pe_gate = tm2
                tsq = plan.ac(lambda e: e.activation(qn_sb[0:33, :], psQ2[0:33, :],
                                                     ACTF.Sqrt), [tm2, tkm])
                t_sq_prev = tsq
                plan.dv(lambda e: e.tensor_scalar(
                    mt_sb[0:33, :], qn_sb[0:33, :], kmax2[0:33, 2:3], 0.125,
                    ALU.mult, ALU.mult), [tsq])
                plan.dv(lambda e: e.tensor_scalar_sub(mt_sb[0:33, :], mt_sb[0:33, :], 70.0))
                t_m2_prev = plan.dv(lambda e: e.tensor_scalar_mul(
                    m2_sb[0:33, :], psM[0:33, :], 0.125))
                plan.dv_drain()
                plan.dv(lambda e: e.tensor_max(mt_sb[0:33, :], mt_sb[0:33, :],
                                               m2_sb[0:33, :]))
                t_m8w = plan.dv(lambda e, slm=slm: e.tensor_scalar_mul(
                    m8[0:33, slm:slm + 512], mt_sb[0:33, :], -8.0))
                qs2.fill(2)
            plan.dv_drain()
            mblock_done = [t_m8w, t_sq_prev, t_m2_prev]

            for qt in range(NQT):
                nkc = 4 * qt + 4
                gates = {}

                def emit_mm2(j, nkc=nkc, hA=hA, hB=hB, gates=gates):
                    nonlocal acc_free
                    sbj = j % 2
                    st0, sp = (j == 0), (j == nkc - 1)
                    wj = [gates[j]]
                    if j == 0 and acc_free:
                        wj = wj + [acc_free]
                    plan.pe(lambda e, hA=hA, j=j, sbj=sbj, st0=st0, sp=sp:
                            e.matmul(psAccA[0:65, :], vn4[:, hA, j, :],
                                     pt[sbj][:, 0:512], start=st0, stop=sp), wj)
                    tj = plan.pe(lambda e, hB=hB, j=j, sbj=sbj, st0=st0, sp=sp:
                                 e.matmul(psAccB[0:65, :], vn4[:, hB, j, :],
                                          pt[sbj][:, 512:1024], start=st0, stop=sp))
                    mm2_tok[sbj] = tj
                    return tj

                for kc in range(nkc):
                    sb = kc % 2
                    w1 = [exp_tok.get(sb)] if exp_tok.get(sb) else []
                    if qt == 0 and kc <= 1:
                        w1 = w1 + mblock_done
                    plan.pe(lambda e, p=p, qt=qt, kc=kc, sb=sb:
                            e.matmul(psS[sb][:, 0:512],
                                     kT[0:64, p * T + kc * 128: p * T + kc * 128 + 128],
                                     qT[0:64, p * T + qt * 512: p * T + qt * 512 + 512],
                                     start=True, stop=False), w1)
                    plan.pe(lambda e, qt=qt, sb=sb:
                            e.matmul(psS[sb][:, 0:512], ones_bf[0:1, 0:128],
                                     m8[0:1, qt * 512: qt * 512 + 512],
                                     start=False, stop=True))
                    plan.pe(lambda e, p=p, qt=qt, kc=kc, sb=sb:
                            e.matmul(psS[sb][:, 512:1024],
                                     kT[64:128, p * T + kc * 128: p * T + kc * 128 + 128],
                                     qT[64:128, p * T + qt * 512: p * T + qt * 512 + 512],
                                     start=True, stop=False))
                    t_mm1 = plan.pe(lambda e, qt=qt, sb=sb:
                                    e.matmul(psS[sb][:, 512:1024], ones_bf[32:33, 0:128],
                                             m8[32:33, qt * 512: qt * 512 + 512],
                                             start=False, stop=True))
                    m8_pe_gate = t_mm1
                    if kc >= 1:
                        emit_mm2(kc - 1)
                    we = [t_mm1] + ([mm2_tok.get(sb)] if mm2_tok.get(sb) else [])
                    texp = plan.ac(lambda e, sb=sb: e.activation(
                        pt[sb], psS[sb], ACTF.Exp, scale=0.125), we)
                    exp_tok[sb] = texp
                    gate = texp
                    if kc >= 4 * qt:
                        off = 384 - (kc * 128 - qt * 512)
                        cm = cmask[:, off:off + 512].unsqueeze(1).broadcast_to([128, 2, 512])
                        pt3 = pt[sb].rearrange("p (t n) -> p t n", t=2)
                        gate = plan.dv(lambda e, sb=sb, cm=cm, pt3=pt3:
                                       e.tensor_tensor(pt3, pt3, cm, ALU.mult), [texp])
                    gates[kc] = gate
                    qs2.fill(2)
                t_last_mm2 = emit_mm2(nkc - 1)
                wrb = [rb_free] if rb_free else []
                plan.dv(lambda e: e.tensor_copy(rsum[64:65, 0:512], psAccA[64:65, :]),
                        [t_last_mm2] + wrb)
                plan.dv(lambda e: e.tensor_copy(rsum[64:65, 512:1024], psAccB[64:65, :]))
                plan.dv_drain()
                trc = plan.dv(lambda e: e.reciprocal(rsum[64:65, :], rsum[64:65, :]))
                qs2.fill(2)
                wsh = [shev_prev] if shev_prev else []
                plan.pe(lambda e: e.matmul(psRbA[0:64, :], ones_all[64:65, 0:64],
                                           rsum[64:65, 0:512], start=True, stop=True),
                        [trc] + wsh)
                trb = plan.pe(lambda e: e.matmul(psRbB[0:64, :], ones_all[64:65, 0:64],
                                                 rsum[64:65, 512:1024], start=True, stop=True))
                plan.dv(lambda e: e.tensor_copy(rb[0:64, 0:512], psRbA[0:64, :]), [trb])
                t_rbev = plan.dv(lambda e: e.tensor_copy(rb[0:64, 512:1024], psRbB[0:64, :]))
                rb_free = t_rbev
                cb = (p * NQT + qt) % 2
                wcat = [cat_st[cb]] if cat_st[cb] else []
                plan.dv(lambda e, cb=cb: e.tensor_tensor(
                    catb[cb][0:64, :], psAccA[0:64, :], rb[0:64, 0:512], ALU.mult), wcat)
                wstg = [sh_prev] if sh_prev else []
                t_nB = plan.dv(lambda e: e.tensor_tensor(
                    stgB[0:64, :], psAccB[0:64, :], rb[0:64, 512:1024], ALU.mult), wstg)
                acc_free = t_nB
                qs2.fill(2)
                t_sh = plan.pe(lambda e: e.matmul(ps_sm[64:128, 0:512], eye64[0:64, :],
                                                  stgB[0:64, :], start=True, stop=True),
                               [t_nB, t_rbev])
                sh_prev = t_sh
                t_shev = plan.dv(lambda e, cb=cb: e.tensor_copy(
                    catb[cb][64:128, :], ps_sm[64:128, 0:512]), [t_sh])
                shev_prev = t_shev
                cat_st[cb] = plan.st(lambda e, p=p, qt=qt, cb=cb: e.dma_start(
                    out=s_catT[p * 128:(p + 1) * 128, qt * 512:qt * 512 + 512],
                    in_=catb[cb]), [t_shev])
                qs2.fill(6)
        while qs2.pending():
            qs2.emit_next()
        plan.barrier()
        cv.pop()
        plan.barrier()
        cv.pop()
        plan.barrier()

        cv.push()
        Wgq = cv.take(NCC * I, BF16, "Wgq")
        Wuq = cv.take(NCC * I, BF16, "Wuq")
        cv.push()
        t_wq_done = plan.barrier_token("ST")
        t_wg_tr = None
        for ccx in range(NCC):
            plan.ld(lambda e, ccx=ccx: e.dma_start_transpose(
                Wgq[:, ccx * I:(ccx + 1) * I], s_wgq[:, ccx * 128:(ccx + 1) * 128]),
                [t_wq_done])
            t_wg_tr = plan.ld(lambda e, ccx=ccx: e.dma_start_transpose(
                Wuq[:, ccx * I:(ccx + 1) * I], s_wuq[:, ccx * 128:(ccx + 1) * 128]))

        qtmp5 = carve_qtmp()
        xq_sb = [cv.take(NCC * 128, BF16, "xq0"), cv.take(NCC * 128, BF16, "xq1")]
        sg_sb = cv.take(I, F32, "sg")
        h_sb = cv.take(I, F32, "h")
        hq_sb = [cv.take(I, BF16, "hq0"), cv.take(I, BF16, "hq1")]
        qs3 = QSeq(plan, qtmp5)
        hstate = {"st": [None, None]}
        xq_used = [None, None]

        def mk_hdone(tcf, half):
            def done(tok):
                if half == 1:
                    b = tcf % 2
                    hstate["st"][b] = plan.st(lambda e: e.dma_start(
                        out=s_hq[tcf * 128:(tcf + 1) * 128, :], in_=hq_sb[b]), [tok])
            return done

        psH = [ps_big[:, 0:1408], ps_big[:, 1536:2944]]
        segs3 = [(0, 512), (512, 512), (1024, 384)]
        gu_free = [None, None]
        pend_tiles = []
        for tcf in range(NTC):
            b = tcf % 2
            tldx = plan.ld(lambda e, tcf=tcf, b=b: e.dma_start(
                out=xq_sb[b].rearrange("p (cc n) -> p cc n", n=128),
                in_=s_xqT.rearrange("(cc p) m -> p cc m", p=128)[:, :, tcf * 128:(tcf + 1) * 128]),
                [xq_used[b]] if xq_used[b] else [])
            xqv = xq_sb[b].rearrange("p (cc n) -> p cc n", n=128)
            tsilu = [None, None]
            tg = [None, None]
            for half in (0, 1):
                hofs = half * 1408
                for ccx in range(NCC):
                    for si, (s0, sw) in enumerate(segs3):
                        wts = []
                        if ccx == 0 and si == 0:
                            wts = [tldx, t_wg_tr] + ([gu_free[half]] if gu_free[half] else [])
                        tg[half] = plan.pe(
                            lambda e, ccx=ccx, s0=s0, sw=sw, xqv=xqv, half=half, hofs=hofs:
                            e.matmul(psH[half][:, s0:s0 + sw], xqv[:, ccx, :],
                                     Wgq[:, ccx * I + hofs + s0: ccx * I + hofs + s0 + sw],
                                     start=(ccx == 0), stop=(ccx == NCC - 1)), wts)
                tsilu[half] = plan.ac(
                    lambda e, half=half, hofs=hofs: e.activation(
                        sg_sb[:, hofs:hofs + 1408], psH[half], ACTF.Silu), [tg[half]])
            for half in (0, 1):
                hofs = half * 1408
                tu = None
                for ccx in range(NCC):
                    for si, (s0, sw) in enumerate(segs3):
                        wts = [tsilu[half]] if (ccx == 0 and si == 0) else []
                        tu = plan.pe(
                            lambda e, ccx=ccx, s0=s0, sw=sw, xqv=xqv, half=half, hofs=hofs:
                            e.matmul(psH[half][:, s0:s0 + sw], xqv[:, ccx, :],
                                     Wuq[:, ccx * I + hofs + s0: ccx * I + hofs + s0 + sw],
                                     start=(ccx == 0), stop=(ccx == NCC - 1)), wts)
                if half == 1:
                    xq_used[b] = tu
                th = plan.dv(lambda e, half=half, hofs=hofs: e.tensor_tensor(
                    h_sb[:, hofs:hofs + 1408], sg_sb[:, hofs:hofs + 1408],
                    psH[half], ALU.mult), [tu])
                gu_free[half] = th
                pend_tiles.append(dict(x=h_sb[:, hofs:hofs + 1408],
                                       out=hq_sb[b][:, hofs:hofs + 1408],
                                       w=1408, gates=[th], prep=None,
                                       on_done=mk_hdone(tcf, half),
                                       out_gate=hstate["st"][b]))
            qs3.feed(pend_tiles)
            pend_tiles = []
            qs3.emit_next()
            qs3.emit_next()
        plan.barrier()
        cv.pop()
        cv.pop()
        plan.barrier()

        cv.push()
        Wdq = cv.take(NIC * 1024, BF16, "Wdq")
        _wu_dead = cv.take(NCC * I, BF16, "wud")
        wpT = cv.take(NCC * 1024, BF16, "wpT")
        hqT_sb = [cv.take(NIC * 128, BF16, "hqT0"), cv.take(NIC * 128, BF16, "hqT1")]
        cat_sb = [cv.take(NCC * 128, BF16, "catA"), cv.take(NCC * 128, BF16, "catB")]
        stgT = [cv.take(T, BF16, "stgT0"), cv.take(T, BF16, "stgT1")]
        out_t = [cv.take(C, F32, "o0"), cv.take(C, F32, "o1")]

        t_hq_all = plan.barrier_token("ST")
        stg_tok2 = [None, None]
        for icx in range(NIC):
            b = icx % 2
            tt = plan.ld(lambda e, icx=icx, b=b: e.dma_start_transpose(
                stgT[b], s_hq[:, icx * 128:(icx + 1) * 128]),
                [t_hq_all] + ([stg_tok2[b]] if stg_tok2[b] else []))
            stg_tok2[b] = plan.st(lambda e, icx=icx, b=b: e.dma_start(
                out=s_hqT[icx * 128:(icx + 1) * 128, :], in_=stgT[b]), [tt])
        t_wd_tr = None
        for icx in range(NIC):
            t_wd_tr = plan.ld(lambda e, icx=icx: e.dma_start_transpose(
                Wdq[:, icx * 1024:(icx + 1) * 1024], s_wdq[:, icx * 128:(icx + 1) * 128]),
                [t_wq_done])
        t_wp = plan.gp(lambda e: e.dma_start(
            out=wpT.rearrange("p (cc n) -> p cc n", n=1024),
            in_=wpT_h.rearrange("(cc p) n -> p cc n", p=128)))
        t_hqT_done = plan.barrier_token("ST")

        psD = [ps_big[:, 0:1024], ps_big[:, 1024:2048]]
        dn_used = [None, None]
        dn_psfree = [None, None]
        out_st = [None, None]
        for tcd in range(NTC):
            b = tcd % 2
            wl = [t_hqT_done, t_wd_tr, t_wp] if tcd < 2 else []
            tld1 = plan.ld(lambda e, tcd=tcd, b=b: e.dma_start(
                out=hqT_sb[b].rearrange("p (ic n) -> p ic n", n=128),
                in_=s_hqT.rearrange("(ic p) m -> p ic m", p=128)[:, :, tcd * 128:(tcd + 1) * 128]),
                wl + [t for t in (dn_used[b], dn_psfree[b]) if t])
            tld2 = plan.ld(lambda e, tcd=tcd, b=b: e.dma_start(
                out=cat_sb[b].rearrange("p (cc n) -> p cc n", n=128),
                in_=s_catT.rearrange("(cc p) m -> p cc m", p=128)[:, :, tcd * 128:(tcd + 1) * 128]))
            tld3 = plan.ld(lambda e, tcd=tcd, b=b: e.dma_start(
                out=x_t[b], in_=x_h[tcd * 128:(tcd + 1) * 128, :]))
            hqv = hqT_sb[b].rearrange("p (ic n) -> p ic n", n=128)
            catv = cat_sb[b].rearrange("p (cc n) -> p cc n", n=128)
            tpe = None
            first = True
            wts0 = [tld1, tld2] + ([dn_psfree[b]] if dn_psfree[b] else [])
            for icx in range(NIC):
                plan.pe(lambda e, icx=icx, b=b, hqv=hqv, first=first:
                        e.matmul(psD[b][:, 0:512], hqv[:, icx, :],
                                 Wdq[:, icx * 1024: icx * 1024 + 512],
                                 start=first, stop=False), wts0 if first else [])
                tpe = plan.pe(lambda e, icx=icx, b=b, hqv=hqv, first=first:
                              e.matmul(psD[b][:, 512:1024], hqv[:, icx, :],
                                       Wdq[:, icx * 1024 + 512: icx * 1024 + 1024],
                                       start=first, stop=False))
                first = False
            for ccx in range(NCC):
                plan.pe(lambda e, ccx=ccx, b=b, catv=catv:
                        e.matmul(psD[b][:, 0:512], catv[:, ccx, :],
                                 wpT[:, ccx * 1024: ccx * 1024 + 512],
                                 start=False, stop=False))
                tpe = plan.pe(lambda e, ccx=ccx, b=b, catv=catv:
                              e.matmul(psD[b][:, 512:1024], catv[:, ccx, :],
                                       wpT[:, ccx * 1024 + 512: ccx * 1024 + 1024],
                                       start=False, stop=False))
            plan.pe(lambda e, b=b: e.matmul(psD[b][:, 0:512], ones_all[0:1, 0:128],
                                            bproj[0:1, 0:512], start=False, stop=True))
            tpe = plan.pe(lambda e, b=b: e.matmul(psD[b][:, 512:1024], ones_all[0:1, 0:128],
                                                  bproj[0:1, 512:1024], start=False, stop=True))
            dn_used[b] = tpe
            tep = plan.dv(lambda e, b=b: e.tensor_tensor(out_t[b], psD[b], x_t[b], ALU.add),
                          [tpe, tld3] + ([out_st[b]] if out_st[b] else []))
            dn_psfree[b] = tep
            out_st[b] = plan.st(lambda e, tcd=tcd, b=b: e.dma_start(
                out=out_h[tcd * 128:(tcd + 1) * 128, :], in_=out_t[b]), [tep])
        cv.pop()

        plan.emit("scalar", None, [("ST", plan.cnt["ST"])])
        plan.emit("sync", None, [("ST", plan.cnt["ST"])])

        def run_plan(eng_obj, name):
            for fn, waits, inc in plan.ops[name]:
                for s, v in waits:
                    eng_obj.wait_ge(sems[s], v)
                if fn is None:
                    continue
                inst = fn(eng_obj)
                if inc is not None:
                    s, d = inc
                    inst.then_inc(sems[s], d)

        @block.sync
        def _(sync):
            run_plan(sync, "sync")

        @block.scalar
        def _(scalar):
            run_plan(scalar, "scalar")

        @block.vector
        def _(vector):
            run_plan(vector, "vector")

        @block.tensor
        def _(tensor):
            run_plan(tensor, "tensor")

        @block.gpsimd
        def _(gpsimd):
            run_plan(gpsimd, "gpsimd")

    return nc


_NC_CACHE = None


def kernel(L0_attn_x, L0_attn_w_qkv, L0_attn_w_proj, L0_attn_b_proj,
           L0_ffn_w_gate, L0_ffn_w_up, L0_ffn_w_down):
    global _NC_CACHE
    x = np.ascontiguousarray(np.asarray(L0_attn_x, dtype=np.float32))
    wqkv = np.asarray(L0_attn_w_qkv, dtype=np.float32)
    wproj = np.asarray(L0_attn_w_proj, dtype=np.float32)
    bproj = np.asarray(L0_attn_b_proj, dtype=np.float32)
    wg = np.ascontiguousarray(np.asarray(L0_ffn_w_gate, dtype=np.float32))
    wu = np.ascontiguousarray(np.asarray(L0_ffn_w_up, dtype=np.float32))
    wd = np.ascontiguousarray(np.asarray(L0_ffn_w_down, dtype=np.float32))

    wT = wqkv.transpose(2, 0, 1)
    blocks = []
    for p in range(NP):
        blocks += [wT[:, 2 * p, 0:64], wT[:, 2 * p + 1, 0:64],
                   wT[:, 2 * p, 64:128], wT[:, 2 * p + 1, 64:128]]
    wqkT = np.ascontiguousarray(np.concatenate(blocks, axis=1))
    wvT = np.ascontiguousarray(wT[:, :, 128:192].reshape(C, C))
    wpT = np.ascontiguousarray(wproj.T)
    bproj2 = np.ascontiguousarray(bproj.reshape(1, C))

    jj = np.arange(896)[None, :]
    kk = np.arange(128)[:, None]
    cmask = ((jj - kk) >= 384).astype(ml_dtypes.bfloat16)
    ones_all = np.ones((128, 128), dtype=np.float32)
    ones_bf = np.ones((128, 128), dtype=ml_dtypes.bfloat16)
    ones33 = np.zeros((128, 33), dtype=ml_dtypes.bfloat16)
    ones33[0:64, 0] = 1
    ones33[64:128, 32] = 1
    eye64 = np.eye(64, dtype=ml_dtypes.bfloat16)

    if _NC_CACHE is None:
        _NC_CACHE = build_program()
    nc = _NC_CACHE

    in_maps = []
    for b in range(8):
        in_maps.append({
            "x": np.ascontiguousarray(x[b]),
            "wqkT": wqkT, "wvT": wvT, "wpT": wpT, "bproj": bproj2,
            "w_gate": wg, "w_up": wu, "w_down": wd,
            "cmask": cmask, "ones_all": ones_all, "ones_bf": ones_bf,
            "ones33": ones33, "eye64": eye64,
        })
    res = run_bass_kernel_spmd(nc, in_maps, core_ids=list(range(8)))
    out = np.stack([np.asarray(r["out"]) for r in res.results], axis=0)
    return out.astype(np.float32)


# version 09
# speedup vs baseline: 1.2287x; 1.0163x over previous
import numpy as np
import ml_dtypes

import concourse.bass as bass
import concourse.mybir as mybir
from concourse.bass_utils import run_bass_kernel_spmd

F32 = mybir.dt.float32
BF16 = mybir.dt.bfloat16
U8 = mybir.dt.uint8
ALU = mybir.AluOpType
ACTF = mybir.ActivationFunctionType
AX = mybir.AxisListType

T, C, H, HS, I = 2048, 1024, 16, 64, 2816
NTC, NCC, NIC, NP, NQT = T // 128, C // 128, I // 128, H // 2, T // 512
ARENA_KB = 206
QW = 1408


class Carver:
    def __init__(self, arena_ap):
        self.arena = arena_ap
        self.off = 0
        self.marks = []

    def take(self, n_elems, dtype, name=""):
        esz = 2 if dtype == BF16 else 4
        nbytes = n_elems * esz
        off = (self.off + 63) & ~63
        self.off = off + nbytes
        assert self.off <= ARENA_KB * 1024, f"arena overflow at {name}: {self.off}"
        return self.arena[:, off:off + nbytes].bitcast(dtype)

    def push(self):
        self.marks.append(self.off)

    def pop(self):
        self.off = self.marks.pop()


class Plan:
    ENGINES = ("sync", "scalar", "vector", "tensor", "gpsimd")

    def __init__(self):
        self.ops = {e: [] for e in self.ENGINES}
        self.cnt = {"LD": 0, "ST": 0, "GP": 0, "PE": 0, "DV": 0, "AC": 0, "WS": 0}

    def emit(self, eng, fn, waits=(), inc=None):
        w = [(s, v) for (s, v) in waits if v and v > 0]
        tok = None
        if inc is not None:
            s, d = inc
            self.cnt[s] += d
            tok = (s, self.cnt[s])
        self.ops[eng].append((fn, w, inc))
        return tok

    def dv(self, fn, waits=()):
        return self.emit("vector", fn, waits, ("DV", 1))

    def ac(self, fn, waits=()):
        return self.emit("scalar", fn, waits, ("AC", 1))

    def pe(self, fn, waits=()):
        return self.emit("tensor", fn, waits, ("PE", 1))

    def ld(self, fn, waits=()):
        return self.emit("sync", fn, waits, ("LD", 16))

    def st(self, fn, waits=()):
        return self.emit("scalar", fn, waits, ("ST", 16))

    def gp(self, fn, waits=()):
        return self.emit("gpsimd", fn, waits, ("GP", 16))

    def dv_drain(self):
        self.emit("vector", lambda v: v.drain())

    def barrier_token(self, sem):
        return (sem, self.cnt[sem])

    def barrier(self):
        toks = [(s, self.cnt[s]) for s in ("LD", "ST", "GP", "PE", "DV", "AC", "WS")]
        for e in self.ENGINES:
            self.ops[e].append((None, [(s, v) for (s, v) in toks if v > 0], None))


class QSeq:

    def __init__(self, plan, tmp):
        self.plan = plan
        self.tmp = tmp
        self.tiles = []
        self.i = 0
        self.heads_done = -1
        self._gen = self._generator()
        self.exhausted = False

    def feed(self, tiles):
        self.tiles.extend(tiles)
        self.exhausted = False

    def _head1(self, j):
        t = self.tiles[j]
        b = j % 2
        nb = t["w"] // 16
        rmax = self.tmp["rmax"][b][:, 0:nb]
        x3 = t["x"].rearrange("p (nb w) -> p nb w", w=16)
        return self.plan.dv(lambda v, rmax=rmax, x3=x3: v.tensor_reduce(
            rmax, x3, AX.X, ALU.max, apply_absolute_value=True), t["gates"])

    def _head2(self, j):
        b = j % 2
        nb = self.tiles[j]["w"] // 16
        rmax = self.tmp["rmax"][b][:, 0:nb]
        rr12 = self.tmp["rr12"][b][:, 0:nb]
        self.plan.dv(lambda v, rmax=rmax: v.tensor_scalar_max(rmax, rmax, 6e-12))
        return self.plan.dv(lambda v, rr12=rr12, rmax=rmax: v.reciprocal(rr12, rmax))

    def _head3(self, j):
        b = j % 2
        nb = self.tiles[j]["w"] // 16
        rmax = self.tmp["rmax"][b][:, 0:nb]
        rr12 = self.tmp["rr12"][b][:, 0:nb]
        s2 = self.tmp["s2"][b][:, 0:nb]
        self.plan.dv(lambda v, rr12=rr12: v.tensor_scalar_mul(rr12, rr12, 12.0))
        return self.plan.dv(lambda v, s2=s2, rmax=rmax: v.tensor_scalar_mul(
            s2, rmax, 1.0 / 12.0))

    def _emit_heads_guarded(self, j):
        if self.tiles[j].get("prep"):
            self.tiles[j]["prep"]()
            self.plan.dv_drain()
        self._head1(j)
        self.plan.dv_drain()
        self._head2(j)
        self.plan.dv_drain()
        self._head3(j)
        self.plan.dv_drain()
        self.heads_done = j

    def _generator(self):
        plan, tmp = self.plan, self.tmp
        while True:
            if self.i >= len(self.tiles):
                yield "idle"
                continue
            i = self.i
            if self.heads_done < i:
                self._emit_heads_guarded(i)
                yield 1
            t = self.tiles[i]
            b = i % 2
            w = t["w"]
            nb = w // 16
            rr12 = tmp["rr12"][b][:, 0:nb]
            s2 = tmp["s2"][b][:, 0:nb]
            v2 = tmp["v2"][:, 0:w]
            u = tmp["u"][:, 0:w]
            g = tmp["g"][:, 0:w]
            hh = tmp["hh"][:, 0:w]
            sgn = tmp["sgn"][:, 0:w]
            x3 = t["x"].rearrange("p (nb w) -> p nb w", w=16)
            v23 = v2.rearrange("p (nb w) -> p nb w", w=16)
            rr12_bc = rr12.unsqueeze(2).broadcast_to([128, nb, 16])
            s2_bc = s2.unsqueeze(2).broadcast_to([128, nb, 16])

            have_next = (i + 1 < len(self.tiles)) and (self.heads_done == i)
            if have_next and self.tiles[i + 1].get("prep"):
                self.tiles[i + 1]["prep"]()
            plan.dv(lambda v, v23=v23, x3=x3, rr12_bc=rr12_bc:
                    v.tensor_tensor(v23, x3, rr12_bc, ALU.mult), list(t["gates"]))
            yield 1
            plan.dv(lambda v, u=u, v2=v2: v.tensor_tensor(u, v2, v2, ALU.mult))
            yield 1
            plan.dv(lambda v, sgn=sgn, v2=v2: v.tensor_scalar(
                sgn, v2, 0.0, 2.0, ALU.is_gt, ALU.mult))
            plan.dv(lambda v, sgn=sgn: v.tensor_scalar_sub(sgn, sgn, 1.0))
            yield 1
            if have_next:
                self._head1(i + 1)
                yield 1
            plan.dv(lambda v, g=g, u=u: v.tensor_single_scalar(g, u, 0.25, ALU.is_gt))
            yield 1
            plan.dv(lambda v, g=g, u=u: v.scalar_tensor_tensor(g, u, 2.25, g, ALU.is_gt, ALU.add))
            yield 1
            plan.dv(lambda v, g=g, u=u: v.scalar_tensor_tensor(g, u, 6.25, g, ALU.is_gt, ALU.add))
            yield 1
            plan.dv(lambda v, g=g, u=u: v.scalar_tensor_tensor(g, u, 12.25, g, ALU.is_gt, ALU.add))
            yield 1
            if have_next:
                self._head2(i + 1)
                yield 1
            plan.dv(lambda v, hh=hh, u=u: v.tensor_single_scalar(hh, u, 100.0, ALU.is_gt))
            yield 1
            plan.dv(lambda v, hh=hh, u=u: v.scalar_tensor_tensor(hh, u, 25.0, hh, ALU.is_gt, ALU.add))
            yield 1
            plan.dv(lambda v, hh=hh, u=u: v.scalar_tensor_tensor(hh, u, 49.0, hh, ALU.is_gt, ALU.add))
            yield 1
            plan.dv(lambda v, hh=hh, u=u: v.scalar_tensor_tensor(hh, u, 100.0, hh, ALU.is_gt, ALU.add))
            yield 1
            if have_next:
                self._head3(i + 1)
                self.heads_done = i + 1
                yield 1
            plan.dv(lambda v, g=g, hh=hh: v.scalar_tensor_tensor(g, hh, 2.0, g, ALU.mult, ALU.add))
            yield 1
            plan.dv(lambda v, g=g, sgn=sgn: v.tensor_tensor(g, g, sgn, ALU.mult))
            yield 1
            o3 = t["out"].rearrange("p (nb w) -> p nb w", w=16)
            g3 = g.rearrange("p (nb w) -> p nb w", w=16)
            wout = [t["out_gate"]] if t.get("out_gate") else []
            tok = plan.dv(lambda v, o3=o3, g3=g3, s2_bc=s2_bc:
                          v.tensor_tensor(o3, g3, s2_bc, ALU.mult), wout)
            if t.get("on_done"):
                t["on_done"](tok)
            self.i += 1
            yield "tile"

    def fill(self, n):
        for _ in range(n):
            if self.i >= len(self.tiles):
                return
            next(self._gen)

    def emit_next(self):
        i0 = self.i
        while self.i == i0:
            next(self._gen)

    def start(self):
        pass

    def pending(self):
        return len(self.tiles) - self.i


def build_program():
    nc = bass.Bass("TRN2", target_bir_lowering=False)

    x_h = nc.dram_tensor("x", [T, C], F32, kind="ExternalInput").ap()
    wqkT_h = nc.dram_tensor("wqkT", [C, 2048], F32, kind="ExternalInput").ap()
    wvT_h = nc.dram_tensor("wvT", [C, C], F32, kind="ExternalInput").ap()
    wpT_h = nc.dram_tensor("wpT", [C, C], F32, kind="ExternalInput").ap()
    bproj_h = nc.dram_tensor("bproj", [1, C], F32, kind="ExternalInput").ap()
    mygu_h = nc.dram_tensor("my_gu", [44 * 128, 128], F32, kind="ExternalInput").ap()
    mydn_h = nc.dram_tensor("my_dn", [256, QW], F32, kind="ExternalInput").ap()
    cmask_h = nc.dram_tensor("cmask", [128, 896], BF16, kind="ExternalInput").ap()
    ones_h = nc.dram_tensor("ones_all", [128, 128], F32, kind="ExternalInput").ap()
    onesb_h = nc.dram_tensor("ones_bf", [128, 128], BF16, kind="ExternalInput").ap()
    ones33_h = nc.dram_tensor("ones33", [128, 33], BF16, kind="ExternalInput").ap()
    eye_h = nc.dram_tensor("eye64", [64, 64], BF16, kind="ExternalInput").ap()
    out_h = nc.dram_tensor("out", [T, C], F32, kind="ExternalOutput").ap()

    s_xnbf = nc.dram_tensor("s_xnbf", [T, C], BF16, kind="Internal").ap()
    s_xq = nc.dram_tensor("s_xq", [T, C], BF16, kind="Internal").ap()
    s_xqT = nc.dram_tensor("s_xqT", [C, T], BF16, kind="Internal").ap()
    s_hq = nc.dram_tensor("s_hq", [T, I], BF16, kind="Internal").ap()
    s_hqT = nc.dram_tensor("s_hqT", [I, T], BF16, kind="Internal").ap()
    s_catT = nc.dram_tensor("s_catT", [C, T], BF16, kind="Internal").ap()
    cc_gu_in = nc.dram_tensor("cc_gu_in", [44 * 128, 128], BF16, kind="Internal").ap()
    cc_dn_in = nc.dram_tensor("cc_dn_in", [256, QW], BF16, kind="Internal").ap()
    s_guq = nc.dram_tensor("s_guq", [8 * 44 * 128, 128], BF16, kind="Internal").ap()
    s_wdq = nc.dram_tensor("s_wdq", [C, I], BF16, kind="Internal").ap()

    plan = Plan()

    with (
        nc.sbuf_tensor("arena", [128, ARENA_KB * 1024], U8) as arena_t,
        nc.psum_tensor("ps_big", [128, 3072], F32) as ps_big,
        nc.psum_tensor("ps_sm", [128, 1024], F32) as ps_sm,
        nc.semaphore("LD") as sem_LD,
        nc.semaphore("ST") as sem_ST,
        nc.semaphore("GP") as sem_GP,
        nc.semaphore("PE") as sem_PE,
        nc.semaphore("DV") as sem_DV,
        nc.semaphore("AC") as sem_AC,
        nc.semaphore("WS") as sem_WS,
        nc.Block() as block,
    ):
        sems = {"LD": sem_LD, "ST": sem_ST, "GP": sem_GP,
                "PE": sem_PE, "DV": sem_DV, "AC": sem_AC, "WS": sem_WS}
        cv = Carver(arena_t[:])

        cmask = cv.take(896, BF16, "cmask")
        ones_all = cv.take(128, F32, "ones")
        ones_bf = cv.take(128, BF16, "onesb")
        ones33 = cv.take(33, BF16, "ones33")
        eye64 = cv.take(64, BF16, "eye")
        bproj = cv.take(C, F32, "bproj")
        rsum = cv.take(1024, F32, "rsum")
        rb = cv.take(1024, F32, "rb")
        x_t = [cv.take(C, F32, "x0"), cv.take(C, F32, "x1")]
        catb = [cv.take(512, BF16, "cat0"), cv.take(512, BF16, "cat1")]
        stgB = cv.take(512, BF16, "stgB")

        def carve_qtmp():
            return {
                "rmax": [cv.take(QW // 16, F32), cv.take(QW // 16, F32)],
                "rr12": [cv.take(QW // 16, F32), cv.take(QW // 16, F32)],
                "s2": [cv.take(QW // 16, F32), cv.take(QW // 16, F32)],
                "v2": cv.take(QW, F32), "u": cv.take(QW, F32),
                "g": cv.take(QW, F32), "hh": cv.take(QW, F32),
                "sgn": cv.take(QW, F32),
            }

        plan.ld(lambda e: e.dma_start(out=cmask, in_=cmask_h))
        plan.ld(lambda e: e.dma_start(out=ones_all, in_=ones_h))
        plan.ld(lambda e: e.dma_start(out=ones_bf, in_=onesb_h))
        plan.ld(lambda e: e.dma_start(out=ones33, in_=ones33_h))
        plan.ld(lambda e: e.dma_start(out=eye64[0:64, :], in_=eye_h))
        plan.ld(lambda e: e.dma_start(out=bproj[0:1, :], in_=bproj_h))
        t_consts = plan.barrier_token("LD")

        cv.push()
        qtmp = carve_qtmp()
        stats = cv.take(4 * NTC, F32, "stats")
        msum_all = stats[:, 0:NTC]
        mean_all = stats[:, 2 * NTC:3 * NTC]
        rstd_all = stats[:, 3 * NTC:4 * NTC]
        sumsq_all = stats[:, NTC:2 * NTC]
        sqs = qtmp["v2"][:, 0:C]

        tok_xread = [None, None]
        tok_xact = [None, None]
        tok_sq = None
        for i in range(NTC):
            b = i % 2
            w = [t for t in (tok_xread[b], tok_xact[b]) if t]
            tld = plan.ld(lambda e, i=i, b=b: e.dma_start(
                out=x_t[b], in_=x_h[i * 128:(i + 1) * 128, :]), w + [t_consts])
            tok_xact[b] = plan.ac(lambda e, b=b: e.activation(sqs, x_t[b], ACTF.Square),
                                  [tld] + ([tok_sq] if tok_sq else []))
            tok_xread[b] = plan.dv(lambda e, i=i, b=b: e.reduce_sum(
                msum_all[:, i:i + 1], x_t[b], AX.X), [tld])
            tok_sq = plan.dv(lambda e, i=i: e.reduce_sum(
                sumsq_all[:, i:i + 1], sqs, AX.X), [tok_xact[b]])
        plan.dv_drain()
        plan.dv(lambda e: e.tensor_scalar_mul(mean_all, msum_all, 1.0 / C))
        plan.dv(lambda e: e.tensor_scalar(rstd_all, sumsq_all, 1.0 / C, 1e-5,
                                          ALU.mult, ALU.add))
        plan.dv_drain()
        plan.dv(lambda e: e.tensor_mul(msum_all, mean_all, mean_all))
        plan.dv_drain()
        tvar = plan.dv(lambda e: e.tensor_sub(rstd_all, rstd_all, msum_all))
        tsd = plan.ac(lambda e: e.activation(rstd_all, rstd_all, ACTF.Sqrt), [tvar])
        plan.dv(lambda e: e.reciprocal(rstd_all, rstd_all), [tsd])
        plan.dv_drain()

        xn_t = [cv.take(C, F32, "xn0"), cv.take(C, F32, "xn1")]
        bfb = [cv.take(C, BF16, "bf0"), cv.take(C, BF16, "bf1")]
        qxb = [cv.take(C, BF16, "qx0"), cv.take(C, BF16, "qx1")]

        qs1 = QSeq(plan, qtmp)
        st1 = {"apply": [None, None], "st_bf": [None, None], "st_qx": [None, None]}

        def mk_prep1(i):
            def prep():
                b = i % 2
                wld = [t for t in (st1["apply"][b], tok_xread[b], tok_xact[b]) if t]
                tld = plan.ld(lambda e: e.dma_start(
                    out=x_t[b], in_=x_h[i * 128:(i + 1) * 128, :]), wld)
                tap = plan.dv(lambda e: e.tensor_scalar(
                    xn_t[b], x_t[b], mean_all[:, i:i + 1], rstd_all[:, i:i + 1],
                    ALU.subtract, ALU.mult), [tld])
                st1["apply"][b] = tap
                tiles1[i]["gates"] = [tap]
                tiles1[i]["out_gate"] = st1["st_qx"][b]
                tcast = plan.ac(lambda e: e.copy(bfb[b], xn_t[b]),
                                [tap] + ([st1["st_bf"][b]] if st1["st_bf"][b] else []))
                st1["st_bf"][b] = plan.st(lambda e: e.dma_start(
                    out=s_xnbf[i * 128:(i + 1) * 128, :], in_=bfb[b]), [tcast])
            return prep

        def mk_done1(i):
            def done(tok):
                b = i % 2
                st1["st_qx"][b] = plan.st(lambda e: e.dma_start(
                    out=s_xq[i * 128:(i + 1) * 128, :], in_=qxb[b]), [tok])
            return done

        tiles1 = [dict(x=xn_t[i % 2], out=qxb[i % 2], w=C, gates=[],
                       prep=mk_prep1(i), on_done=mk_done1(i), out_gate=None)
                  for i in range(NTC)]
        qs1.feed(tiles1)
        while qs1.pending():
            qs1.emit_next()
        plan.barrier()
        cv.pop()

        cv.push()
        qT = cv.take(NP * T, BF16, "qT")
        kT = cv.take(NP * T, BF16, "kT")
        vn = cv.take(H * NTC * 65, BF16, "vn")
        cv.push()
        xnT = cv.take(NCC * T, BF16, "xnT")
        stg = [cv.take(T, BF16, "stg0"), cv.take(T, BF16, "stg1")]
        wqk = [cv.take(NCC * 256, BF16, "wqk0"), cv.take(NCC * 256, BF16, "wqk1")]
        wvT = cv.take(NCC * 1024, BF16, "wvT")

        t_st_all = plan.barrier_token("ST")
        for ccx in range(NCC):
            plan.ld(lambda e, ccx=ccx: e.dma_start_transpose(
                xnT[:, ccx * T:(ccx + 1) * T], s_xnbf[:, ccx * 128:(ccx + 1) * 128]),
                [t_st_all])
        stg_tok = [None, None]
        for ccx in range(NCC):
            b = ccx % 2
            tt = plan.ld(lambda e, ccx=ccx, b=b: e.dma_start_transpose(
                stg[b], s_xq[:, ccx * 128:(ccx + 1) * 128]),
                ([t_st_all] if ccx < 2 else []) + ([stg_tok[b]] if stg_tok[b] else []))
            stg_tok[b] = plan.st(lambda e, ccx=ccx, b=b: e.dma_start(
                out=s_xqT[ccx * 128:(ccx + 1) * 128, :], in_=stg[b]), [tt])
        vn3 = vn.rearrange("p (b o) -> p b o", o=65)
        t_vn_ones = plan.dv(lambda e: e.memset(vn3[:, :, 64:65], 1.0))
        wvT_v = wvT.rearrange("p (cc n) -> p cc n", n=1024)
        t_wv = plan.gp(lambda e: e.dma_start(
            out=wvT_v, in_=wvT_h.rearrange("(cc p) n -> p cc n", p=128)))
        wqk_tok = [None, None]
        wqk_used = [None, None]

        def load_wqk(p):
            b = p % 2
            wqk_tok[b] = plan.gp(lambda e, p=p, b=b: e.dma_start(
                out=wqk[b].rearrange("p (cc n) -> p cc n", n=256),
                in_=wqkT_h.rearrange("(cc p) n -> p cc n", p=128)[:, :, p * 256:(p + 1) * 256]),
                [wqk_used[b]] if wqk_used[b] else [])

        load_wqk(0)
        psQ = [ps_big[:, 0:512], ps_big[:, 1024:1536]]
        psK = [ps_big[:, 512:1024], ps_big[:, 1536:2048]]
        psV = [ps_big[:, 2048:2560], ps_big[:, 2560:3072],
               ps_sm[:, 0:512], ps_sm[:, 512:1024]]
        evacQ = {}
        t_xnT = plan.barrier_token("LD")
        for p in range(NP):
            b = p % 2
            if p + 1 < NP:
                load_wqk(p + 1)
            wqk_v = wqk[b].rearrange("p (cc n) -> p cc n", n=256)
            for qt in range(NQT):
                ib = (p * NQT + qt) % 2
                wts = [t_xnT, wqk_tok[b]] + ([evacQ.get(ib)] if evacQ.get(ib) else [])
                tpe = None
                for ccx in range(NCC):
                    st0, sp = (ccx == 0), (ccx == NCC - 1)
                    plan.pe(lambda e, qt=qt, ccx=ccx, ib=ib, wqk_v=wqk_v, st0=st0, sp=sp:
                            e.matmul(psQ[ib], wqk_v[:, ccx, 0:128],
                                     xnT[:, ccx * T + qt * 512: ccx * T + qt * 512 + 512],
                                     start=st0, stop=sp),
                            wts if ccx == 0 else [])
                    tpe = plan.pe(lambda e, qt=qt, ccx=ccx, ib=ib, wqk_v=wqk_v, st0=st0, sp=sp:
                                  e.matmul(psK[ib], wqk_v[:, ccx, 128:256],
                                           xnT[:, ccx * T + qt * 512: ccx * T + qt * 512 + 512],
                                           start=st0, stop=sp))
                plan.dv(lambda e, p=p, qt=qt, ib=ib: e.tensor_copy(
                    qT[:, p * T + qt * 512: p * T + qt * 512 + 512], psQ[ib]), [tpe])
                evacQ[ib] = plan.dv(lambda e, p=p, qt=qt, ib=ib: e.tensor_copy(
                    kT[:, p * T + qt * 512: p * T + qt * 512 + 512], psK[ib]))
            wqk_used[b] = evacQ[(p * NQT + NQT - 1) % 2]
        evacV = {}
        for tcv in range(NTC):
            vb0, vb1 = (2 * tcv) % 4, (2 * tcv + 1) % 4
            wts = [t_wv, t_vn_ones] + [evacV[q] for q in (vb0, vb1) if q in evacV]
            tpe = None
            for ccx in range(NCC):
                st0, sp = (ccx == 0), (ccx == NCC - 1)
                plan.pe(lambda e, tcv=tcv, ccx=ccx, vb0=vb0, st0=st0, sp=sp:
                        e.matmul(psV[vb0], xnT[:, ccx * T + tcv * 128: ccx * T + tcv * 128 + 128],
                                 wvT_v[:, ccx, 0:512], start=st0, stop=sp),
                        wts if ccx == 0 else [])
                tpe = plan.pe(lambda e, tcv=tcv, ccx=ccx, vb1=vb1, st0=st0, sp=sp:
                              e.matmul(psV[vb1], xnT[:, ccx * T + tcv * 128: ccx * T + tcv * 128 + 128],
                                       wvT_v[:, ccx, 512:1024], start=st0, stop=sp))
            dst3 = vn.rearrange("p (hh kc o) -> p hh kc o", kc=NTC, o=65)
            for half, q in ((0, vb0), (1, vb1)):
                src = psV[q].rearrange("p (h d) -> p h d", d=64)
                evacV[q] = plan.dv(
                    lambda e, src=src, dst3=dst3, half=half, tcv=tcv:
                    e.tensor_copy(dst3[:, half * 8:(half + 1) * 8, tcv, 0:64], src),
                    [tpe] if half == 0 else [])
        cv.pop()
        plan.barrier()

        cv.push()
        qtmp4 = carve_qtmp()
        pt = [cv.take(1024, BF16, "pt0"), cv.take(1024, BF16, "pt1")]
        m8 = cv.take(T, BF16, "m8")
        prod = cv.take(512, BF16, "prod")
        qn_sb = cv.take(512, F32, "qn")
        mt_sb = cv.take(512, F32, "mt")
        m2_sb = cv.take(512, F32, "m2t")
        kmax2 = cv.take(4, F32, "kmax2")
        wld = [cv.take(QW, F32, "wld0"), cv.take(QW, F32, "wld1")]
        wqb = [cv.take(QW, BF16, "wqb0"), cv.take(QW, BF16, "wqb1")]

        qs2 = QSeq(plan, qtmp4)
        wqs = {"ld": {}, "st": [None, None], "last_gu": None, "last_dn": None}

        mygu3 = mygu_h.rearrange("(t p) c -> p t c", p=128)
        ccgu3 = cc_gu_in.rearrange("(t p) c -> p t c", p=128)
        NWQ = 6

        def wio(j):
            b = j % 2
            if j < 4:
                return (mygu3[:, 11 * j:11 * (j + 1), :],
                        ccgu3[:, 11 * j:11 * (j + 1), :],
                        wld[b].rearrange("p (t c) -> p t c", c=128),
                        wqb[b].rearrange("p (t c) -> p t c", c=128))
            r = j - 4
            return (mydn_h[r * 128:(r + 1) * 128, :],
                    cc_dn_in[r * 128:(r + 1) * 128, :],
                    wld[b][:, 0:QW], wqb[b][:, 0:QW])

        def mk_wprep(j):
            src, _, ldst, _ = wio(j)
            def prep():
                wts = [t for t in (wqs["ld"].get(j - 2),) if t]
                wqs["ld"][j] = plan.ld(lambda e, src=src, ldst=ldst: e.dma_start(
                    out=ldst, in_=src), wts)
                wtiles[j]["gates"] = [wqs["ld"][j]]
                wtiles[j]["out_gate"] = wqs["st"][j % 2]
            return prep

        def mk_wdone(j):
            _, dst, _, qsrc = wio(j)
            def done(tok):
                st = plan.emit("scalar", lambda e, dst=dst, qsrc=qsrc: e.dma_start(
                    out=dst, in_=qsrc), [tok], ("WS", 16))
                wqs["st"][j % 2] = st
                wqs["ld"][j] = tok
            return done

        wtiles = [dict(x=wld[j % 2][:, 0:QW],
                       out=wqb[j % 2][:, 0:QW], w=QW,
                       gates=[], prep=mk_wprep(j), on_done=mk_wdone(j),
                       out_gate=None) for j in range(NWQ)]
        qs2.feed(wtiles)

        psS = [ps_big[:, 0:1024], ps_big[:, 1024:2048]]
        psAccA = ps_big[:, 2048:2560]
        psAccB = ps_big[:, 2560:3072]
        psRbA = ps_sm[:, 0:512]
        psRbB = ps_sm[:, 512:1024]
        psM = ps_big[:, 0:512]
        psQ2 = ps_big[:, 512:1024]
        psK2 = ps_sm[:, 0:512]
        vn4 = vn.rearrange("p (hh kc o) -> p hh kc o", kc=NTC, o=65)

        exp_tok = {}
        mm2_tok = {}
        acc_free = None
        rb_free = None
        sh_prev = None
        shev_prev = None
        cat_st = [None, None]
        prod_pe_gate = None
        m8_pe_gate = None

        for p in range(NP):
            hA, hB = 2 * p, 2 * p + 1
            wz = [t for t in (sh_prev, exp_tok.get(0), exp_tok.get(1)) if t]
            tz = plan.dv(lambda e: e.memset(kmax2[:, 0:1], 0.0), wz)
            tred = None
            for tt in range(NQT):
                sl = p * T + tt * 512
                w = ([tred] if tred else []) + ([prod_pe_gate] if prod_pe_gate else [])
                tq = plan.dv(lambda e, sl=sl: e.tensor_tensor(
                    prod, kT[:, sl:sl + 512], kT[:, sl:sl + 512], ALU.mult), w)
                wk2 = [tq, tz] + ([shev_prev] if (tt == 0 and shev_prev) else [])
                tm = plan.pe(lambda e: e.matmul(psK2[0:33, :], ones33[:, 0:33], prod,
                                                start=True, stop=True), wk2)
                prod_pe_gate = tm
                tred = plan.dv(lambda e: e.reduce_max(kmax2[0:33, 1:2], psK2[0:33, :],
                                                      AX.X), [tm])
                plan.dv_drain()
                tred = plan.dv(lambda e: e.tensor_max(kmax2[0:33, 0:1], kmax2[0:33, 0:1],
                                                      kmax2[0:33, 1:2]))
                plan.dv_drain()
                qs2.fill(2)
            tkm = plan.ac(lambda e: e.activation(kmax2[0:33, 2:3], kmax2[0:33, 0:1],
                                                 ACTF.Sqrt), [tred])
            t_m2_prev = None
            t_sq_prev = None
            for tt in range(NQT):
                sl = p * T + tt * 512
                slm = tt * 512
                w1 = [tred, prod_pe_gate] + ([m8_pe_gate] if (tt == 0 and m8_pe_gate) else [])
                tq1 = plan.dv(lambda e, sl=sl: e.tensor_tensor(
                    prod, qT[:, sl:sl + 512], kT[:, sl:sl + 512], ALU.mult), w1)
                tm1 = plan.pe(lambda e: e.matmul(psM[0:33, :], ones33[:, 0:33], prod,
                                                 start=True, stop=True),
                              [tq1] + ([t_m2_prev] if t_m2_prev else []))
                tq2 = plan.dv(lambda e, sl=sl: e.tensor_tensor(
                    prod, qT[:, sl:sl + 512], qT[:, sl:sl + 512], ALU.mult), [tm1])
                tm2 = plan.pe(lambda e: e.matmul(psQ2[0:33, :], ones33[:, 0:33], prod,
                                                 start=True, stop=True),
                              [tq2] + ([t_sq_prev] if t_sq_prev else []))
                prod_pe_gate = tm2
                tsq = plan.ac(lambda e: e.activation(qn_sb[0:33, :], psQ2[0:33, :],
                                                     ACTF.Sqrt), [tm2, tkm])
                t_sq_prev = tsq
                plan.dv(lambda e: e.tensor_scalar(
                    mt_sb[0:33, :], qn_sb[0:33, :], kmax2[0:33, 2:3], 0.125,
                    ALU.mult, ALU.mult), [tsq])
                plan.dv(lambda e: e.tensor_scalar_sub(mt_sb[0:33, :], mt_sb[0:33, :], 70.0))
                t_m2_prev = plan.dv(lambda e: e.tensor_scalar_mul(
                    m2_sb[0:33, :], psM[0:33, :], 0.125))
                plan.dv_drain()
                plan.dv(lambda e: e.tensor_max(mt_sb[0:33, :], mt_sb[0:33, :],
                                               m2_sb[0:33, :]))
                t_m8w = plan.dv(lambda e, slm=slm: e.tensor_scalar_mul(
                    m8[0:33, slm:slm + 512], mt_sb[0:33, :], -8.0))
                qs2.fill(2)
            plan.dv_drain()
            mblock_done = [t_m8w, t_sq_prev, t_m2_prev]

            for qt in range(NQT):
                nkc = 4 * qt + 4
                gates = {}

                def emit_mm2(j, nkc=nkc, hA=hA, hB=hB, gates=gates):
                    nonlocal acc_free
                    sbj = j % 2
                    st0, sp = (j == 0), (j == nkc - 1)
                    wj = [gates[j]]
                    if j == 0 and acc_free:
                        wj = wj + [acc_free]
                    plan.pe(lambda e, hA=hA, j=j, sbj=sbj, st0=st0, sp=sp:
                            e.matmul(psAccA[0:65, :], vn4[:, hA, j, :],
                                     pt[sbj][:, 0:512], start=st0, stop=sp), wj)
                    tj = plan.pe(lambda e, hB=hB, j=j, sbj=sbj, st0=st0, sp=sp:
                                 e.matmul(psAccB[0:65, :], vn4[:, hB, j, :],
                                          pt[sbj][:, 512:1024], start=st0, stop=sp))
                    mm2_tok[sbj] = tj
                    return tj

                for kc in range(nkc):
                    sb = kc % 2
                    w1 = [exp_tok.get(sb)] if exp_tok.get(sb) else []
                    if qt == 0 and kc <= 1:
                        w1 = w1 + mblock_done
                    plan.pe(lambda e, p=p, qt=qt, kc=kc, sb=sb:
                            e.matmul(psS[sb][:, 0:512],
                                     kT[0:64, p * T + kc * 128: p * T + kc * 128 + 128],
                                     qT[0:64, p * T + qt * 512: p * T + qt * 512 + 512],
                                     start=True, stop=False), w1)
                    plan.pe(lambda e, qt=qt, sb=sb:
                            e.matmul(psS[sb][:, 0:512], ones_bf[0:1, 0:128],
                                     m8[0:1, qt * 512: qt * 512 + 512],
                                     start=False, stop=True))
                    plan.pe(lambda e, p=p, qt=qt, kc=kc, sb=sb:
                            e.matmul(psS[sb][:, 512:1024],
                                     kT[64:128, p * T + kc * 128: p * T + kc * 128 + 128],
                                     qT[64:128, p * T + qt * 512: p * T + qt * 512 + 512],
                                     start=True, stop=False))
                    t_mm1 = plan.pe(lambda e, qt=qt, sb=sb:
                                    e.matmul(psS[sb][:, 512:1024], ones_bf[32:33, 0:128],
                                             m8[32:33, qt * 512: qt * 512 + 512],
                                             start=False, stop=True))
                    m8_pe_gate = t_mm1
                    if kc >= 1:
                        emit_mm2(kc - 1)
                    we = [t_mm1] + ([mm2_tok.get(sb)] if mm2_tok.get(sb) else [])
                    texp = plan.ac(lambda e, sb=sb: e.activation(
                        pt[sb], psS[sb], ACTF.Exp, scale=0.125), we)
                    exp_tok[sb] = texp
                    gate = texp
                    if kc >= 4 * qt:
                        off = 384 - (kc * 128 - qt * 512)
                        cm = cmask[:, off:off + 512].unsqueeze(1).broadcast_to([128, 2, 512])
                        pt3 = pt[sb].rearrange("p (t n) -> p t n", t=2)
                        gate = plan.dv(lambda e, sb=sb, cm=cm, pt3=pt3:
                                       e.tensor_tensor(pt3, pt3, cm, ALU.mult), [texp])
                    gates[kc] = gate
                    qs2.fill(2)
                t_last_mm2 = emit_mm2(nkc - 1)
                wrb = [rb_free] if rb_free else []
                plan.dv(lambda e: e.tensor_copy(rsum[64:65, 0:512], psAccA[64:65, :]),
                        [t_last_mm2] + wrb)
                plan.dv(lambda e: e.tensor_copy(rsum[64:65, 512:1024], psAccB[64:65, :]))
                plan.dv_drain()
                trc = plan.dv(lambda e: e.reciprocal(rsum[64:65, :], rsum[64:65, :]))
                qs2.fill(2)
                wsh = [shev_prev] if shev_prev else []
                plan.pe(lambda e: e.matmul(psRbA[0:64, :], ones_all[64:65, 0:64],
                                           rsum[64:65, 0:512], start=True, stop=True),
                        [trc] + wsh)
                trb = plan.pe(lambda e: e.matmul(psRbB[0:64, :], ones_all[64:65, 0:64],
                                                 rsum[64:65, 512:1024], start=True, stop=True))
                plan.dv(lambda e: e.tensor_copy(rb[0:64, 0:512], psRbA[0:64, :]), [trb])
                t_rbev = plan.dv(lambda e: e.tensor_copy(rb[0:64, 512:1024], psRbB[0:64, :]))
                rb_free = t_rbev
                cb = (p * NQT + qt) % 2
                wcat = [cat_st[cb]] if cat_st[cb] else []
                plan.dv(lambda e, cb=cb: e.tensor_tensor(
                    catb[cb][0:64, :], psAccA[0:64, :], rb[0:64, 0:512], ALU.mult), wcat)
                wstg = [sh_prev] if sh_prev else []
                t_nB = plan.dv(lambda e: e.tensor_tensor(
                    stgB[0:64, :], psAccB[0:64, :], rb[0:64, 512:1024], ALU.mult), wstg)
                acc_free = t_nB
                qs2.fill(2)
                t_sh = plan.pe(lambda e: e.matmul(ps_sm[64:128, 0:512], eye64[0:64, :],
                                                  stgB[0:64, :], start=True, stop=True),
                               [t_nB, t_rbev])
                sh_prev = t_sh
                t_shev = plan.dv(lambda e, cb=cb: e.tensor_copy(
                    catb[cb][64:128, :], ps_sm[64:128, 0:512]), [t_sh])
                shev_prev = t_shev
                cat_st[cb] = plan.st(lambda e, p=p, qt=qt, cb=cb: e.dma_start(
                    out=s_catT[p * 128:(p + 1) * 128, qt * 512:qt * 512 + 512],
                    in_=catb[cb]), [t_shev])
                qs2.fill(6)
        while qs2.pending():
            qs2.emit_next()
        t_ws_all = ("WS", 16 * NWQ)
        t_ag_gu = plan.emit("gpsimd", lambda e: e.collective_compute(
            "AllGather", ALU.bypass, replica_groups=[list(range(8))],
            ins=[cc_gu_in], outs=[s_guq]), [t_ws_all], ("GP", 1))
        t_ag_dn = plan.emit("gpsimd", lambda e: e.collective_compute(
            "AllGather", ALU.bypass, replica_groups=[list(range(8))],
            ins=[cc_dn_in], outs=[s_wdq]), [t_ws_all], ("GP", 1))
        plan.barrier()
        cv.pop()
        plan.barrier()
        cv.pop()
        plan.barrier()

        cv.push()
        Wgq = cv.take(NCC * I, BF16, "Wgq")
        Wuq = cv.take(NCC * I, BF16, "Wuq")
        cv.push()
        t_wg_tr = None
        for ccx in range(NCC):
            plan.ld(lambda e, ccx=ccx: e.dma_start_transpose(
                Wgq[:, ccx * I:(ccx + 1) * I],
                s_guq[ccx * 5632:ccx * 5632 + 2816, :]), [t_ag_gu])
            t_wg_tr = plan.ld(lambda e, ccx=ccx: e.dma_start_transpose(
                Wuq[:, ccx * I:(ccx + 1) * I],
                s_guq[ccx * 5632 + 2816:(ccx + 1) * 5632, :]))

        qtmp5 = carve_qtmp()
        xq_sb = [cv.take(NCC * 128, BF16, "xq0"), cv.take(NCC * 128, BF16, "xq1")]
        sg_sb = cv.take(I, F32, "sg")
        h_sb = cv.take(I, F32, "h")
        hq_sb = [cv.take(I, BF16, "hq0"), cv.take(I, BF16, "hq1")]
        qs3 = QSeq(plan, qtmp5)
        hstate = {"st": [None, None]}
        xq_used = [None, None]

        def mk_hdone(tcf, half):
            def done(tok):
                if half == 1:
                    b = tcf % 2
                    hstate["st"][b] = plan.st(lambda e: e.dma_start(
                        out=s_hq[tcf * 128:(tcf + 1) * 128, :], in_=hq_sb[b]), [tok])
            return done

        psH = [ps_big[:, 0:1408], ps_big[:, 1536:2944]]
        segs3 = [(0, 512), (512, 512), (1024, 384)]
        gu_free = [None, None]
        pend_tiles = []
        for tcf in range(NTC):
            b = tcf % 2
            tldx = plan.ld(lambda e, tcf=tcf, b=b: e.dma_start(
                out=xq_sb[b].rearrange("p (cc n) -> p cc n", n=128),
                in_=s_xqT.rearrange("(cc p) m -> p cc m", p=128)[:, :, tcf * 128:(tcf + 1) * 128]),
                [xq_used[b]] if xq_used[b] else [])
            xqv = xq_sb[b].rearrange("p (cc n) -> p cc n", n=128)
            tsilu = [None, None]
            tg = [None, None]
            for half in (0, 1):
                hofs = half * 1408
                for ccx in range(NCC):
                    for si, (s0, sw) in enumerate(segs3):
                        wts = []
                        if ccx == 0 and si == 0:
                            wts = [tldx, t_wg_tr] + ([gu_free[half]] if gu_free[half] else [])
                        tg[half] = plan.pe(
                            lambda e, ccx=ccx, s0=s0, sw=sw, xqv=xqv, half=half, hofs=hofs:
                            e.matmul(psH[half][:, s0:s0 + sw], xqv[:, ccx, :],
                                     Wgq[:, ccx * I + hofs + s0: ccx * I + hofs + s0 + sw],
                                     start=(ccx == 0), stop=(ccx == NCC - 1)), wts)
                tsilu[half] = plan.ac(
                    lambda e, half=half, hofs=hofs: e.activation(
                        sg_sb[:, hofs:hofs + 1408], psH[half], ACTF.Silu), [tg[half]])
            for half in (0, 1):
                hofs = half * 1408
                tu = None
                for ccx in range(NCC):
                    for si, (s0, sw) in enumerate(segs3):
                        wts = [tsilu[half]] if (ccx == 0 and si == 0) else []
                        tu = plan.pe(
                            lambda e, ccx=ccx, s0=s0, sw=sw, xqv=xqv, half=half, hofs=hofs:
                            e.matmul(psH[half][:, s0:s0 + sw], xqv[:, ccx, :],
                                     Wuq[:, ccx * I + hofs + s0: ccx * I + hofs + s0 + sw],
                                     start=(ccx == 0), stop=(ccx == NCC - 1)), wts)
                if half == 1:
                    xq_used[b] = tu
                th = plan.dv(lambda e, half=half, hofs=hofs: e.tensor_tensor(
                    h_sb[:, hofs:hofs + 1408], sg_sb[:, hofs:hofs + 1408],
                    psH[half], ALU.mult), [tu])
                gu_free[half] = th
                pend_tiles.append(dict(x=h_sb[:, hofs:hofs + 1408],
                                       out=hq_sb[b][:, hofs:hofs + 1408],
                                       w=1408, gates=[th], prep=None,
                                       on_done=mk_hdone(tcf, half),
                                       out_gate=hstate["st"][b]))
            qs3.feed(pend_tiles)
            pend_tiles = []
            qs3.emit_next()
            qs3.emit_next()
        plan.barrier()
        cv.pop()
        cv.pop()
        plan.barrier()

        cv.push()
        Wdq = cv.take(NIC * 1024, BF16, "Wdq")
        _wu_dead = cv.take(NCC * I, BF16, "wud")
        wpT = cv.take(NCC * 1024, BF16, "wpT")
        hqT_sb = [cv.take(NIC * 128, BF16, "hqT0"), cv.take(NIC * 128, BF16, "hqT1")]
        cat_sb = [cv.take(NCC * 128, BF16, "catA"), cv.take(NCC * 128, BF16, "catB")]
        stgT = [cv.take(T, BF16, "stgT0"), cv.take(T, BF16, "stgT1")]
        out_t = [cv.take(C, F32, "o0"), cv.take(C, F32, "o1")]

        t_hq_all = plan.barrier_token("ST")
        stg_tok2 = [None, None]
        for icx in range(NIC):
            b = icx % 2
            tt = plan.ld(lambda e, icx=icx, b=b: e.dma_start_transpose(
                stgT[b], s_hq[:, icx * 128:(icx + 1) * 128]),
                [t_hq_all] + ([stg_tok2[b]] if stg_tok2[b] else []))
            stg_tok2[b] = plan.st(lambda e, icx=icx, b=b: e.dma_start(
                out=s_hqT[icx * 128:(icx + 1) * 128, :], in_=stgT[b]), [tt])
        t_wd_tr = None
        for icx in range(NIC):
            t_wd_tr = plan.ld(lambda e, icx=icx: e.dma_start_transpose(
                Wdq[:, icx * 1024:(icx + 1) * 1024], s_wdq[:, icx * 128:(icx + 1) * 128]),
                [t_ag_dn])
        t_wp = plan.gp(lambda e: e.dma_start(
            out=wpT.rearrange("p (cc n) -> p cc n", n=1024),
            in_=wpT_h.rearrange("(cc p) n -> p cc n", p=128)))
        t_hqT_done = plan.barrier_token("ST")

        psD = [ps_big[:, 0:1024], ps_big[:, 1024:2048]]
        dn_used = [None, None]
        dn_psfree = [None, None]
        out_st = [None, None]
        for tcd in range(NTC):
            b = tcd % 2
            wl = [t_hqT_done, t_wd_tr, t_wp] if tcd < 2 else []
            tld1 = plan.ld(lambda e, tcd=tcd, b=b: e.dma_start(
                out=hqT_sb[b].rearrange("p (ic n) -> p ic n", n=128),
                in_=s_hqT.rearrange("(ic p) m -> p ic m", p=128)[:, :, tcd * 128:(tcd + 1) * 128]),
                wl + [t for t in (dn_used[b], dn_psfree[b]) if t])
            tld2 = plan.ld(lambda e, tcd=tcd, b=b: e.dma_start(
                out=cat_sb[b].rearrange("p (cc n) -> p cc n", n=128),
                in_=s_catT.rearrange("(cc p) m -> p cc m", p=128)[:, :, tcd * 128:(tcd + 1) * 128]))
            tld3 = plan.ld(lambda e, tcd=tcd, b=b: e.dma_start(
                out=x_t[b], in_=x_h[tcd * 128:(tcd + 1) * 128, :]))
            hqv = hqT_sb[b].rearrange("p (ic n) -> p ic n", n=128)
            catv = cat_sb[b].rearrange("p (cc n) -> p cc n", n=128)
            tpe = None
            first = True
            wts0 = [tld1, tld2] + ([dn_psfree[b]] if dn_psfree[b] else [])
            for icx in range(NIC):
                plan.pe(lambda e, icx=icx, b=b, hqv=hqv, first=first:
                        e.matmul(psD[b][:, 0:512], hqv[:, icx, :],
                                 Wdq[:, icx * 1024: icx * 1024 + 512],
                                 start=first, stop=False), wts0 if first else [])
                tpe = plan.pe(lambda e, icx=icx, b=b, hqv=hqv, first=first:
                              e.matmul(psD[b][:, 512:1024], hqv[:, icx, :],
                                       Wdq[:, icx * 1024 + 512: icx * 1024 + 1024],
                                       start=first, stop=False))
                first = False
            for ccx in range(NCC):
                plan.pe(lambda e, ccx=ccx, b=b, catv=catv:
                        e.matmul(psD[b][:, 0:512], catv[:, ccx, :],
                                 wpT[:, ccx * 1024: ccx * 1024 + 512],
                                 start=False, stop=False))
                tpe = plan.pe(lambda e, ccx=ccx, b=b, catv=catv:
                              e.matmul(psD[b][:, 512:1024], catv[:, ccx, :],
                                       wpT[:, ccx * 1024 + 512: ccx * 1024 + 1024],
                                       start=False, stop=False))
            plan.pe(lambda e, b=b: e.matmul(psD[b][:, 0:512], ones_all[0:1, 0:128],
                                            bproj[0:1, 0:512], start=False, stop=True))
            tpe = plan.pe(lambda e, b=b: e.matmul(psD[b][:, 512:1024], ones_all[0:1, 0:128],
                                                  bproj[0:1, 512:1024], start=False, stop=True))
            dn_used[b] = tpe
            tep = plan.dv(lambda e, b=b: e.tensor_tensor(out_t[b], psD[b], x_t[b], ALU.add),
                          [tpe, tld3] + ([out_st[b]] if out_st[b] else []))
            dn_psfree[b] = tep
            out_st[b] = plan.st(lambda e, tcd=tcd, b=b: e.dma_start(
                out=out_h[tcd * 128:(tcd + 1) * 128, :], in_=out_t[b]), [tep])
        cv.pop()

        plan.emit("scalar", None, [("ST", plan.cnt["ST"])])
        plan.emit("sync", None, [("ST", plan.cnt["ST"])])

        def run_plan(eng_obj, name):
            for fn, waits, inc in plan.ops[name]:
                for s, v in waits:
                    eng_obj.wait_ge(sems[s], v)
                if fn is None:
                    continue
                inst = fn(eng_obj)
                if inc is not None:
                    s, d = inc
                    inst.then_inc(sems[s], d)

        @block.sync
        def _(sync):
            run_plan(sync, "sync")

        @block.scalar
        def _(scalar):
            run_plan(scalar, "scalar")

        @block.vector
        def _(vector):
            run_plan(vector, "vector")

        @block.tensor
        def _(tensor):
            run_plan(tensor, "tensor")

        @block.gpsimd
        def _(gpsimd):
            run_plan(gpsimd, "gpsimd")

    return nc


_NC_CACHE = None


def kernel(L0_attn_x, L0_attn_w_qkv, L0_attn_w_proj, L0_attn_b_proj,
           L0_ffn_w_gate, L0_ffn_w_up, L0_ffn_w_down):
    global _NC_CACHE
    x = np.ascontiguousarray(np.asarray(L0_attn_x, dtype=np.float32))
    wqkv = np.asarray(L0_attn_w_qkv, dtype=np.float32)
    wproj = np.asarray(L0_attn_w_proj, dtype=np.float32)
    bproj = np.asarray(L0_attn_b_proj, dtype=np.float32)
    wg = np.ascontiguousarray(np.asarray(L0_ffn_w_gate, dtype=np.float32))
    wu = np.ascontiguousarray(np.asarray(L0_ffn_w_up, dtype=np.float32))
    wd = np.ascontiguousarray(np.asarray(L0_ffn_w_down, dtype=np.float32))

    wT = wqkv.transpose(2, 0, 1)
    blocks = []
    for p in range(NP):
        blocks += [wT[:, 2 * p, 0:64], wT[:, 2 * p + 1, 0:64],
                   wT[:, 2 * p, 64:128], wT[:, 2 * p + 1, 64:128]]
    wqkT = np.ascontiguousarray(np.concatenate(blocks, axis=1))
    wvT = np.ascontiguousarray(wT[:, :, 128:192].reshape(C, C))
    wpT = np.ascontiguousarray(wproj.T)
    bproj2 = np.ascontiguousarray(bproj.reshape(1, C))

    jj = np.arange(896)[None, :]
    kk = np.arange(128)[:, None]
    cmask = ((jj - kk) >= 384).astype(ml_dtypes.bfloat16)
    ones_all = np.ones((128, 128), dtype=np.float32)
    ones_bf = np.ones((128, 128), dtype=ml_dtypes.bfloat16)
    ones33 = np.zeros((128, 33), dtype=ml_dtypes.bfloat16)
    ones33[0:64, 0] = 1
    ones33[64:128, 32] = 1
    eye64 = np.eye(64, dtype=ml_dtypes.bfloat16)

    gu_all = np.vstack([wg, wu])
    dn_all = wd.reshape(2048, QW)

    if _NC_CACHE is None:
        _NC_CACHE = build_program()
    nc = _NC_CACHE

    in_maps = []
    for b in range(8):
        in_maps.append({
            "x": np.ascontiguousarray(x[b]),
            "wqkT": wqkT, "wvT": wvT, "wpT": wpT, "bproj": bproj2,
            "my_gu": np.ascontiguousarray(gu_all[:, 128 * b:128 * (b + 1)]),
            "my_dn": np.ascontiguousarray(dn_all[256 * b:256 * (b + 1)]),
            "cmask": cmask, "ones_all": ones_all, "ones_bf": ones_bf,
            "ones33": ones33, "eye64": eye64,
        })
    res = run_bass_kernel_spmd(nc, in_maps, core_ids=list(range(8)))
    out = np.stack([np.asarray(r["out"]) for r in res.results], axis=0)
    return out.astype(np.float32)


# version 10
# speedup vs baseline: 1.2385x; 1.0079x over previous
import contextlib
import numpy as np
import ml_dtypes

import concourse.bass as bass
import concourse.mybir as mybir
from concourse.bass_utils import run_bass_kernel_spmd

F32 = mybir.dt.float32
BF16 = mybir.dt.bfloat16
U8 = mybir.dt.uint8
ALU = mybir.AluOpType
ACTF = mybir.ActivationFunctionType
AX = mybir.AxisListType

T, C, H, HS, I = 2048, 1024, 16, 64, 2816
NTC, NCC, NIC, NP, NQT = T // 128, C // 128, I // 128, H // 2, T // 512
ARENA_KB = 206
QW = 1408


class Carver:
    def __init__(self, arena_ap):
        self.arena = arena_ap
        self.off = 0
        self.marks = []

    def take(self, n_elems, dtype, name=""):
        esz = 2 if dtype == BF16 else 4
        nbytes = n_elems * esz
        off = (self.off + 63) & ~63
        self.off = off + nbytes
        assert self.off <= ARENA_KB * 1024, f"arena overflow at {name}: {self.off}"
        return self.arena[:, off:off + nbytes].bitcast(dtype)

    def push(self):
        self.marks.append(self.off)

    def pop(self):
        self.off = self.marks.pop()


class Plan:
    ENGINES = ("sync", "scalar", "vector", "tensor", "gpsimd")

    def __init__(self):
        self.ops = {e: [] for e in self.ENGINES}
        self.cnt = {}

    def emit(self, eng, fn, waits=(), inc=None):
        w = [(s, v) for (s, v) in waits if v and v > 0]
        tok = None
        if inc is not None:
            s, d = inc
            self.cnt[s] = self.cnt.get(s, 0) + d
            tok = (s, self.cnt[s])
        self.ops[eng].append((fn, w, inc))
        return tok

    def dv(self, fn, waits=()):
        return self.emit("vector", fn, waits, ("DV", 1))

    def ac(self, fn, waits=()):
        return self.emit("scalar", fn, waits, ("AC", 1))

    def pe(self, fn, waits=()):
        return self.emit("tensor", fn, waits, ("PE", 1))

    def ld(self, fn, waits=(), q=None):
        return self.emit("sync", fn, waits, (q, 16))

    def st(self, fn, waits=(), q=None):
        return self.emit("scalar", fn, waits, (q, 16))

    def gp(self, fn, waits=(), q=None):
        return self.emit("gpsimd", fn, waits, (q, 16))

    def dv_drain(self):
        self.emit("vector", lambda v: v.drain())

    def barrier_token(self, sem):
        return (sem, self.cnt.get(sem, 0))

    def barrier(self):
        toks = sorted((s, v) for s, v in self.cnt.items() if v > 0)
        for e in self.ENGINES:
            self.ops[e].append((None, [(s, v) for (s, v) in toks if v > 0], None))


class QSeq:

    def __init__(self, plan, tmp):
        self.plan = plan
        self.tmp = tmp
        self.tiles = []
        self.i = 0
        self.heads_done = -1
        self._gen = self._generator()
        self.exhausted = False

    def feed(self, tiles):
        self.tiles.extend(tiles)
        self.exhausted = False

    def _head1(self, j):
        t = self.tiles[j]
        b = j % 2
        nb = t["w"] // 16
        rmax = self.tmp["rmax"][b][:, 0:nb]
        x3 = t["x"].rearrange("p (nb w) -> p nb w", w=16)
        return self.plan.dv(lambda v, rmax=rmax, x3=x3: v.tensor_reduce(
            rmax, x3, AX.X, ALU.max, apply_absolute_value=True), t["gates"])

    def _head2(self, j):
        b = j % 2
        nb = self.tiles[j]["w"] // 16
        rmax = self.tmp["rmax"][b][:, 0:nb]
        rr12 = self.tmp["rr12"][b][:, 0:nb]
        self.plan.dv(lambda v, rmax=rmax: v.tensor_scalar_max(rmax, rmax, 6e-12))
        return self.plan.dv(lambda v, rr12=rr12, rmax=rmax: v.reciprocal(rr12, rmax))

    def _head3(self, j):
        b = j % 2
        nb = self.tiles[j]["w"] // 16
        rmax = self.tmp["rmax"][b][:, 0:nb]
        rr12 = self.tmp["rr12"][b][:, 0:nb]
        s2 = self.tmp["s2"][b][:, 0:nb]
        self.plan.dv(lambda v, rr12=rr12: v.tensor_scalar_mul(rr12, rr12, 12.0))
        return self.plan.dv(lambda v, s2=s2, rmax=rmax: v.tensor_scalar_mul(
            s2, rmax, 1.0 / 12.0))

    def _emit_heads_guarded(self, j):
        if self.tiles[j].get("prep"):
            self.tiles[j]["prep"]()
            self.plan.dv_drain()
        self._head1(j)
        self.plan.dv_drain()
        self._head2(j)
        self.plan.dv_drain()
        self._head3(j)
        self.plan.dv_drain()
        self.heads_done = j

    def _generator(self):
        plan, tmp = self.plan, self.tmp
        while True:
            if self.i >= len(self.tiles):
                yield "idle"
                continue
            i = self.i
            if self.heads_done < i:
                self._emit_heads_guarded(i)
                yield 1
            t = self.tiles[i]
            b = i % 2
            w = t["w"]
            nb = w // 16
            rr12 = tmp["rr12"][b][:, 0:nb]
            s2 = tmp["s2"][b][:, 0:nb]
            v2 = tmp["v2"][:, 0:w]
            u = tmp["u"][:, 0:w]
            g = tmp["g"][:, 0:w]
            hh = tmp["hh"][:, 0:w]
            sgn = tmp["sgn"][:, 0:w]
            x3 = t["x"].rearrange("p (nb w) -> p nb w", w=16)
            v23 = v2.rearrange("p (nb w) -> p nb w", w=16)
            rr12_bc = rr12.unsqueeze(2).broadcast_to([128, nb, 16])
            s2_bc = s2.unsqueeze(2).broadcast_to([128, nb, 16])

            have_next = (i + 1 < len(self.tiles)) and (self.heads_done == i)
            if have_next and self.tiles[i + 1].get("prep"):
                self.tiles[i + 1]["prep"]()
            plan.dv(lambda v, v23=v23, x3=x3, rr12_bc=rr12_bc:
                    v.tensor_tensor(v23, x3, rr12_bc, ALU.mult), list(t["gates"]))
            yield 1
            plan.dv(lambda v, u=u, v2=v2: v.tensor_tensor(u, v2, v2, ALU.mult))
            yield 1
            plan.dv(lambda v, sgn=sgn, v2=v2: v.tensor_scalar(
                sgn, v2, 0.0, 2.0, ALU.is_gt, ALU.mult))
            plan.dv(lambda v, sgn=sgn: v.tensor_scalar_sub(sgn, sgn, 1.0))
            yield 1
            if have_next:
                self._head1(i + 1)
                yield 1
            plan.dv(lambda v, g=g, u=u: v.tensor_single_scalar(g, u, 0.25, ALU.is_gt))
            yield 1
            plan.dv(lambda v, g=g, u=u: v.scalar_tensor_tensor(g, u, 2.25, g, ALU.is_gt, ALU.add))
            yield 1
            plan.dv(lambda v, g=g, u=u: v.scalar_tensor_tensor(g, u, 6.25, g, ALU.is_gt, ALU.add))
            yield 1
            plan.dv(lambda v, g=g, u=u: v.scalar_tensor_tensor(g, u, 12.25, g, ALU.is_gt, ALU.add))
            yield 1
            if have_next:
                self._head2(i + 1)
                yield 1
            plan.dv(lambda v, hh=hh, u=u: v.tensor_single_scalar(hh, u, 100.0, ALU.is_gt))
            yield 1
            plan.dv(lambda v, hh=hh, u=u: v.scalar_tensor_tensor(hh, u, 25.0, hh, ALU.is_gt, ALU.add))
            yield 1
            plan.dv(lambda v, hh=hh, u=u: v.scalar_tensor_tensor(hh, u, 49.0, hh, ALU.is_gt, ALU.add))
            yield 1
            plan.dv(lambda v, hh=hh, u=u: v.scalar_tensor_tensor(hh, u, 100.0, hh, ALU.is_gt, ALU.add))
            yield 1
            if have_next:
                self._head3(i + 1)
                self.heads_done = i + 1
                yield 1
            plan.dv(lambda v, g=g, hh=hh: v.scalar_tensor_tensor(g, hh, 2.0, g, ALU.mult, ALU.add))
            yield 1
            plan.dv(lambda v, g=g, sgn=sgn: v.tensor_tensor(g, g, sgn, ALU.mult))
            yield 1
            o3 = t["out"].rearrange("p (nb w) -> p nb w", w=16)
            g3 = g.rearrange("p (nb w) -> p nb w", w=16)
            wout = [t["out_gate"]] if t.get("out_gate") else []
            tok = plan.dv(lambda v, o3=o3, g3=g3, s2_bc=s2_bc:
                          v.tensor_tensor(o3, g3, s2_bc, ALU.mult), wout)
            if t.get("on_done"):
                t["on_done"](tok)
            self.i += 1
            yield "tile"

    def fill(self, n):
        for _ in range(n):
            if self.i >= len(self.tiles):
                return
            next(self._gen)

    def emit_next(self):
        i0 = self.i
        while self.i == i0:
            next(self._gen)

    def start(self):
        pass

    def pending(self):
        return len(self.tiles) - self.i


def build_program():
    nc = bass.Bass("TRN2", target_bir_lowering=False)

    x_h = nc.dram_tensor("x", [T, C], F32, kind="ExternalInput").ap()
    wqkT_h = nc.dram_tensor("wqkT", [C, 2048], F32, kind="ExternalInput").ap()
    wvT_h = nc.dram_tensor("wvT", [C, C], F32, kind="ExternalInput").ap()
    wpT_h = nc.dram_tensor("wpT", [C, C], F32, kind="ExternalInput").ap()
    bproj_h = nc.dram_tensor("bproj", [1, C], F32, kind="ExternalInput").ap()
    mygu_h = nc.dram_tensor("my_gu", [44 * 128, 128], F32, kind="ExternalInput").ap()
    mydn_h = nc.dram_tensor("my_dn", [256, QW], F32, kind="ExternalInput").ap()
    cmask_h = nc.dram_tensor("cmask", [128, 896], BF16, kind="ExternalInput").ap()
    ones_h = nc.dram_tensor("ones_all", [128, 128], F32, kind="ExternalInput").ap()
    onesb_h = nc.dram_tensor("ones_bf", [128, 128], BF16, kind="ExternalInput").ap()
    ones33_h = nc.dram_tensor("ones33", [128, 33], BF16, kind="ExternalInput").ap()
    eye_h = nc.dram_tensor("eye64", [64, 64], BF16, kind="ExternalInput").ap()
    out_h = nc.dram_tensor("out", [T, C], F32, kind="ExternalOutput").ap()

    s_xnbf = nc.dram_tensor("s_xnbf", [T, C], BF16, kind="Internal").ap()
    s_xq = nc.dram_tensor("s_xq", [T, C], BF16, kind="Internal").ap()
    s_xqT = nc.dram_tensor("s_xqT", [C, T], BF16, kind="Internal").ap()
    s_hq = nc.dram_tensor("s_hq", [T, I], BF16, kind="Internal").ap()
    s_hqT = nc.dram_tensor("s_hqT", [I, T], BF16, kind="Internal").ap()
    s_catT = nc.dram_tensor("s_catT", [C, T], BF16, kind="Internal").ap()
    cc_gu_in = nc.dram_tensor("cc_gu_in", [44 * 128, 128], BF16, kind="Internal").ap()
    cc_dn_in = nc.dram_tensor("cc_dn_in", [256, QW], BF16, kind="Internal").ap()
    s_guq = nc.dram_tensor("s_guq", [8 * 44 * 128, 128], BF16, kind="Internal").ap()
    s_wdq = nc.dram_tensor("s_wdq", [C, I], BF16, kind="Internal").ap()

    plan = Plan()

    SEM_NAMES = ["PE", "DV", "AC", "LK", "LX0", "LX1", "SN0", "SN1",
                 "SQ0", "SQ1", "LT", "LR0", "LR1", "SR0", "SR1",
                 "GW0", "GW1", "GV", "LW0", "LW1", "WS0", "WS1",
                 "SC0", "SC1", "CCG", "CCD", "LG", "LQ0", "LQ1",
                 "SH0", "SH1"]
    es = contextlib.ExitStack()
    with es:
        arena_t = es.enter_context(nc.sbuf_tensor("arena", [128, ARENA_KB * 1024], U8))
        ps_big = es.enter_context(nc.psum_tensor("ps_big", [128, 3072], F32))
        ps_sm = es.enter_context(nc.psum_tensor("ps_sm", [128, 1024], F32))
        sems = {n: es.enter_context(nc.semaphore(n)) for n in SEM_NAMES}
        block = es.enter_context(nc.Block())
        cv = Carver(arena_t[:])

        cmask = cv.take(896, BF16, "cmask")
        ones_all = cv.take(128, F32, "ones")
        ones_bf = cv.take(128, BF16, "onesb")
        ones33 = cv.take(33, BF16, "ones33")
        eye64 = cv.take(64, BF16, "eye")
        bproj = cv.take(C, F32, "bproj")
        rsum = cv.take(1024, F32, "rsum")
        rb = cv.take(1024, F32, "rb")
        x_t = [cv.take(C, F32, "x0"), cv.take(C, F32, "x1")]
        catb = [cv.take(512, BF16, "cat0"), cv.take(512, BF16, "cat1")]
        stgB = cv.take(512, BF16, "stgB")

        def carve_qtmp():
            return {
                "rmax": [cv.take(QW // 16, F32), cv.take(QW // 16, F32)],
                "rr12": [cv.take(QW // 16, F32), cv.take(QW // 16, F32)],
                "s2": [cv.take(QW // 16, F32), cv.take(QW // 16, F32)],
                "v2": cv.take(QW, F32), "u": cv.take(QW, F32),
                "g": cv.take(QW, F32), "hh": cv.take(QW, F32),
                "sgn": cv.take(QW, F32),
            }

        plan.ld(lambda e: e.dma_start(out=cmask, in_=cmask_h), q="LK")
        plan.ld(lambda e: e.dma_start(out=ones_all, in_=ones_h), q="LK")
        plan.ld(lambda e: e.dma_start(out=ones_bf, in_=onesb_h), q="LK")
        plan.ld(lambda e: e.dma_start(out=ones33, in_=ones33_h), q="LK")
        plan.ld(lambda e: e.dma_start(out=eye64[0:64, :], in_=eye_h), q="LK")
        plan.ld(lambda e: e.dma_start(out=bproj[0:1, :], in_=bproj_h), q="LK")
        t_consts = plan.barrier_token("LK")

        cv.push()
        qtmp = carve_qtmp()
        stats = cv.take(4 * NTC, F32, "stats")
        msum_all = stats[:, 0:NTC]
        mean_all = stats[:, 2 * NTC:3 * NTC]
        rstd_all = stats[:, 3 * NTC:4 * NTC]
        sumsq_all = stats[:, NTC:2 * NTC]
        sqs = qtmp["v2"][:, 0:C]

        tok_xread = [None, None]
        tok_xact = [None, None]
        tok_sq = None
        for i in range(NTC):
            b = i % 2
            w = [t for t in (tok_xread[b], tok_xact[b]) if t]
            tld = plan.ld(lambda e, i=i, b=b: e.dma_start(
                out=x_t[b], in_=x_h[i * 128:(i + 1) * 128, :]), w + [t_consts],
                q=f"LX{b}")
            tok_xact[b] = plan.ac(lambda e, b=b: e.activation(sqs, x_t[b], ACTF.Square),
                                  [tld] + ([tok_sq] if tok_sq else []))
            tok_xread[b] = plan.dv(lambda e, i=i, b=b: e.reduce_sum(
                msum_all[:, i:i + 1], x_t[b], AX.X), [tld])
            tok_sq = plan.dv(lambda e, i=i: e.reduce_sum(
                sumsq_all[:, i:i + 1], sqs, AX.X), [tok_xact[b]])
        plan.dv_drain()
        plan.dv(lambda e: e.tensor_scalar_mul(mean_all, msum_all, 1.0 / C))
        plan.dv(lambda e: e.tensor_scalar(rstd_all, sumsq_all, 1.0 / C, 1e-5,
                                          ALU.mult, ALU.add))
        plan.dv_drain()
        plan.dv(lambda e: e.tensor_mul(msum_all, mean_all, mean_all))
        plan.dv_drain()
        tvar = plan.dv(lambda e: e.tensor_sub(rstd_all, rstd_all, msum_all))
        tsd = plan.ac(lambda e: e.activation(rstd_all, rstd_all, ACTF.Sqrt), [tvar])
        plan.dv(lambda e: e.reciprocal(rstd_all, rstd_all), [tsd])
        plan.dv_drain()

        xn_t = [cv.take(C, F32, "xn0"), cv.take(C, F32, "xn1")]
        bfb = [cv.take(C, BF16, "bf0"), cv.take(C, BF16, "bf1")]
        qxb = [cv.take(C, BF16, "qx0"), cv.take(C, BF16, "qx1")]

        qs1 = QSeq(plan, qtmp)
        st1 = {"apply": [None, None], "st_bf": [None, None], "st_qx": [None, None]}

        def mk_prep1(i):
            def prep():
                b = i % 2
                wld = [t for t in (st1["apply"][b], tok_xread[b], tok_xact[b]) if t]
                tld = plan.ld(lambda e: e.dma_start(
                    out=x_t[b], in_=x_h[i * 128:(i + 1) * 128, :]), wld, q=f"LX{b}")
                tap = plan.dv(lambda e: e.tensor_scalar(
                    xn_t[b], x_t[b], mean_all[:, i:i + 1], rstd_all[:, i:i + 1],
                    ALU.subtract, ALU.mult), [tld])
                st1["apply"][b] = tap
                tiles1[i]["gates"] = [tap]
                tiles1[i]["out_gate"] = st1["st_qx"][b]
                tcast = plan.ac(lambda e: e.copy(bfb[b], xn_t[b]),
                                [tap] + ([st1["st_bf"][b]] if st1["st_bf"][b] else []))
                st1["st_bf"][b] = plan.st(lambda e: e.dma_start(
                    out=s_xnbf[i * 128:(i + 1) * 128, :], in_=bfb[b]), [tcast],
                    q=f"SN{b}")
            return prep

        def mk_done1(i):
            def done(tok):
                b = i % 2
                st1["st_qx"][b] = plan.st(lambda e: e.dma_start(
                    out=s_xq[i * 128:(i + 1) * 128, :], in_=qxb[b]), [tok],
                    q=f"SQ{b}")
            return done

        tiles1 = [dict(x=xn_t[i % 2], out=qxb[i % 2], w=C, gates=[],
                       prep=mk_prep1(i), on_done=mk_done1(i), out_gate=None)
                  for i in range(NTC)]
        qs1.feed(tiles1)
        while qs1.pending():
            qs1.emit_next()
        plan.barrier()
        cv.pop()

        cv.push()
        qT = cv.take(NP * T, BF16, "qT")
        kT = cv.take(NP * T, BF16, "kT")
        vn = cv.take(H * NTC * 65, BF16, "vn")
        cv.push()
        xnT = cv.take(NCC * T, BF16, "xnT")
        stg = [cv.take(T, BF16, "stg0"), cv.take(T, BF16, "stg1")]
        wqk = [cv.take(NCC * 256, BF16, "wqk0"), cv.take(NCC * 256, BF16, "wqk1")]
        wvT = cv.take(NCC * 1024, BF16, "wvT")

        t_sn = [plan.barrier_token("SN0"), plan.barrier_token("SN1")]
        t_sq = [plan.barrier_token("SQ0"), plan.barrier_token("SQ1")]
        for ccx in range(NCC):
            plan.ld(lambda e, ccx=ccx: e.dma_start_transpose(
                xnT[:, ccx * T:(ccx + 1) * T], s_xnbf[:, ccx * 128:(ccx + 1) * 128]),
                t_sn, q="LT")
        stg_tok = [None, None]
        for ccx in range(NCC):
            b = ccx % 2
            tt = plan.ld(lambda e, ccx=ccx, b=b: e.dma_start_transpose(
                stg[b], s_xq[:, ccx * 128:(ccx + 1) * 128]),
                (t_sq if ccx < 2 else []) + ([stg_tok[b]] if stg_tok[b] else []),
                q=f"LR{b}")
            stg_tok[b] = plan.st(lambda e, ccx=ccx, b=b: e.dma_start(
                out=s_xqT[ccx * 128:(ccx + 1) * 128, :], in_=stg[b]), [tt],
                q=f"SR{b}")
        vn3 = vn.rearrange("p (b o) -> p b o", o=65)
        t_vn_ones = plan.dv(lambda e: e.memset(vn3[:, :, 64:65], 1.0))
        wvT_v = wvT.rearrange("p (cc n) -> p cc n", n=1024)
        t_wv = plan.gp(lambda e: e.dma_start(
            out=wvT_v, in_=wvT_h.rearrange("(cc p) n -> p cc n", p=128)), q="GV")
        wqk_tok = [None, None]
        wqk_used = [None, None]

        def load_wqk(p):
            b = p % 2
            wqk_tok[b] = plan.gp(lambda e, p=p, b=b: e.dma_start(
                out=wqk[b].rearrange("p (cc n) -> p cc n", n=256),
                in_=wqkT_h.rearrange("(cc p) n -> p cc n", p=128)[:, :, p * 256:(p + 1) * 256]),
                [wqk_used[b]] if wqk_used[b] else [], q=f"GW{b}")

        load_wqk(0)
        psQ = [ps_big[:, 0:512], ps_big[:, 1024:1536]]
        psK = [ps_big[:, 512:1024], ps_big[:, 1536:2048]]
        psV = [ps_big[:, 2048:2560], ps_big[:, 2560:3072],
               ps_sm[:, 0:512], ps_sm[:, 512:1024]]
        evacQ = {}
        t_xnT = plan.barrier_token("LT")
        for p in range(NP):
            b = p % 2
            if p + 1 < NP:
                load_wqk(p + 1)
            wqk_v = wqk[b].rearrange("p (cc n) -> p cc n", n=256)
            for qt in range(NQT):
                ib = (p * NQT + qt) % 2
                wts = [t_xnT, wqk_tok[b]] + ([evacQ.get(ib)] if evacQ.get(ib) else [])
                tpe = None
                for ccx in range(NCC):
                    st0, sp = (ccx == 0), (ccx == NCC - 1)
                    plan.pe(lambda e, qt=qt, ccx=ccx, ib=ib, wqk_v=wqk_v, st0=st0, sp=sp:
                            e.matmul(psQ[ib], wqk_v[:, ccx, 0:128],
                                     xnT[:, ccx * T + qt * 512: ccx * T + qt * 512 + 512],
                                     start=st0, stop=sp),
                            wts if ccx == 0 else [])
                    tpe = plan.pe(lambda e, qt=qt, ccx=ccx, ib=ib, wqk_v=wqk_v, st0=st0, sp=sp:
                                  e.matmul(psK[ib], wqk_v[:, ccx, 128:256],
                                           xnT[:, ccx * T + qt * 512: ccx * T + qt * 512 + 512],
                                           start=st0, stop=sp))
                plan.dv(lambda e, p=p, qt=qt, ib=ib: e.tensor_copy(
                    qT[:, p * T + qt * 512: p * T + qt * 512 + 512], psQ[ib]), [tpe])
                evacQ[ib] = plan.dv(lambda e, p=p, qt=qt, ib=ib: e.tensor_copy(
                    kT[:, p * T + qt * 512: p * T + qt * 512 + 512], psK[ib]))
            wqk_used[b] = evacQ[(p * NQT + NQT - 1) % 2]
        evacV = {}
        for tcv in range(NTC):
            vb0, vb1 = (2 * tcv) % 4, (2 * tcv + 1) % 4
            wts = [t_wv, t_vn_ones] + [evacV[q] for q in (vb0, vb1) if q in evacV]
            tpe = None
            for ccx in range(NCC):
                st0, sp = (ccx == 0), (ccx == NCC - 1)
                plan.pe(lambda e, tcv=tcv, ccx=ccx, vb0=vb0, st0=st0, sp=sp:
                        e.matmul(psV[vb0], xnT[:, ccx * T + tcv * 128: ccx * T + tcv * 128 + 128],
                                 wvT_v[:, ccx, 0:512], start=st0, stop=sp),
                        wts if ccx == 0 else [])
                tpe = plan.pe(lambda e, tcv=tcv, ccx=ccx, vb1=vb1, st0=st0, sp=sp:
                              e.matmul(psV[vb1], xnT[:, ccx * T + tcv * 128: ccx * T + tcv * 128 + 128],
                                       wvT_v[:, ccx, 512:1024], start=st0, stop=sp))
            dst3 = vn.rearrange("p (hh kc o) -> p hh kc o", kc=NTC, o=65)
            for half, q in ((0, vb0), (1, vb1)):
                src = psV[q].rearrange("p (h d) -> p h d", d=64)
                evacV[q] = plan.dv(
                    lambda e, src=src, dst3=dst3, half=half, tcv=tcv:
                    e.tensor_copy(dst3[:, half * 8:(half + 1) * 8, tcv, 0:64], src),
                    [tpe] if half == 0 else [])
        cv.pop()
        plan.barrier()

        cv.push()
        qtmp4 = carve_qtmp()
        pt = [cv.take(1024, BF16, "pt0"), cv.take(1024, BF16, "pt1")]
        m8 = cv.take(T, BF16, "m8")
        prod = cv.take(512, BF16, "prod")
        qn_sb = cv.take(512, F32, "qn")
        mt_sb = cv.take(512, F32, "mt")
        m2_sb = cv.take(512, F32, "m2t")
        kmax2 = cv.take(4, F32, "kmax2")
        wld = [cv.take(QW, F32, "wld0"), cv.take(QW, F32, "wld1")]
        wqb = [cv.take(QW, BF16, "wqb0"), cv.take(QW, BF16, "wqb1")]

        qs2 = QSeq(plan, qtmp4)
        wqs = {"ld": {}, "st": [None, None], "last_gu": None, "last_dn": None}

        mygu3 = mygu_h.rearrange("(t p) c -> p t c", p=128)
        ccgu3 = cc_gu_in.rearrange("(t p) c -> p t c", p=128)
        NWQ = 6

        def wio(j):
            b = j % 2
            if j < 4:
                return (mygu3[:, 11 * j:11 * (j + 1), :],
                        ccgu3[:, 11 * j:11 * (j + 1), :],
                        wld[b].rearrange("p (t c) -> p t c", c=128),
                        wqb[b].rearrange("p (t c) -> p t c", c=128))
            r = j - 4
            return (mydn_h[r * 128:(r + 1) * 128, :],
                    cc_dn_in[r * 128:(r + 1) * 128, :],
                    wld[b][:, 0:QW], wqb[b][:, 0:QW])

        def mk_wprep(j):
            src, _, ldst, _ = wio(j)
            def prep():
                wts = [t for t in (wqs["ld"].get(j - 2),) if t]
                wqs["ld"][j] = plan.ld(lambda e, src=src, ldst=ldst: e.dma_start(
                    out=ldst, in_=src), wts, q=f"LW{j % 2}")
                wtiles[j]["gates"] = [wqs["ld"][j]]
                wtiles[j]["out_gate"] = wqs["st"][j % 2]
            return prep

        def mk_wdone(j):
            _, dst, _, qsrc = wio(j)
            def done(tok):
                st = plan.emit("scalar", lambda e, dst=dst, qsrc=qsrc: e.dma_start(
                    out=dst, in_=qsrc), [tok], (f"WS{j % 2}", 16))
                wqs["st"][j % 2] = st
                wqs["ld"][j] = tok
            return done

        wtiles = [dict(x=wld[j % 2][:, 0:QW],
                       out=wqb[j % 2][:, 0:QW], w=QW,
                       gates=[], prep=mk_wprep(j), on_done=mk_wdone(j),
                       out_gate=None) for j in range(NWQ)]
        qs2.feed(wtiles)

        psS = [ps_big[:, 0:1024], ps_big[:, 1024:2048]]
        psAccA = ps_big[:, 2048:2560]
        psAccB = ps_big[:, 2560:3072]
        psRbA = ps_sm[:, 0:512]
        psRbB = ps_sm[:, 512:1024]
        psM = ps_big[:, 0:512]
        psQ2 = ps_big[:, 512:1024]
        psK2 = ps_sm[:, 0:512]
        vn4 = vn.rearrange("p (hh kc o) -> p hh kc o", kc=NTC, o=65)

        exp_tok = {}
        mm2_tok = {}
        acc_free = None
        rb_free = None
        sh_prev = None
        shev_prev = None
        cat_st = [None, None]
        prod_pe_gate = None
        m8_pe_gate = None

        for p in range(NP):
            hA, hB = 2 * p, 2 * p + 1
            wz = [t for t in (sh_prev, exp_tok.get(0), exp_tok.get(1)) if t]
            tz = plan.dv(lambda e: e.memset(kmax2[:, 0:1], 0.0), wz)
            tred = None
            for tt in range(NQT):
                sl = p * T + tt * 512
                w = ([tred] if tred else []) + ([prod_pe_gate] if prod_pe_gate else [])
                tq = plan.dv(lambda e, sl=sl: e.tensor_tensor(
                    prod, kT[:, sl:sl + 512], kT[:, sl:sl + 512], ALU.mult), w)
                wk2 = [tq, tz] + ([shev_prev] if (tt == 0 and shev_prev) else [])
                tm = plan.pe(lambda e: e.matmul(psK2[0:33, :], ones33[:, 0:33], prod,
                                                start=True, stop=True), wk2)
                prod_pe_gate = tm
                tred = plan.dv(lambda e: e.reduce_max(kmax2[0:33, 1:2], psK2[0:33, :],
                                                      AX.X), [tm])
                plan.dv_drain()
                tred = plan.dv(lambda e: e.tensor_max(kmax2[0:33, 0:1], kmax2[0:33, 0:1],
                                                      kmax2[0:33, 1:2]))
                plan.dv_drain()
                qs2.fill(2)
            tkm = plan.ac(lambda e: e.activation(kmax2[0:33, 2:3], kmax2[0:33, 0:1],
                                                 ACTF.Sqrt), [tred])
            t_m2_prev = None
            t_sq_prev = None
            for tt in range(NQT):
                sl = p * T + tt * 512
                slm = tt * 512
                w1 = [tred, prod_pe_gate] + ([m8_pe_gate] if (tt == 0 and m8_pe_gate) else [])
                tq1 = plan.dv(lambda e, sl=sl: e.tensor_tensor(
                    prod, qT[:, sl:sl + 512], kT[:, sl:sl + 512], ALU.mult), w1)
                tm1 = plan.pe(lambda e: e.matmul(psM[0:33, :], ones33[:, 0:33], prod,
                                                 start=True, stop=True),
                              [tq1] + ([t_m2_prev] if t_m2_prev else []))
                tq2 = plan.dv(lambda e, sl=sl: e.tensor_tensor(
                    prod, qT[:, sl:sl + 512], qT[:, sl:sl + 512], ALU.mult), [tm1])
                tm2 = plan.pe(lambda e: e.matmul(psQ2[0:33, :], ones33[:, 0:33], prod,
                                                 start=True, stop=True),
                              [tq2] + ([t_sq_prev] if t_sq_prev else []))
                prod_pe_gate = tm2
                tsq = plan.ac(lambda e: e.activation(qn_sb[0:33, :], psQ2[0:33, :],
                                                     ACTF.Sqrt), [tm2, tkm])
                t_sq_prev = tsq
                plan.dv(lambda e: e.tensor_scalar(
                    mt_sb[0:33, :], qn_sb[0:33, :], kmax2[0:33, 2:3], 0.125,
                    ALU.mult, ALU.mult), [tsq])
                plan.dv(lambda e: e.tensor_scalar_sub(mt_sb[0:33, :], mt_sb[0:33, :], 70.0))
                t_m2_prev = plan.dv(lambda e: e.tensor_scalar_mul(
                    m2_sb[0:33, :], psM[0:33, :], 0.125))
                plan.dv_drain()
                plan.dv(lambda e: e.tensor_max(mt_sb[0:33, :], mt_sb[0:33, :],
                                               m2_sb[0:33, :]))
                t_m8w = plan.dv(lambda e, slm=slm: e.tensor_scalar_mul(
                    m8[0:33, slm:slm + 512], mt_sb[0:33, :], -8.0))
                qs2.fill(2)
            plan.dv_drain()
            mblock_done = [t_m8w, t_sq_prev, t_m2_prev]

            for qt in range(NQT):
                nkc = 4 * qt + 4
                gates = {}

                def emit_mm2(j, nkc=nkc, hA=hA, hB=hB, gates=gates):
                    nonlocal acc_free
                    sbj = j % 2
                    st0, sp = (j == 0), (j == nkc - 1)
                    wj = [gates[j]]
                    if j == 0 and acc_free:
                        wj = wj + [acc_free]
                    plan.pe(lambda e, hA=hA, j=j, sbj=sbj, st0=st0, sp=sp:
                            e.matmul(psAccA[0:65, :], vn4[:, hA, j, :],
                                     pt[sbj][:, 0:512], start=st0, stop=sp), wj)
                    tj = plan.pe(lambda e, hB=hB, j=j, sbj=sbj, st0=st0, sp=sp:
                                 e.matmul(psAccB[0:65, :], vn4[:, hB, j, :],
                                          pt[sbj][:, 512:1024], start=st0, stop=sp))
                    mm2_tok[sbj] = tj
                    return tj

                for kc in range(nkc):
                    sb = kc % 2
                    w1 = [exp_tok.get(sb)] if exp_tok.get(sb) else []
                    if qt == 0 and kc <= 1:
                        w1 = w1 + mblock_done
                    plan.pe(lambda e, p=p, qt=qt, kc=kc, sb=sb:
                            e.matmul(psS[sb][:, 0:512],
                                     kT[0:64, p * T + kc * 128: p * T + kc * 128 + 128],
                                     qT[0:64, p * T + qt * 512: p * T + qt * 512 + 512],
                                     start=True, stop=False), w1)
                    plan.pe(lambda e, qt=qt, sb=sb:
                            e.matmul(psS[sb][:, 0:512], ones_bf[0:1, 0:128],
                                     m8[0:1, qt * 512: qt * 512 + 512],
                                     start=False, stop=True))
                    plan.pe(lambda e, p=p, qt=qt, kc=kc, sb=sb:
                            e.matmul(psS[sb][:, 512:1024],
                                     kT[64:128, p * T + kc * 128: p * T + kc * 128 + 128],
                                     qT[64:128, p * T + qt * 512: p * T + qt * 512 + 512],
                                     start=True, stop=False))
                    t_mm1 = plan.pe(lambda e, qt=qt, sb=sb:
                                    e.matmul(psS[sb][:, 512:1024], ones_bf[32:33, 0:128],
                                             m8[32:33, qt * 512: qt * 512 + 512],
                                             start=False, stop=True))
                    m8_pe_gate = t_mm1
                    if kc >= 1:
                        emit_mm2(kc - 1)
                    we = [t_mm1] + ([mm2_tok.get(sb)] if mm2_tok.get(sb) else [])
                    texp = plan.ac(lambda e, sb=sb: e.activation(
                        pt[sb], psS[sb], ACTF.Exp, scale=0.125), we)
                    exp_tok[sb] = texp
                    gate = texp
                    if kc >= 4 * qt:
                        off = 384 - (kc * 128 - qt * 512)
                        cm = cmask[:, off:off + 512].unsqueeze(1).broadcast_to([128, 2, 512])
                        pt3 = pt[sb].rearrange("p (t n) -> p t n", t=2)
                        gate = plan.dv(lambda e, sb=sb, cm=cm, pt3=pt3:
                                       e.tensor_tensor(pt3, pt3, cm, ALU.mult), [texp])
                    gates[kc] = gate
                    qs2.fill(2)
                t_last_mm2 = emit_mm2(nkc - 1)
                wrb = [rb_free] if rb_free else []
                plan.dv(lambda e: e.tensor_copy(rsum[64:65, 0:512], psAccA[64:65, :]),
                        [t_last_mm2] + wrb)
                plan.dv(lambda e: e.tensor_copy(rsum[64:65, 512:1024], psAccB[64:65, :]))
                plan.dv_drain()
                trc = plan.dv(lambda e: e.reciprocal(rsum[64:65, :], rsum[64:65, :]))
                qs2.fill(2)
                wsh = [shev_prev] if shev_prev else []
                plan.pe(lambda e: e.matmul(psRbA[0:64, :], ones_all[64:65, 0:64],
                                           rsum[64:65, 0:512], start=True, stop=True),
                        [trc] + wsh)
                trb = plan.pe(lambda e: e.matmul(psRbB[0:64, :], ones_all[64:65, 0:64],
                                                 rsum[64:65, 512:1024], start=True, stop=True))
                plan.dv(lambda e: e.tensor_copy(rb[0:64, 0:512], psRbA[0:64, :]), [trb])
                t_rbev = plan.dv(lambda e: e.tensor_copy(rb[0:64, 512:1024], psRbB[0:64, :]))
                rb_free = t_rbev
                cb = (p * NQT + qt) % 2
                wcat = [cat_st[cb]] if cat_st[cb] else []
                plan.dv(lambda e, cb=cb: e.tensor_tensor(
                    catb[cb][0:64, :], psAccA[0:64, :], rb[0:64, 0:512], ALU.mult), wcat)
                wstg = [sh_prev] if sh_prev else []
                t_nB = plan.dv(lambda e: e.tensor_tensor(
                    stgB[0:64, :], psAccB[0:64, :], rb[0:64, 512:1024], ALU.mult), wstg)
                acc_free = t_nB
                qs2.fill(2)
                t_sh = plan.pe(lambda e: e.matmul(ps_sm[64:128, 0:512], eye64[0:64, :],
                                                  stgB[0:64, :], start=True, stop=True),
                               [t_nB, t_rbev])
                sh_prev = t_sh
                t_shev = plan.dv(lambda e, cb=cb: e.tensor_copy(
                    catb[cb][64:128, :], ps_sm[64:128, 0:512]), [t_sh])
                shev_prev = t_shev
                cat_st[cb] = plan.st(lambda e, p=p, qt=qt, cb=cb: e.dma_start(
                    out=s_catT[p * 128:(p + 1) * 128, qt * 512:qt * 512 + 512],
                    in_=catb[cb]), [t_shev], q=f"SC{cb}")
                qs2.fill(6)
        while qs2.pending():
            qs2.emit_next()
        t_ws_all = [("WS0", 16 * (NWQ // 2)), ("WS1", 16 * (NWQ // 2))]
        t_ag_gu = plan.emit("gpsimd", lambda e: e.collective_compute(
            "AllGather", ALU.bypass, replica_groups=[list(range(8))],
            ins=[cc_gu_in], outs=[s_guq]), t_ws_all, ("CCG", 1))
        t_ag_dn = plan.emit("gpsimd", lambda e: e.collective_compute(
            "AllGather", ALU.bypass, replica_groups=[list(range(8))],
            ins=[cc_dn_in], outs=[s_wdq]), t_ws_all, ("CCD", 1))
        plan.barrier()
        cv.pop()
        plan.barrier()
        cv.pop()
        plan.barrier()

        cv.push()
        Wgq = cv.take(NCC * I, BF16, "Wgq")
        Wuq = cv.take(NCC * I, BF16, "Wuq")
        cv.push()
        t_wg_tr = None
        for ccx in range(NCC):
            plan.ld(lambda e, ccx=ccx: e.dma_start_transpose(
                Wgq[:, ccx * I:(ccx + 1) * I],
                s_guq[ccx * 5632:ccx * 5632 + 2816, :]), [t_ag_gu], q="LG")
            t_wg_tr = plan.ld(lambda e, ccx=ccx: e.dma_start_transpose(
                Wuq[:, ccx * I:(ccx + 1) * I],
                s_guq[ccx * 5632 + 2816:(ccx + 1) * 5632, :]), q="LG")

        qtmp5 = carve_qtmp()
        xq_sb = [cv.take(NCC * 128, BF16, "xq0"), cv.take(NCC * 128, BF16, "xq1")]
        sg_sb = cv.take(I, F32, "sg")
        h_sb = cv.take(I, F32, "h")
        hq_sb = [cv.take(I, BF16, "hq0"), cv.take(I, BF16, "hq1")]
        qs3 = QSeq(plan, qtmp5)
        hstate = {"st": [None, None]}
        xq_used = [None, None]

        def mk_hdone(tcf, half):
            def done(tok):
                if half == 1:
                    b = tcf % 2
                    hstate["st"][b] = plan.st(lambda e: e.dma_start(
                        out=s_hq[tcf * 128:(tcf + 1) * 128, :], in_=hq_sb[b]), [tok],
                        q=f"SH{b}")
            return done

        psH = [ps_big[:, 0:1408], ps_big[:, 1536:2944]]
        segs3 = [(0, 512), (512, 512), (1024, 384)]
        gu_free = [None, None]
        pend_tiles = []
        for tcf in range(NTC):
            b = tcf % 2
            tldx = plan.ld(lambda e, tcf=tcf, b=b: e.dma_start(
                out=xq_sb[b].rearrange("p (cc n) -> p cc n", n=128),
                in_=s_xqT.rearrange("(cc p) m -> p cc m", p=128)[:, :, tcf * 128:(tcf + 1) * 128]),
                [xq_used[b]] if xq_used[b] else [], q=f"LQ{b}")
            xqv = xq_sb[b].rearrange("p (cc n) -> p cc n", n=128)
            tsilu = [None, None]
            tg = [None, None]
            for half in (0, 1):
                hofs = half * 1408
                for ccx in range(NCC):
                    for si, (s0, sw) in enumerate(segs3):
                        wts = []
                        if ccx == 0 and si == 0:
                            wts = [tldx, t_wg_tr] + ([gu_free[half]] if gu_free[half] else [])
                        tg[half] = plan.pe(
                            lambda e, ccx=ccx, s0=s0, sw=sw, xqv=xqv, half=half, hofs=hofs:
                            e.matmul(psH[half][:, s0:s0 + sw], xqv[:, ccx, :],
                                     Wgq[:, ccx * I + hofs + s0: ccx * I + hofs + s0 + sw],
                                     start=(ccx == 0), stop=(ccx == NCC - 1)), wts)
                tsilu[half] = plan.ac(
                    lambda e, half=half, hofs=hofs: e.activation(
                        sg_sb[:, hofs:hofs + 1408], psH[half], ACTF.Silu), [tg[half]])
            for half in (0, 1):
                hofs = half * 1408
                tu = None
                for ccx in range(NCC):
                    for si, (s0, sw) in enumerate(segs3):
                        wts = [tsilu[half]] if (ccx == 0 and si == 0) else []
                        tu = plan.pe(
                            lambda e, ccx=ccx, s0=s0, sw=sw, xqv=xqv, half=half, hofs=hofs:
                            e.matmul(psH[half][:, s0:s0 + sw], xqv[:, ccx, :],
                                     Wuq[:, ccx * I + hofs + s0: ccx * I + hofs + s0 + sw],
                                     start=(ccx == 0), stop=(ccx == NCC - 1)), wts)
                if half == 1:
                    xq_used[b] = tu
                th = plan.dv(lambda e, half=half, hofs=hofs: e.tensor_tensor(
                    h_sb[:, hofs:hofs + 1408], sg_sb[:, hofs:hofs + 1408],
                    psH[half], ALU.mult), [tu])
                gu_free[half] = th
                pend_tiles.append(dict(x=h_sb[:, hofs:hofs + 1408],
                                       out=hq_sb[b][:, hofs:hofs + 1408],
                                       w=1408, gates=[th], prep=None,
                                       on_done=mk_hdone(tcf, half),
                                       out_gate=hstate["st"][b]))
            qs3.feed(pend_tiles)
            pend_tiles = []
            qs3.emit_next()
            qs3.emit_next()
        plan.barrier()
        cv.pop()
        cv.pop()
        plan.barrier()

        cv.push()
        Wdq = cv.take(NIC * 1024, BF16, "Wdq")
        _wu_dead = cv.take(NCC * I, BF16, "wud")
        wpT = cv.take(NCC * 1024, BF16, "wpT")
        hqT_sb = [cv.take(NIC * 128, BF16, "hqT0"), cv.take(NIC * 128, BF16, "hqT1")]
        cat_sb = [cv.take(NCC * 128, BF16, "catA"), cv.take(NCC * 128, BF16, "catB")]
        stgT = [cv.take(T, BF16, "stgT0"), cv.take(T, BF16, "stgT1")]
        out_t = [cv.take(C, F32, "o0"), cv.take(C, F32, "o1")]

        t_hq_all = [plan.barrier_token("SH0"), plan.barrier_token("SH1")]
        stg_tok2 = [None, None]
        for icx in range(NIC):
            b = icx % 2
            tt = plan.ld(lambda e, icx=icx, b=b: e.dma_start_transpose(
                stgT[b], s_hq[:, icx * 128:(icx + 1) * 128]),
                t_hq_all + ([stg_tok2[b]] if stg_tok2[b] else []), q=f"LR{b}")
            stg_tok2[b] = plan.st(lambda e, icx=icx, b=b: e.dma_start(
                out=s_hqT[icx * 128:(icx + 1) * 128, :], in_=stgT[b]), [tt],
                q=f"SR{b}")
        t_wd_tr = None
        for icx in range(NIC):
            t_wd_tr = plan.ld(lambda e, icx=icx: e.dma_start_transpose(
                Wdq[:, icx * 1024:(icx + 1) * 1024], s_wdq[:, icx * 128:(icx + 1) * 128]),
                [t_ag_dn], q="LG")
        t_wp = plan.gp(lambda e: e.dma_start(
            out=wpT.rearrange("p (cc n) -> p cc n", n=1024),
            in_=wpT_h.rearrange("(cc p) n -> p cc n", p=128)), q="GV")
        t_hqT_done = [plan.barrier_token("SR0"), plan.barrier_token("SR1")]

        psD = [ps_big[:, 0:1024], ps_big[:, 1024:2048]]
        dn_used = [None, None]
        dn_psfree = [None, None]
        out_st = [None, None]
        for tcd in range(NTC):
            b = tcd % 2
            wl = t_hqT_done + [t_wd_tr, t_wp] if tcd < 2 else []
            tld1 = plan.ld(lambda e, tcd=tcd, b=b: e.dma_start(
                out=hqT_sb[b].rearrange("p (ic n) -> p ic n", n=128),
                in_=s_hqT.rearrange("(ic p) m -> p ic m", p=128)[:, :, tcd * 128:(tcd + 1) * 128]),
                wl + [t for t in (dn_used[b], dn_psfree[b]) if t], q=f"LQ{b}")
            tld2 = plan.ld(lambda e, tcd=tcd, b=b: e.dma_start(
                out=cat_sb[b].rearrange("p (cc n) -> p cc n", n=128),
                in_=s_catT.rearrange("(cc p) m -> p cc m", p=128)[:, :, tcd * 128:(tcd + 1) * 128]),
                [dn_used[b]] if dn_used[b] else [], q=f"LW{b}")
            tld3 = plan.ld(lambda e, tcd=tcd, b=b: e.dma_start(
                out=x_t[b], in_=x_h[tcd * 128:(tcd + 1) * 128, :]),
                [dn_psfree[b]] if dn_psfree[b] else [], q=f"LX{b}")
            hqv = hqT_sb[b].rearrange("p (ic n) -> p ic n", n=128)
            catv = cat_sb[b].rearrange("p (cc n) -> p cc n", n=128)
            tpe = None
            first = True
            wts0 = [tld1, tld2] + ([dn_psfree[b]] if dn_psfree[b] else [])
            for icx in range(NIC):
                plan.pe(lambda e, icx=icx, b=b, hqv=hqv, first=first:
                        e.matmul(psD[b][:, 0:512], hqv[:, icx, :],
                                 Wdq[:, icx * 1024: icx * 1024 + 512],
                                 start=first, stop=False), wts0 if first else [])
                tpe = plan.pe(lambda e, icx=icx, b=b, hqv=hqv, first=first:
                              e.matmul(psD[b][:, 512:1024], hqv[:, icx, :],
                                       Wdq[:, icx * 1024 + 512: icx * 1024 + 1024],
                                       start=first, stop=False))
                first = False
            for ccx in range(NCC):
                plan.pe(lambda e, ccx=ccx, b=b, catv=catv:
                        e.matmul(psD[b][:, 0:512], catv[:, ccx, :],
                                 wpT[:, ccx * 1024: ccx * 1024 + 512],
                                 start=False, stop=False))
                tpe = plan.pe(lambda e, ccx=ccx, b=b, catv=catv:
                              e.matmul(psD[b][:, 512:1024], catv[:, ccx, :],
                                       wpT[:, ccx * 1024 + 512: ccx * 1024 + 1024],
                                       start=False, stop=False))
            plan.pe(lambda e, b=b: e.matmul(psD[b][:, 0:512], ones_all[0:1, 0:128],
                                            bproj[0:1, 0:512], start=False, stop=True))
            tpe = plan.pe(lambda e, b=b: e.matmul(psD[b][:, 512:1024], ones_all[0:1, 0:128],
                                                  bproj[0:1, 512:1024], start=False, stop=True))
            dn_used[b] = tpe
            tep = plan.dv(lambda e, b=b: e.tensor_tensor(out_t[b], psD[b], x_t[b], ALU.add),
                          [tpe, tld3] + ([out_st[b]] if out_st[b] else []))
            dn_psfree[b] = tep
            out_st[b] = plan.st(lambda e, tcd=tcd, b=b: e.dma_start(
                out=out_h[tcd * 128:(tcd + 1) * 128, :], in_=out_t[b]), [tep],
                q=f"SC{b}")
        cv.pop()

        t_out_done = [("SC0", plan.cnt["SC0"]), ("SC1", plan.cnt["SC1"])]
        plan.emit("scalar", None, t_out_done)
        plan.emit("sync", None, t_out_done)

        def run_plan(eng_obj, name):
            for fn, waits, inc in plan.ops[name]:
                for s, v in waits:
                    eng_obj.wait_ge(sems[s], v)
                if fn is None:
                    continue
                inst = fn(eng_obj)
                if inc is not None:
                    s, d = inc
                    inst.then_inc(sems[s], d)

        @block.sync
        def _(sync):
            run_plan(sync, "sync")

        @block.scalar
        def _(scalar):
            run_plan(scalar, "scalar")

        @block.vector
        def _(vector):
            run_plan(vector, "vector")

        @block.tensor
        def _(tensor):
            run_plan(tensor, "tensor")

        @block.gpsimd
        def _(gpsimd):
            run_plan(gpsimd, "gpsimd")

    return nc


_NC_CACHE = None


def kernel(L0_attn_x, L0_attn_w_qkv, L0_attn_w_proj, L0_attn_b_proj,
           L0_ffn_w_gate, L0_ffn_w_up, L0_ffn_w_down):
    global _NC_CACHE
    x = np.ascontiguousarray(np.asarray(L0_attn_x, dtype=np.float32))
    wqkv = np.asarray(L0_attn_w_qkv, dtype=np.float32)
    wproj = np.asarray(L0_attn_w_proj, dtype=np.float32)
    bproj = np.asarray(L0_attn_b_proj, dtype=np.float32)
    wg = np.ascontiguousarray(np.asarray(L0_ffn_w_gate, dtype=np.float32))
    wu = np.ascontiguousarray(np.asarray(L0_ffn_w_up, dtype=np.float32))
    wd = np.ascontiguousarray(np.asarray(L0_ffn_w_down, dtype=np.float32))

    wT = wqkv.transpose(2, 0, 1)
    blocks = []
    for p in range(NP):
        blocks += [wT[:, 2 * p, 0:64], wT[:, 2 * p + 1, 0:64],
                   wT[:, 2 * p, 64:128], wT[:, 2 * p + 1, 64:128]]
    wqkT = np.ascontiguousarray(np.concatenate(blocks, axis=1))
    wvT = np.ascontiguousarray(wT[:, :, 128:192].reshape(C, C))
    wpT = np.ascontiguousarray(wproj.T)
    bproj2 = np.ascontiguousarray(bproj.reshape(1, C))

    jj = np.arange(896)[None, :]
    kk = np.arange(128)[:, None]
    cmask = ((jj - kk) >= 384).astype(ml_dtypes.bfloat16)
    ones_all = np.ones((128, 128), dtype=np.float32)
    ones_bf = np.ones((128, 128), dtype=ml_dtypes.bfloat16)
    ones33 = np.zeros((128, 33), dtype=ml_dtypes.bfloat16)
    ones33[0:64, 0] = 1
    ones33[64:128, 32] = 1
    eye64 = np.eye(64, dtype=ml_dtypes.bfloat16)

    gu_all = np.vstack([wg, wu])
    dn_all = wd.reshape(2048, QW)

    if _NC_CACHE is None:
        _NC_CACHE = build_program()
    nc = _NC_CACHE

    in_maps = []
    for b in range(8):
        in_maps.append({
            "x": np.ascontiguousarray(x[b]),
            "wqkT": wqkT, "wvT": wvT, "wpT": wpT, "bproj": bproj2,
            "my_gu": np.ascontiguousarray(gu_all[:, 128 * b:128 * (b + 1)]),
            "my_dn": np.ascontiguousarray(dn_all[256 * b:256 * (b + 1)]),
            "cmask": cmask, "ones_all": ones_all, "ones_bf": ones_bf,
            "ones33": ones33, "eye64": eye64,
        })
    res = run_bass_kernel_spmd(nc, in_maps, core_ids=list(range(8)))
    out = np.stack([np.asarray(r["out"]) for r in res.results], axis=0)
    return out.astype(np.float32)
